# version 01
# baseline (speedup 1.0000x reference)
import math
import os
import numpy as np
from contextlib import ExitStack

import concourse.bass as bass
import concourse.mybir as mybir
from concourse.bass_utils import run_bass_kernel_spmd

F32 = mybir.dt.float32
BF16 = mybir.dt.bfloat16
AF = mybir.ActivationFunctionType
ALU = mybir.AluOpType
AX = mybir.AxisListType.X

NSEQ = 4
S = 2048
D = 1024
H = 16
KPH = 32
LAT = 512
E = 8
DFF = 2048
SC1 = 1.0 / 8.0
SC2 = 1.0 / math.sqrt(KPH)

ENGINES = ("tensor", "vector", "scalar", "gpsimd", "sync")


class _Sem:
    __slots__ = ("h", "n", "name")

    def __init__(self, h, name):
        self.h = h
        self.n = 0
        self.name = name


class _Prog:
    def __init__(self):
        self.ops = {e: [] for e in ENGINES}

    def emit(self, eng, fn, waits=(), inc=None, inc_n=1):
        w = tuple((s, v) for (s, v) in waits if s is not None and v > 0)
        ev = None
        if inc is not None:
            inc.n += inc_n
            ev = (inc, inc.n)
        self.ops[eng].append((w, fn, inc, inc_n if inc else 0))
        return ev

    def run(self, eng_name, eng):
        seen = {}
        for w, fn, inc, n in self.ops[eng_name]:
            for (s, v) in w:
                if seen.get(s.name, 0) >= v:
                    continue
                seen[s.name] = v
                eng.wait_ge(s.h, v)
            ins = fn(eng) if fn is not None else None
            if inc is not None:
                ins.then_inc(inc.h, n)


def _later(*evs):
    best = {}
    for ev in evs:
        if ev is None:
            continue
        s, v = ev
        if s is None or v <= 0:
            continue
        if best.get(s.name, (None, 0))[1] < v:
            best[s.name] = (s, v)
    return list(best.values())


class Build:
    def __init__(self):
        self.nc = bass.Bass("TRN2", target_bir_lowering=False)
        self.p = _Prog()
        self.ctx = ExitStack()
        self.sems = {}
        self.free = {}

    def sem(self, name):
        if name not in self.sems:
            h = self.ctx.enter_context(self.nc.semaphore(name))
            self.sems[name] = _Sem(h, name)
        return self.sems[name]

    def pe_now(self):
        return (self.sems["s_pe"], self.sems["s_pe"].n)

    def setup(self):
        nc = self.nc
        self.x_in = nc.dram_tensor("x", [NSEQ, S, D], F32, kind="ExternalInput").ap()
        wn = lambda n, shp: nc.dram_tensor(n, shp, F32, kind="ExternalInput").ap()
        self.aqw = wn("L0_attn_q_w", [D, D]);   self.aqb = wn("L0_attn_q_b", [D])
        self.akw = wn("L0_attn_k_w", [D, D]);   self.akb = wn("L0_attn_k_b", [D])
        self.avw = wn("L0_attn_v_w", [D, D]);   self.avb = wn("L0_attn_v_b", [D])
        self.acw = wn("L0_attn_conv_w", [D, 1, 3]); self.acb = wn("L0_attn_conv_b", [D])
        self.aow = wn("L0_attn_out_w", [D, D]); self.aob = wn("L0_attn_out_b", [D])
        self.fqw = wn("L0_ffn_q_w", [D, D]);    self.fqb = wn("L0_ffn_q_b", [D])
        self.fkw = wn("L0_ffn_k_w", [D, LAT]);  self.fkb = wn("L0_ffn_k_b", [LAT])
        self.fvw = wn("L0_ffn_v_w", [D, LAT]);  self.fvb = wn("L0_ffn_v_b", [LAT])
        self.fow = wn("L0_ffn_o_w", [D, D]);    self.fob = wn("L0_ffn_o_b", [D])
        self.frw = wn("L0_ffn_router_w", [D, E]); self.frb = wn("L0_ffn_router_b", [E])
        self.w1 = wn("L0_ffn_w1", [E, D, DFF]); self.b1 = wn("L0_ffn_b1", [E, DFF])
        self.w2 = wn("L0_ffn_w2", [E, DFF, D]); self.b2 = wn("L0_ffn_b2", [E, D])
        self.y_out = nc.dram_tensor("y", [NSEQ, S, D], F32, kind="ExternalOutput").ap()

        ds = lambda n, shp, dt: nc.dram_tensor(n, shp, dt, kind="Internal").ap()
        self.aqw_b = ds("aqw_b", [D, D], BF16)
        self.akw_b = ds("akw_b", [D, D], BF16)
        self.avw_b = ds("avw_b", [D, D], BF16)
        self.aow_b = ds("aow_b", [D, D], BF16)
        self.fqw_b = ds("fqw_b", [D, D], BF16)
        self.fkw_b = ds("fkw_b", [D, LAT], BF16)
        self.fvw_b = ds("fvw_b", [D, LAT], BF16)
        self.fow_b = ds("fow_b", [LAT, D], BF16)
        self.w1_b = ds("w1_b", [E, D, DFF], BF16)
        self.w2_b = ds("w2_b", [E, DFF, D], BF16)
        self.outscr = ds("outscr", [D, S], F32)

        sel = np.zeros((E, E * 128), dtype=np.float32)
        for e in range(E):
            sel[e, e * 128:(e + 1) * 128] = 1.0
        import ml_dtypes
        self.selc_d = nc.inline_tensor(sel.astype(ml_dtypes.bfloat16), name="selcd").ap()

        arena = nc.alloc_sbuf_tensor("arena", [128, 206848], mybir.dt.uint8)
        base = nc.lookup_mloc(arena).addr
        self._cur = base
        self._limit = base + 206848

        def at(name, shape, dt, off=None):
            nbytes = shape[1] * (2 if dt == BF16 else 4)
            if off is None:
                off = self._cur
                self._cur = off + ((nbytes + 31) // 32) * 32
            assert off + nbytes <= self._limit, (name, off + nbytes - base)
            return nc.alloc_sbuf_tensor_at(name, shape, dt, offset=off).ap(), off

        self.xnT, off_A = at("xnT", [128, 8 * S], BF16)
        self.mlaT, _ = at("mlaT", [128, 8 * S], BF16, off=off_A)
        self.qT, off_B = at("qT", [128, 8192], BF16)
        self.q2c, _ = at("q2c", [64, 8192], BF16, off=off_B)
        self.h1g, _ = at("h1g", [128, 8192], BF16, off=off_B)
        self.xdT, off_C = at("xdT", [128, 8192], BF16)
        self.kT, _ = at("kT", [128, 8192], BF16)
        self.k2T, _ = at("k2T", [64, 16384], BF16, off=off_C)
        self.w2s, _ = at("w2s", [128, 16384], BF16, off=off_C)
        self.v1tok, off_D = at("v1tok", [128, 8320], BF16)
        self.v2tok, _ = at("v2tok", [128, 8448], BF16, off=off_D)
        self.bcast, _ = at("bcast", [128, 2048], BF16, off=off_D)
        self._cur = max(self._cur, off_D + 16896)
        self.oT, off_E = at("oT", [128, 8192], BF16)
        self.aoT, _ = at("aoT", [64, 8192], BF16, off=off_E)
        self.w1s, off_F = at("w1s", [128, 16384], BF16)
        self.wtA, _ = at("wtA", [128, 8192], BF16, off=off_F)
        self.wtB, _ = at("wtB", [128, 8192], BF16, off=off_F + 16384)
        self.foB, _ = at("foB", [64, 8192], BF16, off=off_F + 16384)
        self.attb, _ = at("attb", [128, 2048], BF16)
        self.identB, _ = at("identB", [128, 128], BF16)
        self.onesB, _ = at("onesB", [1, 128], BF16)
        self.selc, _ = at("selc", [8, E * 128], BF16)
        self.b2s, _ = at("b2s", [8, D], BF16)
        self.combT, _ = at("combT", [8, S], BF16)
        self.vb1, _ = at("vb1", [1, D], BF16)
        self.v2b, _ = at("v2b", [1, LAT], BF16)
        self.routw, _ = at("routw", [128, 64], BF16)
        self.combB, _ = at("combB", [128, 8], BF16)
        self.stg, off_I = at("stg", [128, 2048], F32)
        self.braw, _ = at("braw", [8, 2048], F32, off=off_I)
        self.osb, _ = at("osb", [128, 1024], F32)
        self.sums, off_K = at("sums", [1, 1024], F32)
        self.probsL, _ = at("probsL", [8, 1024], F32, off=off_K)
        self._cur = max(self._cur, off_K + 4096)
        self.recip, _ = at("recip", [1, 1024], F32)
        self.x_any, _ = at("x_any", [128, 2048], F32)
        self.fin_in, _ = at("fin_in", [128, 2048], F32, off=off_I)
        self.out_st, off_N = at("out_st", [128, 2048], F32)
        self.sq_scr, _ = at("sq_scr", [128, 1024], F32, off=off_N)
        self.identF, _ = at("identF", [128, 128], F32)
        self.onesF, _ = at("onesF", [1, 128], F32)
        self.stats, _ = at("stats", [128, 80], F32)
        self.biasc, _ = at("biasc", [128, 224], F32)
        self.routb, _ = at("routb", [8, 1], F32)
        self.combF, _ = at("combF", [128, 8], F32)
        self.eq8, _ = at("eq8", [128, 8], F32)
        self.constv, _ = at("constv", [128, 8], F32)
        assert self._cur <= self._limit, self._cur - base

        bc = self.biasc
        self.qbcol = bc[:, 0:8]
        self.kbcol = bc[:, 8:16]
        self.q2bcol = bc[:, 16:24]
        self.k2bcol = bc[:, 24:28]
        self.obcol = bc[:, 28:36]
        self.outbcol = bc[:, 36:44]
        self.convbcol = bc[:, 44:52]
        self.convw_j = [bc[:, 52 + 8 * j:60 + 8 * j] for j in range(3)]
        self.b1col = bc[:, 76:204]
        st = self.stats
        self.negm = st[:, 0:16]
        self.rstd = st[:, 16:32]
        self.rstd1 = st[:, 32:48]
        self.nmr = st[:, 48:64]
        self.varsum = st[:, 64:65]
        self.t0 = st[:, 65:66]
        self.t1 = st[:, 66:67]
        self.t2 = st[:, 67:68]
        self.t3 = st[:, 68:69]

        self.ps_a = self.ctx.enter_context(nc.psum_tensor("ps_a", [128, 512], F32)).ap()
        self.ps_b = self.ctx.enter_context(nc.psum_tensor("ps_b", [128, 512], F32)).ap()
        self.ps_big = self.ctx.enter_context(nc.psum_tensor("ps_big", [128, 1024], F32)).ap()
        self.ps_tr = self.ctx.enter_context(nc.psum_tensor("ps_tr", [128, 2048], BF16)).ap()
        self.ps_c = self.ctx.enter_context(nc.psum_tensor("ps_c", [128, 512], F32)).ap()
        self.ps_d = self.ctx.enter_context(nc.psum_tensor("ps_d", [128, 512], F32)).ap()

        for n in ("sd_wa", "sd_wb", "sd_w1", "sd_w2", "sd_x0", "sd_x1",
                  "sd_f0", "sd_f1", "sd_sp0", "sd_sp1", "sd_st0", "sd_st1",
                  "sd_ac0", "sd_ac1", "sd_g", "sd_misc",
                  "s_pe", "s_act", "s_dve", "s_pool"):
            self.sem(n)

    def acc_group(self, ps, pname, pairs, waits0, rows=128, cols=512, col0=0):
        p, s_pe = self.p, self.sems["s_pe"]
        n = len(pairs)
        ev = None
        for i, (lt, rt) in enumerate(pairs):
            w = []
            if i == 0:
                w = _later(self.free.get((pname,)), *waits0)
            ev = p.emit("tensor",
                        lambda t, l=lt, r=rt, st=(i == 0), sp=(i == n - 1),
                        o=ps[0:rows, col0:col0 + cols]:
                        t.matmul(o, l, r, start=st, stop=sp),
                        waits=w, inc=(s_pe if i == n - 1 else None))
        return ev

    def phase0(self):
        p = self.p
        sd_g, sd_misc = self.sems["sd_g"], self.sems["sd_misc"]
        s_pe, s_dve, s_pool = (self.sems[n] for n in ("s_pe", "s_dve", "s_pool"))

        casts = [("aq", self.aqw_b, self.aqw), ("ak", self.akw_b, self.akw),
                 ("av", self.avw_b, self.avw), ("ao", self.aow_b, self.aow),
                 ("fq", self.fqw_b, self.fqw), ("fk", self.fkw_b, self.fkw),
                 ("fv", self.fvw_b, self.fvw), ("fo", self.fow_b, self.fow[0:LAT, :])]
        for e in range(E):
            casts.append((f"w1_{e}", self.w1_b[e], self.w1[e]))
            casts.append((f"w2_{e}", self.w2_b[e], self.w2[e]))
        self.ev_cast = {}
        for nm, out_ap, in_ap in casts:
            self.ev_cast[nm] = p.emit(
                "gpsimd", lambda g, o=out_ap, i=in_ap: g.dma_start(o, i),
                inc=sd_g, inc_n=16)
        self.ev_cast_all = (sd_g, sd_g.n)

        def mkident(g, ident, size):
            g.memset(ident, 0.0)
            return g.affine_select(out=ident, in_=ident, compare_op=ALU.not_equal,
                                   fill=1.0, base=0, pattern=[[-1, size]],
                                   channel_multiplier=1)
        p.emit("gpsimd", lambda g: mkident(g, self.identB, 128))
        p.emit("gpsimd", lambda g: mkident(g, self.identF, 128))
        p.emit("gpsimd", lambda g: g.memset(self.onesB, 1.0))
        for i, val in enumerate([1e-5, 1.0 / D, SC1, SC2]):
            ap = self.constv[:, i:i + 1]
            p.emit("gpsimd", lambda g, o=ap, v=val: g.memset(o, v))
            self.nc.const_aps.register(F32, val, ap)
        p.emit("gpsimd", lambda g: g.memset(self.onesF, 1.0), inc=s_pool)
        self.ev_consts = (s_pool, s_pool.n)

        for out_ap, in_ap, slow in [
                (self.selc, self.selc_d, False),
                (self.b2s, self.b2, False),
                (self.vb1, self.avb.rearrange("(o d) -> o d", o=1), False),
                (self.v2b, self.fvb.rearrange("(o d) -> o d", o=1), False),
                (self.routw.rearrange("p (ci e) -> p ci e", e=8),
                 self.frw.rearrange("(ci p) e -> p ci e", p=128), True)]:
            p.emit("gpsimd", lambda g, o=out_ap, i=in_ap, sl=slow:
                   g.dma_start(o, i, allow_slow_non_contiguous=sl), inc=sd_g, inc_n=16)
        self.ev_small = (sd_g, sd_g.n)

        ev_copy2 = None
        ev_copy1 = None
        slot = [0]

        def tr_step(src_ap, rows, dest_ap, ev_ld):
            nonlocal ev_copy1, ev_copy2
            po = (slot[0] % 2) * 512
            w = _later(ev_ld, self.ev_consts, ev_copy2)
            ev_tr = p.emit("tensor", lambda t, s=src_ap, r=rows, o=po:
                           t.transpose(self.ps_big[0:128, o:o + r], s,
                                       self.identF[0:r, 0:r]),
                           waits=w, inc=s_pe)
            ev_c = p.emit("vector", lambda v, d=dest_ap, o=po, r=rows:
                          v.tensor_copy(d, self.ps_big[0:128, o:o + r]),
                          waits=[ev_tr], inc=s_dve)
            ev_copy2, ev_copy1 = ev_copy1, ev_c
            slot[0] += 1

        jobs = [(self.aqb, 8, self.qbcol), (self.akb, 8, self.kbcol),
                (self.fqb, 8, self.q2bcol), (self.fkb, 4, self.k2bcol),
                (self.fob, 8, self.obcol), (self.aob, 8, self.outbcol),
                (self.acb, 8, self.convbcol)]
        for vec, rows, dest in jobs:
            src = self.braw[0:rows, 0:128]
            ev_ld = p.emit("sync", lambda sy, o=src,
                           i=vec.rearrange("(a p) -> a p", p=128): sy.dma_start(o, i),
                           waits=[self.pe_now()], inc=sd_misc, inc_n=16)
            tr_step(src, rows, dest[:, 0:rows], ev_ld)
        b1raw = self.braw[0:8, 0:2048]
        ev_ld = p.emit("sync", lambda sy: sy.dma_start(b1raw, self.b1),
                       waits=[self.pe_now()], inc=sd_misc, inc_n=16)
        for fi in range(16):
            tr_step(b1raw[0:8, fi * 128:(fi + 1) * 128], 8,
                    self.b1col[:, fi * 8:fi * 8 + 8], ev_ld)
        cwraw = self.braw[0:8, 0:384]
        ev_ld = p.emit("sync", lambda sy: sy.dma_start(
            cwraw, self.acw.rearrange("(a p) one j -> a (p one j)", a=8)),
            waits=[self.pe_now()], inc=sd_misc, inc_n=16)
        for j in range(3):
            cw2 = self.braw[0:8, 1024 + j * 128: 1024 + (j + 1) * 128]
            ev_cp = p.emit("vector", lambda v, o=cw2, i=cwraw[0:8, j:384:3]:
                           v.tensor_copy(o, i), waits=[ev_ld], inc=s_dve)
            tr_step(cw2, 8, self.convw_j[j], ev_cp)
        p.emit("sync", lambda sy: sy.dma_start(
            self.routb, self.frb.rearrange("(e o) -> e o", o=1)),
            waits=[self.pe_now()], inc=sd_misc, inc_n=16)
        self.ev_bias_done = ev_copy1
        self.ev_misc_ld = (sd_misc, sd_misc.n)
        self.ev_p0_pe = self.pe_now()
        self.free[("ps_big",)] = ev_copy1
        self.free[("stg", 0)] = self.ev_p0_pe
        self.free[("stg", 1)] = self.ev_p0_pe

    def load_wt(self, half, kind, extra_waits=()):
        p = self.p
        sem = self.sems["sd_wa"] if half == 0 else self.sems["sd_wb"]
        dst_t = self.wtA if half == 0 else self.wtB
        waits = _later(self.free.get(("wt", half)), self.ev_cast_all, *extra_waits)
        ev = None
        if kind in ("aq", "ak", "av", "ao"):
            src = {"aq": self.aqw_b, "ak": self.akw_b,
                   "av": self.avw_b, "ao": self.aow_b}[kind]
            for ci in range(8):
                ev = p.emit("scalar",
                            lambda sc, d=dst_t[:, ci * 1024:(ci + 1) * 1024],
                            i=src[:, ci * 128:(ci + 1) * 128]:
                            sc.dma_start(d, i, transpose=True),
                            waits=waits if ci == 0 else (), inc=sem, inc_n=16)
        elif kind == "fq":
            ev = p.emit("sync", lambda sy,
                        d=dst_t[:, 0:8192].rearrange("p (ci o) -> p ci o", o=1024):
                        sy.dma_start(
                d, self.fqw_b.rearrange("(ci p) o -> p ci o", p=128)),
                waits=waits, inc=sem, inc_n=16)
        elif kind in ("fk", "fv"):
            src = self.fkw_b if kind == "fk" else self.fvw_b
            ev = p.emit("sync", lambda sy,
                        d=dst_t[:, 0:4096].rearrange("p (ci o) -> p ci o", o=512),
                        s=src: sy.dma_start(
                d, s.rearrange("(ci p) o -> p ci o", p=128)),
                waits=waits, inc=sem, inc_n=16)
        elif kind == "fo":
            ev = p.emit("sync", lambda sy,
                        d=self.foB.rearrange("p (kc o) -> p kc o", o=1024):
                        sy.dma_start(
                d, self.fow_b.rearrange("(kc p) o -> p kc o", p=64)),
                waits=waits, inc=sem, inc_n=16)
        return ev

    def ln_phase(self, s):
        p = self.p
        s_pe, s_act, s_dve = (self.sems[n] for n in ("s_pe", "s_act", "s_dve"))
        for ti in range(16):
            b = ti % 2
            sd_x = self.sems["sd_x0"] if b == 0 else self.sems["sd_x1"]
            xb = self.x_any[:, b * 1024:(b + 1) * 1024]
            ev_x = p.emit("sync", lambda sy, o=xb, ss=s, t=ti: sy.dma_start(
                o, self.x_in[ss, t * 128:(t + 1) * 128, :]),
                waits=_later(self.free.get(("x_any", b))), inc=sd_x, inc_n=16)
            nm = self.negm[:, ti:ti + 1]
            rs = self.rstd[:, ti:ti + 1]
            ev = p.emit("vector", lambda v, o=nm, i=xb:
                        v.reduce_sum(o, i, axis=AX, negate=True),
                        waits=[ev_x], inc=s_dve)
            ev = p.emit("vector", lambda v, o=nm:
                        v.tensor_scalar_mul(o, o, 1.0 / D), waits=[ev], inc=s_dve)
            ev = p.emit("scalar", lambda sc, i=xb, n=nm:
                        sc.activation(self.sq_scr, i, AF.Square, bias=n,
                                      accum_out=self.varsum),
                        waits=_later(ev, self.free.get(("out_st", 0))
                                     if ti == 0 else None,
                                     self.ev_consts if ti == 0 else None),
                        inc=s_act)
            ev = p.emit("scalar", lambda sc, r=rs:
                        sc.activation(r, self.varsum, AF.Sqrt, scale=1.0 / D,
                                      bias=1e-5),
                        waits=[ev], inc=s_act)
            ev = p.emit("vector", lambda v, r=rs: v.reciprocal(r, r),
                        waits=[ev], inc=s_dve)
            p.emit("vector", lambda v, t=ti: v.tensor_scalar_add(
                self.rstd1[:, t:t + 1], self.rstd[:, t:t + 1], 1.0),
                waits=[ev], inc=s_dve)
            p.emit("vector", lambda v, t=ti: v.tensor_mul(
                self.nmr[:, t:t + 1], self.negm[:, t:t + 1], self.rstd[:, t:t + 1]),
                inc=s_dve)
            xn_tok = self.attb[:, b * 1024:(b + 1) * 1024]
            wv = _later(ev, self.free.get(("attb", b)),
                        self.free.get(("attb_s", 0)) if (b == 0 and ti == 0) else None,
                        self.free.get(("attb_s", 1)) if (b == 0 and ti == 0) else None)
            ev = p.emit("vector", lambda v, o=xn_tok, i=xb, t=ti:
                        v.tensor_scalar(o, i, self.negm[:, t:t + 1],
                                        self.rstd[:, t:t + 1],
                                        op0=ALU.add, op1=ALU.mult),
                        waits=wv, inc=s_dve)
            self.free[("x_any", b)] = ev
            ho = b * 1024
            evt = None
            for k in range(8):
                wts = _later(ev, self.ev_p0_pe,
                             self.free.get(("ps_tr", b)) if k == 0 else None,
                             self.free.get(("xnT_w",)) if (ti == 0 and k == 0) else None)
                evt = p.emit("tensor", lambda t, kk=k, o=ho, i=xn_tok:
                             t.transpose(self.ps_tr[:, o + kk * 128:o + (kk + 1) * 128],
                                         i[:, kk * 128:(kk + 1) * 128], self.identB),
                             waits=wts, inc=(s_pe if k == 7 else None))
            self.free[("attb", b)] = evt
            dest = self.xnT.rearrange("p (c t) -> p c t", t=S)[:, :, ti * 128:(ti + 1) * 128]
            ev_e = p.emit("vector", lambda v, d=dest, o=ho:
                          v.tensor_copy(d, self.ps_tr[:, o:o + 1024]
                                        .rearrange("p (c t) -> p c t", t=128)),
                          waits=[evt], inc=s_dve)
            self.free[("ps_tr", b)] = ev_e
        self.ev_xnT = (s_dve, s_dve.n)

    def conv_phase(self, s):
        p = self.p
        s_pool = self.sems["s_pool"]
        s_dve = self.sems["s_dve"]
        tA = self.x_any[:, 0:1024]
        tB = self.x_any[:, 1024:2048]
        ev = None
        for ci in range(8):
            xn = self.xnT[:, ci * S:(ci + 1) * S]
            xd = self.xdT[:, ci * 1024:(ci + 1) * 1024]
            w0 = _later(self.ev_xnT, self.ev_bias_done,
                        self.free.get(("C",)) if ci == 0 else None,
                        self.free.get(("x_any", 0)) if ci == 0 else None,
                        self.free.get(("x_any", 1)) if ci == 0 else None)
            ev = p.emit("vector", lambda v, o=tB[:, 1:1024], i=xn[:, 1:2046:2], c=ci:
                        v.tensor_scalar_mul(o, i, self.convw_j[0][:, c:c + 1]),
                        waits=w0, inc=s_dve)
            for step in [
                    lambda v, o=tA[:, 1:1024], i=xn[:, 2:2047:2], c=ci:
                    v.scalar_tensor_tensor(o, i, self.convw_j[1][:, c:c + 1],
                                           tB[:, 1:1024], op0=ALU.mult, op1=ALU.add),
                    lambda v, o=tB[:, 1:1024], i=xn[:, 3:2048:2], c=ci:
                    v.scalar_tensor_tensor(o, i, self.convw_j[2][:, c:c + 1],
                                           tA[:, 1:1024], op0=ALU.mult, op1=ALU.add),
                    lambda v, o=tA[:, 0:1], i=xn[:, 0:1], c=ci:
                    v.tensor_scalar_mul(o, i, self.convw_j[1][:, c:c + 1]),
                    lambda v, o=tB[:, 0:1], i=xn[:, 1:2], c=ci:
                    v.scalar_tensor_tensor(o, i, self.convw_j[2][:, c:c + 1],
                                           tA[:, 0:1], op0=ALU.mult, op1=ALU.add),
                    lambda v, o=xd, c=ci:
                    v.tensor_scalar_add(o, tB[:, 0:1024], self.convbcol[:, c:c + 1])]:
                ev = p.emit("vector", step, waits=[ev], inc=s_dve)
        self.free[("x_any", 0)] = ev
        self.free[("x_any", 1)] = ev
        self.ev_xdT = ev

    def kv1_phase(self, s):
        p = self.p
        s_act, s_dve, s_pool = (self.sems[n] for n in ("s_act", "s_dve", "s_pool"))
        ev_k = self.ev_wtB
        ev_cfree = self.free.get(("C",))
        slot = 0
        for ii in range(8):
            for qc in range(2):
                ps = self.ps_a if slot % 2 == 0 else self.ps_b
                pn = "ps_a" if slot % 2 == 0 else "ps_b"
                pairs = [(self.wtB[:, ci * 1024 + ii * 128: ci * 1024 + ii * 128 + 128],
                          self.xdT[:, ci * 1024 + qc * 512: ci * 1024 + qc * 512 + 512])
                         for ci in range(8)]
                ev = self.acc_group(ps, pn, pairs, [ev_k, self.ev_xdT])
                ev_e = p.emit("scalar", lambda sc,
                              o=self.kT[:, ii * 1024 + qc * 512: ii * 1024 + qc * 512 + 512],
                              i=ps, c=ii:
                              sc.activation(o, i, AF.Identity,
                                            bias=self.kbcol[:, c:c + 1]),
                              waits=_later(ev, ev_cfree if slot == 0 else None),
                              inc=s_act)
                self.free[(pn,)] = ev_e
                slot += 1
        self.ev_kT = (s_act, s_act.n)
        self.free[("wt", 1)] = self.pe_now()

        ev_v = self.load_wt(1, "av")
        self.ev_wtB = ev_v
        ev_ones = p.emit("gpsimd", lambda g: g.memset(
            self.v1tok[:, 0:8320].rearrange("p (x d) -> p x d", d=65)[:, :, 64:65], 1.0),
            waits=_later(self.free.get(("vtok",)), self.free.get(("vtok2",))),
            inc=s_pool)
        for kti in range(8):
            for qc in range(2):
                ps = self.ps_a if slot % 2 == 0 else self.ps_b
                pn = "ps_a" if slot % 2 == 0 else "ps_b"
                pairs = [(self.xdT[:, ci * 1024 + kti * 128: ci * 1024 + kti * 128 + 128],
                          self.wtB[:, ci * 1024 + qc * 512: ci * 1024 + qc * 512 + 512])
                         for ci in range(8)]
                pairs.append((self.onesB, self.vb1[:, qc * 512:(qc + 1) * 512]))
                ev = self.acc_group(ps, pn, pairs, [ev_v])
                base = kti * 1040 + qc * 520
                dest = self.v1tok[:, base:base + 520].rearrange(
                    "p (h d) -> p h d", d=65)[:, :, 0:64]
                ev_e = p.emit("vector", lambda v, d=dest,
                              i=ps.rearrange("p (h d) -> p h d", d=64):
                              v.tensor_copy(d, i),
                              waits=_later(ev, ev_ones), inc=s_dve)
                self.free[(pn,)] = ev_e
                slot += 1
        self.ev_v1 = (s_dve, s_dve.n)
        self.free[("wt", 1)] = self.pe_now()

    def kv2_phase(self, s):
        p = self.p
        s_act, s_dve, s_pool = (self.sems[n] for n in ("s_act", "s_dve", "s_pool"))
        ev_k = self.ev_wtB
        ev_cfree = self.pe_now()
        slot = 0
        for hc in range(8):
            for qc in range(4):
                ps = self.ps_a if slot % 2 == 0 else self.ps_b
                pn = "ps_a" if slot % 2 == 0 else "ps_b"
                pairs = [(self.wtB[:, ci * 512 + hc * 64: ci * 512 + hc * 64 + 64],
                          self.xnT[:, ci * S + qc * 512: ci * S + qc * 512 + 512])
                         for ci in range(8)]
                ev = self.acc_group(ps, pn, pairs, [ev_k, self.ev_xnT], rows=64)
                ev_e = p.emit("scalar", lambda sc,
                              o=self.k2T[0:64, hc * 2048 + qc * 512:
                                         hc * 2048 + qc * 512 + 512],
                              i=ps[0:64, :], c=hc:
                              sc.activation(o, i, AF.Identity,
                                            bias=self.k2bcol[(c % 2) * 64:(c % 2) * 64 + 64,
                                                             c // 2:c // 2 + 1]),
                              waits=_later(ev, ev_cfree if slot == 0 else None),
                              inc=s_act)
                self.free[(pn,)] = ev_e
                slot += 1
        self.ev_k2 = (s_act, s_act.n)
        self.free[("wt", 1)] = self.pe_now()

        ev_v = self.load_wt(1, "fv")
        self.ev_wtB = ev_v
        ev_ones = p.emit("gpsimd", lambda g: g.memset(
            self.v2tok[:, 0:8448].rearrange("p (x d) -> p x d", d=33)[:, :, 32:33], 1.0),
            waits=_later(self.pe_now()), inc=s_pool)
        for kti in range(16):
            ps = self.ps_a if slot % 2 == 0 else self.ps_b
            pn = "ps_a" if slot % 2 == 0 else "ps_b"
            pairs = [(self.xnT[:, ci * S + kti * 128: ci * S + kti * 128 + 128],
                      self.wtB[:, ci * 512: ci * 512 + 512])
                     for ci in range(8)]
            pairs.append((self.onesB, self.v2b[:, 0:512]))
            ev = self.acc_group(ps, pn, pairs, [ev_v])
            base = kti * 528
            dest = self.v2tok[:, base:base + 528].rearrange(
                "p (h d) -> p h d", d=33)[:, :, 0:32]
            ev_e = p.emit("vector", lambda v, d=dest,
                          i=ps.rearrange("p (h d) -> p h d", d=32):
                          v.tensor_copy(d, i),
                          waits=_later(ev, ev_ones), inc=s_dve)
            self.free[(pn,)] = ev_e
            slot += 1
        self.ev_v2 = (s_dve, s_dve.n)
        self.free[("wt", 1)] = self.pe_now()

    def q_proj(self, s, qh, branch):
        p = self.p
        s_act = self.sems["s_act"]
        qfree = self.free.get(("qbuf",))
        slot = 0
        for ii in range(8):
            for qc in range(2):
                ps = self.ps_a if slot % 2 == 0 else self.ps_b
                pn = "ps_a" if slot % 2 == 0 else "ps_b"
                pairs = [(self.wtA[:, ci * 1024 + ii * 128: ci * 1024 + ii * 128 + 128],
                          self.xnT[:, ci * S + qh * 1024 + qc * 512:
                                   ci * S + qh * 1024 + qc * 512 + 512])
                         for ci in range(8)]
                ev = self.acc_group(ps, pn, pairs, [self.ev_wtA, self.ev_xnT])
                o0 = ii * 1024 + qc * 512
                if branch == 1:
                    ev_e = p.emit("scalar", lambda sc, o=self.qT[:, o0:o0 + 512],
                                  i=ps, c=ii:
                                  sc.activation(o, i, AF.Identity,
                                                bias=self.qbcol[:, c:c + 1]),
                                  waits=_later(ev, qfree if slot == 0 else None),
                                  inc=s_act)
                else:
                    p.emit("scalar", lambda sc, o=self.q2c[0:32, o0:o0 + 512],
                           i=ps[0:32, :], c=ii:
                           sc.activation(o, i, AF.Identity,
                                         bias=self.q2bcol[0:32, c:c + 1]),
                           waits=_later(ev, qfree if slot == 0 else None), inc=s_act)
                    ev_e = p.emit("scalar", lambda sc, o=self.q2c[32:64, o0:o0 + 512],
                                  i=ps[64:96, :], c=ii:
                                  sc.activation(o, i, AF.Identity,
                                                bias=self.q2bcol[64:96, c:c + 1]),
                                  inc=s_act)
                self.free[(pn,)] = ev_e
                slot += 1
        self.ev_q = (s_act, s_act.n)

    def attn_phase(self, s, qh, branch):
        p = self.p
        s_pe, s_act, s_dve = (self.sems[n] for n in ("s_pe", "s_act", "s_dve"))
        if branch == 1:
            n_kti, dv, sc = 8, 64, SC1
            kv_ready = _later(self.ev_kT, self.ev_v1, self.ev_q)
        else:
            n_kti, dv, sc = 16, 32, SC2
            kv_ready = _later(self.ev_k2, self.ev_v2, self.ev_q)
        oa_free = self.free.get(("oa",))

        for h in range(H):
            oslot = h % 2
            ev_av = None
            for kti in range(n_kti):
                for qc in range(2):
                    sl = (kti * 2 + qc) % 2
                    ps_s = self.ps_a if sl == 0 else self.ps_b
                    pn = "ps_a" if sl == 0 else "ps_b"
                    if branch == 1:
                        lt = self.kT[(h % 2) * 64:(h % 2) * 64 + 64,
                                     (h // 2) * 1024 + kti * 128:
                                     (h // 2) * 1024 + kti * 128 + 128]
                        rt = self.qT[(h % 2) * 64:(h % 2) * 64 + 64,
                                     (h // 2) * 1024 + qc * 512:
                                     (h // 2) * 1024 + qc * 512 + 512]
                        vt = self.v1tok[:, kti * 1040 + h * 65: kti * 1040 + h * 65 + 65]
                    else:
                        lt = self.k2T[(h % 2) * 32:(h % 2) * 32 + 32,
                                      (h // 2) * 2048 + kti * 128:
                                      (h // 2) * 2048 + kti * 128 + 128]
                        rt = self.q2c[(h % 2) * 32:(h % 2) * 32 + 32,
                                      (h // 2) * 1024 + qc * 512:
                                      (h // 2) * 1024 + qc * 512 + 512]
                        vt = self.v2tok[:, kti * 528 + h * 33: kti * 528 + h * 33 + 33]
                    w = _later(self.free.get((pn,)),
                               *(kv_ready if (h == 0 and kti == 0) else []))
                    ev_mm = p.emit("tensor", lambda t, l=lt, r=rt, o=ps_s:
                                   t.matmul(o, l, r, start=True, stop=True),
                                   waits=w, inc=s_pe)
                    ab = self.attb[:, sl * 512: sl * 512 + 512]
                    ev_x = p.emit("scalar", lambda scl, o=ab, i=ps_s, f=sc:
                                  scl.activation(o, i, AF.Exp, scale=f),
                                  waits=_later(ev_mm, self.free.get(("attb_s", sl)),
                                               self.free.get(("attb", 0))
                                               if (h == 0 and kti == 0 and qc == 0)
                                               else None),
                                  inc=s_act)
                    self.free[(pn,)] = ev_x
                    if oslot == 0:
                        po = self.ps_big[0:dv + 1, qc * 512: qc * 512 + 512]
                        okeys = [("ps_big",)]
                    else:
                        po = (self.ps_c if qc == 0 else self.ps_d)[0:dv + 1, 0:512]
                        okeys = [("ps_c",), ("ps_d",)]
                    okw = [self.free.get(k) for k in okeys] if kti == 0 else []
                    w2 = _later(ev_x, *okw)
                    ev_av = p.emit("tensor", lambda t, l=vt, r=ab, o=po,
                                   st=(kti == 0), sp=(kti == n_kti - 1):
                                   t.matmul(o, l, r, start=st, stop=sp),
                                   waits=w2, inc=s_pe)
                    self.free[("attb_s", sl)] = ev_av
            if oslot == 0:
                srcs = [self.ps_big[0:dv + 1, 0:512], self.ps_big[0:dv + 1, 512:1024]]
                bcs = [self.ps_big[0:dv, 0:512], self.ps_big[0:dv, 512:1024]]
            else:
                srcs = [self.ps_c[0:dv + 1, 0:512], self.ps_d[0:dv + 1, 0:512]]
                bcs = [self.ps_c[0:dv, 0:512], self.ps_d[0:dv, 0:512]]
            ev_ev = None
            for qc in range(2):
                ev_ev = p.emit("vector", lambda v, i=srcs[qc],
                               o=self.osb[0:dv + 1, qc * 512:qc * 512 + 512]:
                               v.tensor_copy(o, i), waits=[ev_av], inc=s_dve)
            ev_r = p.emit("vector", lambda v, r=dv:
                          v.reciprocal(self.recip[0:1, 0:1024],
                                       self.osb[r:r + 1, 0:1024]),
                          waits=[ev_ev], inc=s_dve)
            ev_bc = None
            for qc in range(2):
                ev_bc = p.emit("tensor", lambda t, o=bcs[qc], q=qc, r=dv:
                               t.matmul(o, self.onesF[0:1, 0:r],
                                        self.recip[0:1, q * 512:q * 512 + 512],
                                        start=True, stop=True),
                               waits=[ev_r], inc=s_pe)
            ev_m = None
            for qc in range(2):
                if branch == 1:
                    dst = self.oT[(h % 2) * 64:(h % 2) * 64 + 64,
                                  (h // 2) * 1024 + qc * 512:
                                  (h // 2) * 1024 + qc * 512 + 512]
                else:
                    dst = self.aoT[(h % 2) * 32:(h % 2) * 32 + 32,
                                   (h // 2) * 1024 + qc * 512:
                                   (h // 2) * 1024 + qc * 512 + 512]
                ev_m = p.emit("vector", lambda v, d=dst,
                              a=self.osb[0:dv, qc * 512:qc * 512 + 512], b=bcs[qc]:
                              v.tensor_mul(d, a, b),
                              waits=_later(ev_bc, oa_free
                                           if (h == 0 and qc == 0) else None),
                              inc=s_dve)
            if oslot == 0:
                self.free[("ps_big",)] = ev_m
            else:
                self.free[("ps_c",)] = ev_m
                self.free[("ps_d",)] = ev_m
        if branch == 1:
            self.ev_oT = (s_dve, s_dve.n)
        else:
            self.ev_aoT = (s_dve, s_dve.n)

    def out_proj(self, s, qh):
        p = self.p
        s_act = self.sems["s_act"]
        scr_rd = _later(self.free.get(("outscr_rd0",)), self.free.get(("outscr_rd1",)))
        for dc in range(8):
            half = dc % 2
            sd_sp = self.sems["sd_sp0"] if half == 0 else self.sems["sd_sp1"]
            ev_e = None
            for qc in range(2):
                pn = "ps_a" if qc == 0 else "ps_b"
                ps = self.ps_a if qc == 0 else self.ps_b
                pairs = [(self.wtB[:, ci * 1024 + dc * 128: ci * 1024 + dc * 128 + 128],
                          self.oT[:, ci * 1024 + qc * 512: ci * 1024 + qc * 512 + 512])
                         for ci in range(8)]
                ev = self.acc_group(ps, pn, pairs, [self.ev_wtB, self.ev_oT])
                ev_e = p.emit("scalar", lambda sc,
                              o=self.stg[:, half * 1024 + qc * 512:
                                         half * 1024 + qc * 512 + 512],
                              i=ps, c=dc:
                              sc.activation(o, i, AF.Identity,
                                            bias=self.outbcol[:, c:c + 1]),
                              waits=_later(ev, self.free.get(("stg", half))
                                           if qc == 0 else None),
                              inc=s_act)
                self.free[(pn,)] = ev_e
            ev_sp = p.emit("sync", lambda sy, d=dc, q=qh:
                           sy.dma_start(self.outscr[d * 128:(d + 1) * 128,
                                                    q * 1024:(q + 1) * 1024],
                                        self.stg[:, (d % 2) * 1024:(d % 2) * 1024 + 1024]),
                           waits=_later(ev_e, *(scr_rd if dc < 2 else [])),
                           inc=sd_sp, inc_n=16)
            self.free[("stg", half)] = ev_sp

    def mla_proj(self, s, qh):
        p = self.p
        s_act = self.sems["s_act"]
        slot = 0
        for dc in range(8):
            for qc in range(2):
                ps = self.ps_a if slot % 2 == 0 else self.ps_b
                pn = "ps_a" if slot % 2 == 0 else "ps_b"
                pairs = [(self.foB[0:64, kc * 1024 + dc * 128: kc * 1024 + dc * 128 + 128],
                          self.aoT[0:64, kc * 1024 + qc * 512: kc * 1024 + qc * 512 + 512])
                         for kc in range(8)]
                ev = self.acc_group(ps, pn, pairs, [self.ev_foB, self.ev_aoT])
                ev_e = p.emit("scalar", lambda sc,
                              o=self.mlaT[:, dc * S + qh * 1024 + qc * 512:
                                          dc * S + qh * 1024 + qc * 512 + 512],
                              i=ps, c=dc:
                              sc.activation(o, i, AF.Identity,
                                            bias=self.obcol[:, c:c + 1]),
                              waits=[ev], inc=s_act)
                self.free[(pn,)] = ev_e
                slot += 1
        self.ev_mla = (s_act, s_act.n)

    def router_phase(self, s, qh):
        p = self.p
        s_pe, s_act, s_dve = (self.sems[n] for n in ("s_pe", "s_act", "s_dve"))
        ev = None
        for qc in range(2):
            pairs = [(self.routw[:, ci * 8: ci * 8 + 8],
                      self.mlaT[:, ci * S + qh * 1024 + qc * 512:
                                ci * S + qh * 1024 + qc * 512 + 512])
                     for ci in range(8)]
            ev = self.acc_group(self.ps_big, "ps_big" if qc == 0 else "__skip",
                                pairs,
                                [self.ev_mla, self.ev_misc_ld] if qc == 0 else [],
                                rows=8, cols=512, col0=qc * 512)
        ev_l = p.emit("scalar", lambda sc:
                      sc.activation(self.probsL[0:8, 0:1024],
                                    self.ps_big[0:8, 0:1024],
                                    AF.Identity, bias=self.routb[0:8, 0:1]),
                      waits=[ev], inc=s_act)
        self.free[("ps_big",)] = ev_l
        for ti in range(8):
            po = (ti % 2) * 512
            ev_tr = p.emit("tensor", lambda t, o=po,
                           i=self.probsL[0:8, ti * 128:(ti + 1) * 128]:
                           t.transpose(self.ps_big[0:128, o:o + 8], i,
                                       self.identF[0:8, 0:8]),
                           waits=_later(ev_l, self.free.get(("rps", ti % 2)),
                                        self.free.get(("ps_big",)) if ti < 2 else None),
                           inc=s_pe)
            pl = self.ps_big[0:128, po:po + 8]
            ev1 = p.emit("vector", lambda v, i=pl: v.reduce_max(
                self.t0, i, axis=AX, negate=True), waits=[ev_tr], inc=s_dve)
            ev2 = p.emit("scalar", lambda sc, i=pl: sc.activation(
                self.combF, i, AF.Exp, bias=self.t0), waits=[ev1], inc=s_act)
            self.free[("rps", ti % 2)] = ev2
            ev = ev2
            for fn in [
                    lambda v: v.reduce_sum(self.t1, self.combF, axis=AX),
                    lambda v: v.reciprocal(self.t1, self.t1),
                    lambda v: v.tensor_scalar_mul(self.combF, self.combF,
                                                  self.t1[:, 0:1]),
                    lambda v: v.reduce_max(self.t0, self.combF, axis=AX),
                    lambda v: v.tensor_single_scalar(self.eq8, self.combF,
                                                     self.t0[:, 0:1], ALU.is_equal),
                    lambda v: v.scalar_tensor_tensor(self.eq8, self.eq8, -1e30,
                                                     self.combF, op0=ALU.mult,
                                                     op1=ALU.add),
                    lambda v: v.reduce_max(self.t2, self.eq8, axis=AX),
                    lambda v: v.tensor_single_scalar(self.eq8, self.combF,
                                                     self.t2[:, 0:1], ALU.is_ge),
                    lambda v: v.tensor_add(self.t3, self.t0, self.t2),
                    lambda v: v.tensor_scalar_add(self.t3, self.t3, 1e-8),
                    lambda v: v.reciprocal(self.t3, self.t3),
                    lambda v: v.tensor_scalar_mul(self.t3, self.t3, 0.3),
                    lambda v: v.tensor_mul(self.combF, self.combF, self.eq8),
                    lambda v: v.tensor_scalar_mul(self.combF, self.combF,
                                                  self.t3[:, 0:1])]:
                ev = p.emit("vector", fn, waits=[ev], inc=s_dve)
            ev4 = p.emit("vector", lambda v: v.tensor_copy(self.combB, self.combF),
                         waits=_later(ev, self.free.get(("combB",))), inc=s_dve)
            ev_t2 = p.emit("tensor", lambda t, o=(ti % 2) * 1024:
                           t.transpose(self.ps_tr[0:8, o:o + 128], self.combB,
                                       self.identB),
                           waits=_later(ev4, self.free.get(("ctr", ti % 2)),
                                        self.free.get(("ps_tr", ti % 2))
                                        if ti < 2 else None),
                           inc=s_pe)
            self.free[("combB",)] = ev_t2
            ev5 = p.emit("vector", lambda v, o=(ti % 2) * 1024, t=ti:
                         v.tensor_copy(self.combT[0:8, qh * 1024 + t * 128:
                                                  qh * 1024 + (t + 1) * 128],
                                       self.ps_tr[0:8, o:o + 128]),
                         waits=[ev_t2], inc=s_dve)
            self.free[("ctr", ti % 2)] = ev5
            self.free[("ps_tr", ti % 2)] = ev5
        self.ev_comb = (s_dve, s_dve.n)
        self.free[("ps_big",)] = (s_act, s_act.n)

    def moe_phase(self, s):
        p = self.p
        s_pe, s_act, s_dve, s_pool = (self.sems[n] for n in
                                      ("s_pe", "s_act", "s_dve", "s_pool"))
        sd_w1, sd_w2 = self.sems["sd_w1"], self.sems["sd_w2"]
        ev_sp = [(self.sems["sd_sp0"], self.sems["sd_sp0"].n),
                 (self.sems["sd_sp1"], self.sems["sd_sp1"].n)]
        ev_fin_prev = _later(self.free.get(("outscr_rd0",)),
                             self.free.get(("outscr_rd1",)))
        ev_b2_done = self.pe_now()

        e_list = [int(t) for t in os.environ.get("KDBG_ELIST", "").split(",")] \
            if os.environ.get("KDBG_ELIST") else list(range(E))
        qc_list = [int(t) for t in os.environ.get("KDBG_QCLIST", "").split(",")] \
            if os.environ.get("KDBG_QCLIST") else list(range(4))
        for ei, e in enumerate(e_list):
            ev_w1 = self.ev_w1[e]
            ev_w2 = self.ev_w2[e]
            ev_bc_e = None
            for half in range(2):
                evb = None
                for qc in range(2):
                    evb = p.emit("tensor", lambda t, hh=half, q=qc, ee=e:
                                 t.matmul(self.ps_big[0:128, q * 512:q * 512 + 512],
                                          self.selc[0:8, ee * 128:(ee + 1) * 128],
                                          self.combT[0:8, hh * 1024 + q * 512:
                                                     hh * 1024 + q * 512 + 512],
                                          start=True, stop=True),
                                 waits=_later(self.ev_comb,
                                              self.free.get(("ps_big",))
                                              if qc == 0 else None,
                                              self.ev_small if (e == 0 and half == 0
                                                                and qc == 0) else None),
                                 inc=s_pe)
                ev_bc_e = p.emit("vector", lambda v, hh=half:
                                 v.tensor_copy(self.bcast[:, hh * 1024:(hh + 1) * 1024],
                                               self.ps_big[0:128, 0:1024]),
                                 waits=_later(evb, ev_b2_done
                                              if (e == 0 and half == 0) else None),
                                 inc=s_dve)
                self.free[("ps_big",)] = ev_bc_e
            for qc in qc_list:
                ev_h2_prev = self.free.get(("h1g",))
                ev_scale = {}
                for fi in range(16):
                    sl = (fi + int(os.environ.get("KDBG_SLSWAP", "0"))) % 2
                    ps = self.ps_a if sl == 0 else self.ps_b
                    pn = "ps_a" if sl == 0 else "ps_b"
                    pairs = [(self.w1s[:, ci * 2048 + fi * 128: ci * 2048 + fi * 128 + 128],
                              self.mlaT[:, ci * S + qc * 512: ci * S + qc * 512 + 512])
                             for ci in range(8)]
                    ev = self.acc_group(ps, pn, pairs,
                                        [ev_w1, self.ev_mla]
                                        if (fi == 0 and qc == 0) else [])
                    if os.environ.get("KDBG_H1PS") and fi == 0:
                        p.emit("vector", lambda v, i=ps:
                               v.tensor_copy(self.out_st[:, 0:512], i),
                               waits=[ev], inc=s_dve)
                        self._h1ps_done = True
                        return
                    hslice = self.h1g[:, fi * 512:(fi + 1) * 512]
                    gb = self.attb[:, (fi % 2) * 512:(fi % 2) * 512 + 512]
                    ev_g = p.emit("scalar", lambda sc, o=gb, i=ps, f=fi, ee=e:
                                  sc.activation(o, i, AF.Gelu,
                                                bias=self.b1col[:, f * 8 + ee:
                                                                f * 8 + ee + 1]),
                                  waits=_later(ev, ev_h2_prev,
                                               self.free.get(("gb", fi % 2))),
                                  inc=s_act)
                    self.free[(pn,)] = ev_g
                    ev_scale[fi] = p.emit("vector", lambda v, o=hslice, i=gb, q=qc:
                                          v.tensor_mul(o, i,
                                                       self.bcast[:, q * 512:
                                                                  q * 512 + 512]),
                                          waits=_later(ev_g, ev_bc_e), inc=s_dve)
                    self.free[("gb", fi % 2)] = ev_scale[fi]
                ev_h2_last = None
                for dc in range(8):
                    sl = dc % 2
                    psh = self.ps_c if sl == 0 else self.ps_d
                    pnh = "ps_c" if sl == 0 else "ps_d"
                    ev = None
                    for fi in range(16):
                        w = []
                        if fi == 0:
                            w = _later(self.free.get((pnh,)),
                                       ev_w2 if (dc == 0 and qc == qc_list[0])
                                       else None)
                        if dc == 0:
                            w = _later(*(w + [ev_scale[fi]]))
                        ev = p.emit("tensor", lambda t,
                                    l=self.w2s[:, fi * 1024 + dc * 128:
                                               fi * 1024 + dc * 128 + 128],
                                    r=self.h1g[:, fi * 512:(fi + 1) * 512],
                                    st=(fi == 0),
                                    sp=(fi == 15 and (ei != 0 or
                                        bool(os.environ.get("KDBG_NOB")))), o=psh:
                                    t.matmul(o, l, r, start=st, stop=sp),
                                    waits=w,
                                    inc=(s_pe if (fi == 15 and (ei != 0 or
                                         bool(os.environ.get("KDBG_NOB"))))
                                         else None))
                    if ei == 0 and not os.environ.get("KDBG_NOB"):
                        ev = p.emit("tensor", lambda t, o=psh, d=dc, q=qc:
                                    t.matmul(o, self.b2s[0:8, d * 128:(d + 1) * 128],
                                             self.combT[0:8, q * 512:(q + 1) * 512],
                                             start=False, stop=True),
                                    inc=s_pe)
                    elif ei == 0:
                        pass
                    ev_h2_last = ev
                    half = (dc // 2) % 2
                    sd_ac = self.sems["sd_ac0"] if half == 0 else self.sems["sd_ac1"]
                    ev_e = p.emit("vector", lambda v,
                                  o=self.stg[:, half * 1024 + (dc % 2) * 512:
                                             half * 1024 + (dc % 2) * 512 + 512],
                                  i=psh[0:128, 0:512]:
                                  v.tensor_copy(o, i),
                                  waits=_later(ev,
                                               self.free.get(("stg2", half))
                                               if dc % 2 == 0 else None,
                                               self.free.get(("stg", half))
                                               if (e == 0 and qc == 0 and dc < 2)
                                               else None),
                                  inc=s_dve)
                    self.free[(pnh,)] = ev_e
                    if dc % 2 == 1:
                        dst = self.outscr.rearrange("(dc p) t -> p dc t", p=128)[
                            :, dc - 1:dc + 1, qc * 512:qc * 512 + 512]
                        src = self.stg[:, half * 1024: half * 1024 + 1024].rearrange(
                            "p (a t) -> p a t", t=512)
                        if ei == 0:
                            w = _later(ev_e, *ev_sp, ev_fin_prev[0]
                                       if ev_fin_prev else None,
                                       ev_fin_prev[1] if ev_fin_prev and
                                       len(ev_fin_prev) > 1 else None)
                        else:
                            w = [ev_e]
                        ev_ac = p.emit("gpsimd", lambda g, d=dst, i=src:
                                       g.dma_start(d, i, accum_op=ALU.add),
                                       waits=w, inc=sd_ac, inc_n=16)
                        self.free[("stg2", half)] = ev_ac
                self.free[("h1g",)] = ev_h2_last
            if ei + 1 < len(e_list):
                nxt = e_list[ei + 1]
                self.ev_w1[nxt] = p.emit(
                    "sync", lambda sy, ee=nxt:
                    sy.dma_start(self.w1s.rearrange("p (ci f) -> p ci f", f=2048),
                                 self.w1_b[ee].rearrange(
                        "(ci p) f -> p ci f", p=128)),
                    waits=_later(self.pe_now(), self.ev_cast[f"w1_{nxt}"]),
                    inc=sd_w1, inc_n=16)
                self.ev_w2[nxt] = p.emit(
                    "sync", lambda sy, ee=nxt:
                    sy.dma_start(self.w2s.rearrange("p (fi d) -> p fi d", d=1024),
                                 self.w2_b[ee].rearrange(
                        "(fi p) d -> p fi d", p=128)),
                    waits=_later(self.pe_now(), self.ev_cast[f"w2_{nxt}"]),
                    inc=sd_w2, inc_n=16)
        self.ev_acc = [(self.sems["sd_ac0"], self.sems["sd_ac0"].n),
                       (self.sems["sd_ac1"], self.sems["sd_ac1"].n)]
        self.free[("stg", 0)] = self.free.get(("stg2", 0))
        self.free[("stg", 1)] = self.free.get(("stg2", 1))
        pe_end = self.pe_now()
        self.free[("wtb",)] = pe_end
        self.free[("C",)] = pe_end
        self.free[("qbuf",)] = pe_end
        self.free[("xnT_w",)] = pe_end
        self.free[("vtok",)] = pe_end
        self.free[("vtok2",)] = (s_pool, s_pool.n)

    def final_phase(self, s):
        p = self.p
        s_pe, s_dve = self.sems["s_pe"], self.sems["s_dve"]
        for ti in range(16):
            b = ti % 2
            sd_f = self.sems["sd_f0"] if b == 0 else self.sems["sd_f1"]
            sd_x = self.sems["sd_x0"] if b == 0 else self.sems["sd_x1"]
            fin = self.fin_in[:, b * 1024:(b + 1) * 1024]
            src = self.outscr.rearrange("(dc p) (ti t) -> p ti dc t",
                                        p=128, t=128)[:, ti]
            ev_f = p.emit("sync", lambda sy, o=fin, i=src:
                          sy.dma_start(o, i),
                          waits=_later(*self.ev_acc, self.free.get(("fin", b))),
                          inc=sd_f, inc_n=16)
            xb = self.x_any[:, b * 1024:(b + 1) * 1024]
            ev_x = p.emit("sync", lambda sy, o=xb, ss=s, t=ti: sy.dma_start(
                o, self.x_in[ss, t * 128:(t + 1) * 128, :]),
                waits=_later(self.free.get(("x_any", b))), inc=sd_x, inc_n=16)
            evt = None
            for dc in range(8):
                if b == 0:
                    po = self.ps_big[:, dc * 128:(dc + 1) * 128]
                    pkeys = [("ps_big",)]
                else:
                    pt = self.ps_c if dc < 4 else self.ps_d
                    po = pt[:, (dc % 4) * 128:(dc % 4) * 128 + 128]
                    pkeys = [("ps_c",), ("ps_d",)]
                w = _later(ev_f,
                           *([self.free.get(k) for k in pkeys] if dc == 0 else []))
                evt = p.emit("tensor", lambda t, o=po,
                             i=fin[:, dc * 128:(dc + 1) * 128]:
                             t.transpose(o, i, self.identF),
                             waits=w, inc=(s_pe if dc == 7 else None))
            self.free[("fin", b)] = evt
            ev_o = None
            for hc in range(2):
                xp = self.out_st[:, b * 1024 + hc * 512: b * 1024 + hc * 512 + 512]
                ev1 = p.emit("vector", lambda v, o=xp,
                             i=xb[:, hc * 512:hc * 512 + 512], t=ti:
                             v.tensor_scalar(o, i, self.rstd1[:, t:t + 1],
                                             self.nmr[:, t:t + 1],
                                             op0=ALU.mult, op1=ALU.add),
                             waits=_later(ev_x, self.free.get(("out_st", b))
                                          if hc == 0 else None),
                             inc=s_dve)
                if b == 0:
                    pa = self.ps_big[:, hc * 512: hc * 512 + 512]
                else:
                    pa = (self.ps_c if hc == 0 else self.ps_d)[:, 0:512]
                ev_o = p.emit("vector", lambda v, o=xp, i=pa:
                              v.tensor_add(o, o, i),
                              waits=_later(evt, ev1), inc=s_dve)
            if b == 0:
                self.free[("ps_big",)] = ev_o
            else:
                self.free[("ps_c",)] = ev_o
                self.free[("ps_d",)] = ev_o
            self.free[("x_any", b)] = ev_o
            sd_st = self.sems["sd_st0"] if b == 0 else self.sems["sd_st1"]
            ev_st = p.emit("sync", lambda sy, ss=s, t=ti,
                           o=self.out_st[:, (ti % 2) * 1024:(ti % 2) * 1024 + 1024]:
                           sy.dma_start(self.y_out[ss, t * 128:(t + 1) * 128, :], o),
                           waits=[ev_o], inc=sd_st, inc_n=16)
            self.free[("out_st", b)] = ev_st
        self.free[("outscr_rd0",)] = (self.sems["sd_f0"], self.sems["sd_f0"].n)
        self.free[("outscr_rd1",)] = (self.sems["sd_f1"], self.sems["sd_f1"].n)
        self.free[("stg", 0)] = self.pe_now()
        self.free[("stg", 1)] = self.pe_now()

    def _dbg_dump(self, dumps):
        p = self.p
        sd_st = self.sems["sd_st0"]
        waits = []
        for nm in ("s_pe", "s_act", "s_dve", "s_pool"):
            waits.append((self.sems[nm], self.sems[nm].n))
        for nm in ("sd_wa", "sd_wb", "sd_w1", "sd_w2", "sd_x0", "sd_x1",
                   "sd_f0", "sd_f1", "sd_sp0", "sd_sp1", "sd_st0", "sd_st1",
                   "sd_ac0", "sd_ac1", "sd_g", "sd_misc"):
            waits.append((self.sems[nm], self.sems[nm].n))
        first = True
        for nm, ap, shape, dt in dumps:
            out = self.nc.dram_tensor("dbg_" + nm, shape, dt,
                                      kind="ExternalOutput").ap()
            p.emit("sync", lambda sy, o=out, i=ap: sy.dma_start(o, i),
                   waits=waits if first else (), inc=sd_st, inc_n=16)
            first = False
        p.emit("sync", None, waits=[(sd_st, sd_st.n)])

    def build_debug(self, stage):
        self.setup()
        p = self.p
        self.phase0()
        self.ev_w1 = {}
        self.ev_w2 = {}
        s = 0
        dumps = []
        while True:
            if stage == "bias":
                dumps = [("biasc", self.biasc, [128, 224], F32),
                         ("braw", self.braw, [8, 2048], F32)]
                break
            self.ev_wtA = self.load_wt(0, "aq",
                                       extra_waits=[self.free.get(("wtb",))])
            self.ev_wtB = self.load_wt(1, "ak",
                                       extra_waits=[self.free.get(("wtb",))])
            self.ln_phase(s)
            if stage == "ln":
                dumps = [("xnT", self.xnT, [128, 8 * S], BF16),
                         ("negm", self.negm, [128, 16], F32),
                         ("rstd", self.rstd, [128, 16], F32)]
                break
            self.conv_phase(s)
            self.kv1_phase(s)
            if stage == "kv1":
                dumps = [("xdT", self.xdT, [128, 8192], BF16),
                         ("kT", self.kT, [128, 8192], BF16),
                         ("v1tok", self.v1tok, [128, 8320], BF16)]
                break
            self.ev_wtB = self.load_wt(1, "ao")
            self.q_proj(s, 0, 1)
            if stage == "q1":
                dumps = [("qT", self.qT, [128, 8192], BF16)]
                break
            self.attn_phase(s, 0, 1)
            if stage == "att1":
                dumps = [("oT", self.oT, [128, 8192], BF16),
                         ("osb", self.osb, [128, 1024], F32)]
                break
            self.free[("qbuf",)] = self.pe_now()
            self.out_proj(s, 0)
            self.free[("oa",)] = self.pe_now()
            if stage == "outp":
                dumps = [("oscr", self.outscr, [D, S], F32)]
                break
            self.q_proj(s, 1, 1)
            self.attn_phase(s, 1, 1)
            self.free[("qbuf",)] = self.pe_now()
            self.out_proj(s, 1)
            self.free[("oa",)] = self.pe_now()
            self.ev_wtA = self.load_wt(0, "fq", extra_waits=[self.pe_now()])
            self.ev_wtB = self.load_wt(1, "fk", extra_waits=[self.pe_now()])
            self.kv2_phase(s)
            if stage == "kv2":
                dumps = [("k2T", self.k2T, [64, 16384], BF16),
                         ("v2tok", self.v2tok, [128, 8448], BF16)]
                break
            self.ev_foB = self.load_wt(1, "fo")
            self.q_proj(s, 0, 2)
            if stage == "q2":
                dumps = [("q2c", self.q2c, [64, 8192], BF16)]
                break
            self.attn_phase(s, 0, 2)
            if stage == "att2":
                dumps = [("aoT", self.aoT, [64, 8192], BF16)]
                break
            self.free[("qbuf",)] = self.pe_now()
            self.mla_proj(s, 0)
            self.free[("oa",)] = self.pe_now()
            if stage == "mla":
                dumps = [("mlaT", self.mlaT, [128, 8 * S], BF16)]
                break
            self.router_phase(s, 0)
            if stage == "rout":
                dumps = [("combT", self.combT, [8, S], BF16),
                         ("probsL", self.probsL, [8, 1024], F32),
                         ("combF", self.combF, [128, 8], F32),
                         ("eq8", self.eq8, [128, 8], F32),
                         ("stats", self.stats, [128, 80], F32),
                         ("combB", self.combB, [128, 8], BF16)]
                break
            self.q_proj(s, 1, 2)
            self.attn_phase(s, 1, 2)
            self.free[("qbuf",)] = self.pe_now()
            self.mla_proj(s, 1)
            self.free[("oa",)] = self.pe_now()
            self.router_phase(s, 1)
            e0 = int(os.environ.get("KDBG_ELIST", "0").split(",")[0])
            self.ev_w1[e0] = p.emit("sync", lambda sy, ee=e0: sy.dma_start(
                self.w1s.rearrange("p (ci f) -> p ci f", f=2048),
                self.w1_b[ee].rearrange("(ci p) f -> p ci f", p=128)),
                waits=_later(self.ev_cast[f"w1_{e0}"], self.pe_now()),
                inc=self.sems["sd_w1"], inc_n=16)
            self.ev_w2[e0] = p.emit("sync", lambda sy, ee=e0: sy.dma_start(
                self.w2s.rearrange("p (fi d) -> p fi d", d=1024),
                self.w2_b[ee].rearrange("(fi p) d -> p fi d", p=128)),
                waits=_later(self.ev_cast[f"w2_{e0}"], self.pe_now()),
                inc=self.sems["sd_w2"], inc_n=16)
            self.moe_phase(s)
            if stage == "moe" and os.environ.get("KDBG_H1PS"):
                dumps = [("h1ps", self.out_st, [128, 2048], F32),
                         ("mlaT", self.mlaT, [128, 8 * S], BF16),
                         ("w1s", self.w1s, [128, 16384], BF16)]
                break
            if stage == "moe":
                dumps = [("oscr", self.outscr, [D, S], F32),
                         ("combT", self.combT, [8, S], BF16),
                         ("mlaT", self.mlaT, [128, 8 * S], BF16),
                         ("h1g", self.h1g, [128, 8192], BF16),
                         ("bcast", self.bcast, [128, 2048], BF16),
                         ("w1s", self.w1s, [128, 16384], BF16),
                         ("w2s", self.w2s, [128, 16384], BF16),
                         ("b2s", self.b2s, [8, D], BF16),
                         ("attb", self.attb, [128, 2048], BF16)]
                break
            self.final_phase(s)
            dumps = [("y0", self.y_out[0], [S, D], F32)]
            break
        self._dbg_dump(dumps)
        with self.nc.Block() as block:
            block.tensor(lambda t: self.p.run("tensor", t))
            block.vector(lambda v: self.p.run("vector", v))
            block.scalar(lambda sc: self.p.run("scalar", sc))
            block.gpsimd(lambda g: self.p.run("gpsimd", g))
            block.sync(lambda sy: self.p.run("sync", sy))
        return self.nc

    def build(self):
        self.setup()
        p = self.p
        self.phase0()
        self.ev_w1 = {}
        self.ev_w2 = {}
        for s in range(NSEQ):
            self.ev_wtA = self.load_wt(0, "aq",
                                       extra_waits=[self.free.get(("wtb",))])
            self.ev_wtB = self.load_wt(1, "ak",
                                       extra_waits=[self.free.get(("wtb",))])
            self.ln_phase(s)
            self.conv_phase(s)
            self.kv1_phase(s)
            self.ev_wtB = self.load_wt(1, "ao")
            for qh in range(2):
                self.q_proj(s, qh, 1)
                self.attn_phase(s, qh, 1)
                self.free[("qbuf",)] = self.pe_now()
                self.out_proj(s, qh)
                self.free[("oa",)] = self.pe_now()
            self.ev_wtA = self.load_wt(0, "fq", extra_waits=[self.pe_now()])
            self.ev_wtB = self.load_wt(1, "fk", extra_waits=[self.pe_now()])
            self.kv2_phase(s)
            self.ev_foB = self.load_wt(1, "fo")
            for qh in range(2):
                self.q_proj(s, qh, 2)
                self.attn_phase(s, qh, 2)
                self.free[("qbuf",)] = self.pe_now()
                self.mla_proj(s, qh)
                self.free[("oa",)] = self.pe_now()
                self.router_phase(s, qh)
            e0 = int(os.environ.get("KDBG_ELIST", "0").split(",")[0])
            self.ev_w1[e0] = p.emit("sync", lambda sy, ee=e0: sy.dma_start(
                self.w1s.rearrange("p (ci f) -> p ci f", f=2048),
                self.w1_b[ee].rearrange("(ci p) f -> p ci f", p=128)),
                waits=_later(self.ev_cast[f"w1_{e0}"], self.pe_now()),
                inc=self.sems["sd_w1"], inc_n=16)
            self.ev_w2[e0] = p.emit("sync", lambda sy, ee=e0: sy.dma_start(
                self.w2s.rearrange("p (fi d) -> p fi d", d=1024),
                self.w2_b[ee].rearrange("(fi p) d -> p fi d", p=128)),
                waits=_later(self.ev_cast[f"w2_{e0}"], self.pe_now()),
                inc=self.sems["sd_w2"], inc_n=16)
            self.moe_phase(s)
            self.final_phase(s)
        p.emit("sync", None, waits=[(self.sems["sd_st0"], self.sems["sd_st0"].n),
                                    (self.sems["sd_st1"], self.sems["sd_st1"].n)])

        with self.nc.Block() as block:
            block.tensor(lambda t: self.p.run("tensor", t))
            block.vector(lambda v: self.p.run("vector", v))
            block.scalar(lambda sc: self.p.run("scalar", sc))
            block.gpsimd(lambda g: self.p.run("gpsimd", g))
            block.sync(lambda sy: self.p.run("sync", sy))
        return self.nc


_built = None


def _get_built():
    global _built
    if _built is None:
        _built = Build().build()
    return _built


def kernel(**inputs):
    nc = _get_built()
    x = np.ascontiguousarray(inputs["L0_attn_x"])
    shards = np.split(x, 8, axis=0)
    weights = {k: np.ascontiguousarray(v) for k, v in inputs.items()
               if k != "L0_attn_x"}
    in_maps = []
    for c in range(8):
        m = {"x": shards[c]}
        m.update(weights)
        in_maps.append(m)
    res = run_bass_kernel_spmd(nc, in_maps, core_ids=list(range(8)))
    return np.concatenate([r["y"] for r in res.results], axis=0)


# version 02
# speedup vs baseline: 1.0059x; 1.0059x over previous
import math
import os
import numpy as np
from contextlib import ExitStack

import concourse.bass as bass
import concourse.mybir as mybir
from concourse.bass_utils import run_bass_kernel_spmd

F32 = mybir.dt.float32
BF16 = mybir.dt.bfloat16
AF = mybir.ActivationFunctionType
ALU = mybir.AluOpType
AX = mybir.AxisListType.X

NSEQ = 4
S = 2048
D = 1024
H = 16
KPH = 32
LAT = 512
E = 8
DFF = 2048
SC1 = 1.0 / 8.0
SC2 = 1.0 / math.sqrt(KPH)

ENGINES = ("tensor", "vector", "scalar", "gpsimd", "sync")


class _Sem:
    __slots__ = ("h", "n", "name")

    def __init__(self, h, name):
        self.h = h
        self.n = 0
        self.name = name


class _Prog:
    def __init__(self):
        self.ops = {e: [] for e in ENGINES}

    def emit(self, eng, fn, waits=(), inc=None, inc_n=1):
        w = tuple((s, v) for (s, v) in waits if s is not None and v > 0)
        ev = None
        if inc is not None:
            inc.n += inc_n
            ev = (inc, inc.n)
        self.ops[eng].append((w, fn, inc, inc_n if inc else 0))
        return ev

    def run(self, eng_name, eng):
        seen = {}
        for w, fn, inc, n in self.ops[eng_name]:
            for (s, v) in w:
                if seen.get(s.name, 0) >= v:
                    continue
                seen[s.name] = v
                eng.wait_ge(s.h, v)
            ins = fn(eng) if fn is not None else None
            if inc is not None:
                ins.then_inc(inc.h, n)


def _later(*evs):
    best = {}
    for ev in evs:
        if ev is None:
            continue
        s, v = ev
        if s is None or v <= 0:
            continue
        if best.get(s.name, (None, 0))[1] < v:
            best[s.name] = (s, v)
    return list(best.values())


class Build:
    def __init__(self):
        self.nc = bass.Bass("TRN2", target_bir_lowering=False)
        self.p = _Prog()
        self.ctx = ExitStack()
        self.sems = {}
        self.free = {}

    def sem(self, name):
        if name not in self.sems:
            h = self.ctx.enter_context(self.nc.semaphore(name))
            self.sems[name] = _Sem(h, name)
        return self.sems[name]

    def pe_now(self):
        return (self.sems["s_pe"], self.sems["s_pe"].n)

    def setup(self):
        nc = self.nc
        self.x_in = nc.dram_tensor("x", [NSEQ, S, D], F32, kind="ExternalInput").ap()
        wn = lambda n, shp: nc.dram_tensor(n, shp, F32, kind="ExternalInput").ap()
        self.aqw = wn("L0_attn_q_w", [D, D]);   self.aqb = wn("L0_attn_q_b", [D])
        self.akw = wn("L0_attn_k_w", [D, D]);   self.akb = wn("L0_attn_k_b", [D])
        self.avw = wn("L0_attn_v_w", [D, D]);   self.avb = wn("L0_attn_v_b", [D])
        self.acw = wn("L0_attn_conv_w", [D, 1, 3]); self.acb = wn("L0_attn_conv_b", [D])
        self.aow = wn("L0_attn_out_w", [D, D]); self.aob = wn("L0_attn_out_b", [D])
        self.fqw = wn("L0_ffn_q_w", [D, D]);    self.fqb = wn("L0_ffn_q_b", [D])
        self.fkw = wn("L0_ffn_k_w", [D, LAT]);  self.fkb = wn("L0_ffn_k_b", [LAT])
        self.fvw = wn("L0_ffn_v_w", [D, LAT]);  self.fvb = wn("L0_ffn_v_b", [LAT])
        self.fow = wn("L0_ffn_o_w", [D, D]);    self.fob = wn("L0_ffn_o_b", [D])
        self.frw = wn("L0_ffn_router_w", [D, E]); self.frb = wn("L0_ffn_router_b", [E])
        self.w1 = wn("L0_ffn_w1", [E, D, DFF]); self.b1 = wn("L0_ffn_b1", [E, DFF])
        self.w2 = wn("L0_ffn_w2", [E, DFF, D]); self.b2 = wn("L0_ffn_b2", [E, D])
        self.y_out = nc.dram_tensor("y", [NSEQ, S, D], F32, kind="ExternalOutput").ap()

        ds = lambda n, shp, dt: nc.dram_tensor(n, shp, dt, kind="Internal").ap()
        self.aqw_b = ds("aqw_b", [D, D], BF16)
        self.akw_b = ds("akw_b", [D, D], BF16)
        self.avw_b = ds("avw_b", [D, D], BF16)
        self.aow_b = ds("aow_b", [D, D], BF16)
        self.fqw_b = ds("fqw_b", [D, D], BF16)
        self.fkw_b = ds("fkw_b", [D, LAT], BF16)
        self.fvw_b = ds("fvw_b", [D, LAT], BF16)
        self.fow_b = ds("fow_b", [LAT, D], BF16)
        self.w1_b = ds("w1_b", [E, D, DFF], BF16)
        self.w2_b = ds("w2_b", [E, DFF, D], BF16)
        self.outscr = ds("outscr", [D, S], F32)

        sel = np.zeros((E, E * 128), dtype=np.float32)
        for e in range(E):
            sel[e, e * 128:(e + 1) * 128] = 1.0
        import ml_dtypes
        self.selc_d = nc.inline_tensor(sel.astype(ml_dtypes.bfloat16), name="selcd").ap()

        arena = nc.alloc_sbuf_tensor("arena", [128, 206848], mybir.dt.uint8)
        base = nc.lookup_mloc(arena).addr
        self._cur = base
        self._limit = base + 206848

        def at(name, shape, dt, off=None):
            nbytes = shape[1] * (2 if dt == BF16 else 4)
            if off is None:
                off = self._cur
                self._cur = off + ((nbytes + 31) // 32) * 32
            assert off + nbytes <= self._limit, (name, off + nbytes - base)
            return nc.alloc_sbuf_tensor_at(name, shape, dt, offset=off).ap(), off

        self.xnT, off_A = at("xnT", [128, 8 * S], BF16)
        self.mlaT, _ = at("mlaT", [128, 8 * S], BF16, off=off_A)
        self.qT, off_B = at("qT", [128, 8192], BF16)
        self.q2c, _ = at("q2c", [64, 8192], BF16, off=off_B)
        self.h1g, _ = at("h1g", [128, 8192], BF16, off=off_B)
        self.xdT, off_C = at("xdT", [128, 8192], BF16)
        self.kT, _ = at("kT", [128, 8192], BF16)
        self.k2T, _ = at("k2T", [64, 16384], BF16, off=off_C)
        self.w2s, _ = at("w2s", [128, 16384], BF16, off=off_C)
        self.v1tok, off_D = at("v1tok", [128, 8320], BF16)
        self.v2tok, _ = at("v2tok", [128, 8448], BF16, off=off_D)
        self.bcast, _ = at("bcast", [128, 2048], BF16, off=off_D)
        self._cur = max(self._cur, off_D + 16896)
        self.oT, off_E = at("oT", [128, 8192], BF16)
        self.aoT, _ = at("aoT", [64, 8192], BF16, off=off_E)
        self.w1s, off_F = at("w1s", [128, 16384], BF16)
        self.wtA, _ = at("wtA", [128, 8192], BF16, off=off_F)
        self.wtB, _ = at("wtB", [128, 8192], BF16, off=off_F + 16384)
        self.foB, _ = at("foB", [64, 8192], BF16, off=off_F + 16384)
        self.attb, _ = at("attb", [128, 2048], BF16)
        self.identB, _ = at("identB", [128, 128], BF16)
        self.onesB, _ = at("onesB", [1, 128], BF16)
        self.selc, _ = at("selc", [8, E * 128], BF16)
        self.b2s, _ = at("b2s", [8, D], BF16)
        self.combT, _ = at("combT", [8, S], BF16)
        self.vb1, _ = at("vb1", [1, D], BF16)
        self.v2b, _ = at("v2b", [1, LAT], BF16)
        self.routw, _ = at("routw", [128, 64], BF16)
        self.combB, _ = at("combB", [128, 8], BF16)
        self.stg, off_I = at("stg", [128, 2048], F32)
        self.braw, _ = at("braw", [8, 2048], F32, off=off_I)
        self.osb, _ = at("osb", [128, 1024], F32)
        self.sums, off_K = at("sums", [1, 1024], F32)
        self.probsL, _ = at("probsL", [8, 1024], F32, off=off_K)
        self._cur = max(self._cur, off_K + 4096)
        self.recip, _ = at("recip", [1, 1024], F32)
        self.x_any, _ = at("x_any", [128, 2048], F32)
        self.fin_in, _ = at("fin_in", [128, 2048], F32, off=off_I)
        self.out_st, off_N = at("out_st", [128, 2048], F32)
        self.sq_scr, _ = at("sq_scr", [128, 1024], F32, off=off_N)
        self.identF, _ = at("identF", [128, 128], F32)
        self.onesF, _ = at("onesF", [1, 128], F32)
        self.stats, _ = at("stats", [128, 80], F32)
        self.biasc, _ = at("biasc", [128, 224], F32)
        self.routb, _ = at("routb", [8, 1], F32)
        self.combF, _ = at("combF", [128, 8], F32)
        self.eq8, _ = at("eq8", [128, 8], F32)
        self.constv, _ = at("constv", [128, 8], F32)
        assert self._cur <= self._limit, self._cur - base

        bc = self.biasc
        self.qbcol = bc[:, 0:8]
        self.kbcol = bc[:, 8:16]
        self.q2bcol = bc[:, 16:24]
        self.k2bcol = bc[:, 24:28]
        self.obcol = bc[:, 28:36]
        self.outbcol = bc[:, 36:44]
        self.convbcol = bc[:, 44:52]
        self.convw_j = [bc[:, 52 + 8 * j:60 + 8 * j] for j in range(3)]
        self.b1col = bc[:, 76:204]
        st = self.stats
        self.negm = st[:, 0:16]
        self.rstd = st[:, 16:32]
        self.rstd1 = st[:, 32:48]
        self.nmr = st[:, 48:64]
        self.varsum = st[:, 64:65]
        self.t0 = st[:, 65:66]
        self.t1 = st[:, 66:67]
        self.t2 = st[:, 67:68]
        self.t3 = st[:, 68:69]

        self.ps_a = self.ctx.enter_context(nc.psum_tensor("ps_a", [128, 512], F32)).ap()
        self.ps_b = self.ctx.enter_context(nc.psum_tensor("ps_b", [128, 512], F32)).ap()
        self.ps_big = self.ctx.enter_context(nc.psum_tensor("ps_big", [128, 1024], F32)).ap()
        self.ps_tr = self.ctx.enter_context(nc.psum_tensor("ps_tr", [128, 2048], BF16)).ap()
        self.ps_c = self.ctx.enter_context(nc.psum_tensor("ps_c", [128, 512], F32)).ap()
        self.ps_d = self.ctx.enter_context(nc.psum_tensor("ps_d", [128, 512], F32)).ap()

        for n in ("sd_wa", "sd_wb", "sd_w1", "sd_w2", "sd_x0", "sd_x1",
                  "sd_f0", "sd_f1", "sd_sp0", "sd_sp1", "sd_st0", "sd_st1",
                  "sd_ac0", "sd_ac1", "sd_g", "sd_misc",
                  "s_pe", "s_act", "s_dve", "s_pool"):
            self.sem(n)

    def acc_group(self, ps, pname, pairs, waits0, rows=128, cols=512, col0=0):
        p, s_pe = self.p, self.sems["s_pe"]
        n = len(pairs)
        ev = None
        for i, (lt, rt) in enumerate(pairs):
            w = []
            if i == 0:
                w = _later(self.free.get((pname,)), *waits0)
            ev = p.emit("tensor",
                        lambda t, l=lt, r=rt, st=(i == 0), sp=(i == n - 1),
                        o=ps[0:rows, col0:col0 + cols]:
                        t.matmul(o, l, r, start=st, stop=sp),
                        waits=w, inc=(s_pe if i == n - 1 else None))
        return ev

    def phase0(self):
        p = self.p
        sd_g, sd_misc = self.sems["sd_g"], self.sems["sd_misc"]
        s_pe, s_dve, s_pool = (self.sems[n] for n in ("s_pe", "s_dve", "s_pool"))

        casts = [("aq", self.aqw_b, self.aqw), ("ak", self.akw_b, self.akw),
                 ("av", self.avw_b, self.avw), ("ao", self.aow_b, self.aow),
                 ("fq", self.fqw_b, self.fqw), ("fk", self.fkw_b, self.fkw),
                 ("fv", self.fvw_b, self.fvw), ("fo", self.fow_b, self.fow[0:LAT, :])]
        for e in range(E):
            casts.append((f"w1_{e}", self.w1_b[e], self.w1[e]))
            casts.append((f"w2_{e}", self.w2_b[e], self.w2[e]))
        self.ev_cast = {}
        for nm, out_ap, in_ap in casts:
            self.ev_cast[nm] = p.emit(
                "gpsimd", lambda g, o=out_ap, i=in_ap: g.dma_start(o, i),
                inc=sd_g, inc_n=16)
        self.ev_cast_all = (sd_g, sd_g.n)

        def mkident(g, ident, size):
            g.memset(ident, 0.0)
            return g.affine_select(out=ident, in_=ident, compare_op=ALU.not_equal,
                                   fill=1.0, base=0, pattern=[[-1, size]],
                                   channel_multiplier=1)
        p.emit("gpsimd", lambda g: mkident(g, self.identB, 128))
        p.emit("gpsimd", lambda g: mkident(g, self.identF, 128))
        p.emit("gpsimd", lambda g: g.memset(self.onesB, 1.0))
        for i, val in enumerate([1e-5, 1.0 / D, SC1, SC2]):
            ap = self.constv[:, i:i + 1]
            p.emit("gpsimd", lambda g, o=ap, v=val: g.memset(o, v))
            self.nc.const_aps.register(F32, val, ap)
        p.emit("gpsimd", lambda g: g.memset(self.onesF, 1.0), inc=s_pool)
        self.ev_consts = (s_pool, s_pool.n)

        for out_ap, in_ap, slow in [
                (self.selc, self.selc_d, False),
                (self.b2s, self.b2, False),
                (self.vb1, self.avb.rearrange("(o d) -> o d", o=1), False),
                (self.v2b, self.fvb.rearrange("(o d) -> o d", o=1), False),
                (self.routw.rearrange("p (ci e) -> p ci e", e=8),
                 self.frw.rearrange("(ci p) e -> p ci e", p=128), True)]:
            p.emit("gpsimd", lambda g, o=out_ap, i=in_ap, sl=slow:
                   g.dma_start(o, i, allow_slow_non_contiguous=sl), inc=sd_g, inc_n=16)
        self.ev_small = (sd_g, sd_g.n)

        ev_copy2 = None
        ev_copy1 = None
        slot = [0]

        def tr_step(src_ap, rows, dest_ap, ev_ld):
            nonlocal ev_copy1, ev_copy2
            po = (slot[0] % 2) * 512
            w = _later(ev_ld, self.ev_consts, ev_copy2)
            ev_tr = p.emit("tensor", lambda t, s=src_ap, r=rows, o=po:
                           t.transpose(self.ps_big[0:128, o:o + r], s,
                                       self.identF[0:r, 0:r]),
                           waits=w, inc=s_pe)
            ev_c = p.emit("vector", lambda v, d=dest_ap, o=po, r=rows:
                          v.tensor_copy(d, self.ps_big[0:128, o:o + r]),
                          waits=[ev_tr], inc=s_dve)
            ev_copy2, ev_copy1 = ev_copy1, ev_c
            slot[0] += 1

        jobs = [(self.aqb, 8, self.qbcol), (self.akb, 8, self.kbcol),
                (self.fqb, 8, self.q2bcol), (self.fkb, 4, self.k2bcol),
                (self.fob, 8, self.obcol), (self.aob, 8, self.outbcol),
                (self.acb, 8, self.convbcol)]
        for vec, rows, dest in jobs:
            src = self.braw[0:rows, 0:128]
            ev_ld = p.emit("sync", lambda sy, o=src,
                           i=vec.rearrange("(a p) -> a p", p=128): sy.dma_start(o, i),
                           waits=[self.pe_now()], inc=sd_misc, inc_n=16)
            tr_step(src, rows, dest[:, 0:rows], ev_ld)
        b1raw = self.braw[0:8, 0:2048]
        ev_ld = p.emit("sync", lambda sy: sy.dma_start(b1raw, self.b1),
                       waits=[self.pe_now()], inc=sd_misc, inc_n=16)
        for fi in range(16):
            tr_step(b1raw[0:8, fi * 128:(fi + 1) * 128], 8,
                    self.b1col[:, fi * 8:fi * 8 + 8], ev_ld)
        cwraw = self.braw[0:8, 0:384]
        ev_ld = p.emit("sync", lambda sy: sy.dma_start(
            cwraw, self.acw.rearrange("(a p) one j -> a (p one j)", a=8)),
            waits=[self.pe_now()], inc=sd_misc, inc_n=16)
        for j in range(3):
            cw2 = self.braw[0:8, 1024 + j * 128: 1024 + (j + 1) * 128]
            ev_cp = p.emit("vector", lambda v, o=cw2, i=cwraw[0:8, j:384:3]:
                           v.tensor_copy(o, i), waits=[ev_ld], inc=s_dve)
            tr_step(cw2, 8, self.convw_j[j], ev_cp)
        p.emit("sync", lambda sy: sy.dma_start(
            self.routb, self.frb.rearrange("(e o) -> e o", o=1)),
            waits=[self.pe_now()], inc=sd_misc, inc_n=16)
        self.ev_bias_done = ev_copy1
        self.ev_misc_ld = (sd_misc, sd_misc.n)
        self.ev_p0_pe = self.pe_now()
        self.free[("ps_big",)] = ev_copy1
        self.free[("stg", 0)] = self.ev_p0_pe
        self.free[("stg", 1)] = self.ev_p0_pe

    def load_wt(self, half, kind, extra_waits=()):
        p = self.p
        sem = self.sems["sd_wa"] if half == 0 else self.sems["sd_wb"]
        dst_t = self.wtA if half == 0 else self.wtB
        waits = _later(self.free.get(("wt", half)), self.ev_cast_all, *extra_waits)
        ev = None
        if kind in ("aq", "ak", "av", "ao"):
            src = {"aq": self.aqw_b, "ak": self.akw_b,
                   "av": self.avw_b, "ao": self.aow_b}[kind]
            for ci in range(8):
                ev = p.emit("scalar",
                            lambda sc, d=dst_t[:, ci * 1024:(ci + 1) * 1024],
                            i=src[:, ci * 128:(ci + 1) * 128]:
                            sc.dma_start(d, i, transpose=True),
                            waits=waits if ci == 0 else (), inc=sem, inc_n=16)
        elif kind == "fq":
            ev = p.emit("sync", lambda sy,
                        d=dst_t[:, 0:8192].rearrange("p (ci o) -> p ci o", o=1024):
                        sy.dma_start(
                d, self.fqw_b.rearrange("(ci p) o -> p ci o", p=128)),
                waits=waits, inc=sem, inc_n=16)
        elif kind in ("fk", "fv"):
            src = self.fkw_b if kind == "fk" else self.fvw_b
            ev = p.emit("sync", lambda sy,
                        d=dst_t[:, 0:4096].rearrange("p (ci o) -> p ci o", o=512),
                        s=src: sy.dma_start(
                d, s.rearrange("(ci p) o -> p ci o", p=128)),
                waits=waits, inc=sem, inc_n=16)
        elif kind == "fo":
            ev = p.emit("sync", lambda sy,
                        d=self.foB.rearrange("p (kc o) -> p kc o", o=1024):
                        sy.dma_start(
                d, self.fow_b.rearrange("(kc p) o -> p kc o", p=64)),
                waits=waits, inc=sem, inc_n=16)
        return ev

    def ln_phase(self, s):
        p = self.p
        s_pe, s_act, s_dve = (self.sems[n] for n in ("s_pe", "s_act", "s_dve"))
        for ti in range(16):
            b = ti % 2
            sd_x = self.sems["sd_x0"] if b == 0 else self.sems["sd_x1"]
            xb = self.x_any[:, b * 1024:(b + 1) * 1024]
            ev_x = p.emit("sync", lambda sy, o=xb, ss=s, t=ti: sy.dma_start(
                o, self.x_in[ss, t * 128:(t + 1) * 128, :]),
                waits=_later(self.free.get(("x_any", b))), inc=sd_x, inc_n=16)
            nm = self.negm[:, ti:ti + 1]
            rs = self.rstd[:, ti:ti + 1]
            ev = p.emit("vector", lambda v, o=nm, i=xb:
                        v.reduce_sum(o, i, axis=AX, negate=True),
                        waits=[ev_x], inc=s_dve)
            ev = p.emit("vector", lambda v, o=nm:
                        v.tensor_scalar_mul(o, o, 1.0 / D), waits=[ev], inc=s_dve)
            ev = p.emit("scalar", lambda sc, i=xb, n=nm:
                        sc.activation(self.sq_scr, i, AF.Square, bias=n,
                                      accum_out=self.varsum),
                        waits=_later(ev, self.free.get(("out_st", 0))
                                     if ti == 0 else None,
                                     self.ev_consts if ti == 0 else None),
                        inc=s_act)
            ev = p.emit("scalar", lambda sc, r=rs:
                        sc.activation(r, self.varsum, AF.Sqrt, scale=1.0 / D,
                                      bias=1e-5),
                        waits=[ev], inc=s_act)
            ev = p.emit("vector", lambda v, r=rs: v.reciprocal(r, r),
                        waits=[ev], inc=s_dve)
            p.emit("vector", lambda v, t=ti: v.tensor_scalar_add(
                self.rstd1[:, t:t + 1], self.rstd[:, t:t + 1], 1.0),
                waits=[ev], inc=s_dve)
            p.emit("vector", lambda v, t=ti: v.tensor_mul(
                self.nmr[:, t:t + 1], self.negm[:, t:t + 1], self.rstd[:, t:t + 1]),
                inc=s_dve)
            xn_tok = self.attb[:, b * 1024:(b + 1) * 1024]
            wv = _later(ev, self.free.get(("attb", b)),
                        self.free.get(("attb_s", 0)) if (b == 0 and ti == 0) else None,
                        self.free.get(("attb_s", 1)) if (b == 0 and ti == 0) else None)
            ev = p.emit("vector", lambda v, o=xn_tok, i=xb, t=ti:
                        v.tensor_scalar(o, i, self.negm[:, t:t + 1],
                                        self.rstd[:, t:t + 1],
                                        op0=ALU.add, op1=ALU.mult),
                        waits=wv, inc=s_dve)
            self.free[("x_any", b)] = ev
            ho = b * 1024
            evt = None
            for k in range(8):
                wts = _later(ev, self.ev_p0_pe,
                             self.free.get(("ps_tr", b)) if k == 0 else None,
                             self.free.get(("xnT_w",)) if (ti == 0 and k == 0) else None)
                evt = p.emit("tensor", lambda t, kk=k, o=ho, i=xn_tok:
                             t.transpose(self.ps_tr[:, o + kk * 128:o + (kk + 1) * 128],
                                         i[:, kk * 128:(kk + 1) * 128], self.identB),
                             waits=wts, inc=(s_pe if k == 7 else None))
            self.free[("attb", b)] = evt
            dest = self.xnT.rearrange("p (c t) -> p c t", t=S)[:, :, ti * 128:(ti + 1) * 128]
            ev_e = p.emit("vector", lambda v, d=dest, o=ho:
                          v.tensor_copy(d, self.ps_tr[:, o:o + 1024]
                                        .rearrange("p (c t) -> p c t", t=128)),
                          waits=[evt], inc=s_dve)
            self.free[("ps_tr", b)] = ev_e
        self.ev_xnT = (s_dve, s_dve.n)

    def conv_phase(self, s):
        p = self.p
        s_pool = self.sems["s_pool"]
        s_dve = self.sems["s_dve"]
        tA = self.x_any[:, 0:1024]
        tB = self.x_any[:, 1024:2048]
        ev = None
        for ci in range(8):
            xn = self.xnT[:, ci * S:(ci + 1) * S]
            xd = self.xdT[:, ci * 1024:(ci + 1) * 1024]
            w0 = _later(self.ev_xnT, self.ev_bias_done,
                        self.free.get(("C",)) if ci == 0 else None,
                        self.free.get(("x_any", 0)) if ci == 0 else None,
                        self.free.get(("x_any", 1)) if ci == 0 else None)
            ev = p.emit("vector", lambda v, o=tB[:, 1:1024], i=xn[:, 1:2046:2], c=ci:
                        v.tensor_scalar_mul(o, i, self.convw_j[0][:, c:c + 1]),
                        waits=w0, inc=s_dve)
            for step in [
                    lambda v, o=tA[:, 1:1024], i=xn[:, 2:2047:2], c=ci:
                    v.scalar_tensor_tensor(o, i, self.convw_j[1][:, c:c + 1],
                                           tB[:, 1:1024], op0=ALU.mult, op1=ALU.add),
                    lambda v, o=tB[:, 1:1024], i=xn[:, 3:2048:2], c=ci:
                    v.scalar_tensor_tensor(o, i, self.convw_j[2][:, c:c + 1],
                                           tA[:, 1:1024], op0=ALU.mult, op1=ALU.add),
                    lambda v, o=tA[:, 0:1], i=xn[:, 0:1], c=ci:
                    v.tensor_scalar_mul(o, i, self.convw_j[1][:, c:c + 1]),
                    lambda v, o=tB[:, 0:1], i=xn[:, 1:2], c=ci:
                    v.scalar_tensor_tensor(o, i, self.convw_j[2][:, c:c + 1],
                                           tA[:, 0:1], op0=ALU.mult, op1=ALU.add),
                    lambda v, o=xd, c=ci:
                    v.tensor_scalar_add(o, tB[:, 0:1024], self.convbcol[:, c:c + 1])]:
                ev = p.emit("vector", step, waits=[ev], inc=s_dve)
        self.free[("x_any", 0)] = ev
        self.free[("x_any", 1)] = ev
        self.ev_xdT = ev

    def kv1_phase(self, s):
        p = self.p
        s_act, s_dve, s_pool = (self.sems[n] for n in ("s_act", "s_dve", "s_pool"))
        ev_k = self.ev_wtB
        ev_cfree = self.free.get(("C",))
        slot = 0
        for ii in range(8):
            for qc in range(2):
                ps = self.ps_a if slot % 2 == 0 else self.ps_b
                pn = "ps_a" if slot % 2 == 0 else "ps_b"
                pairs = [(self.wtB[:, ci * 1024 + ii * 128: ci * 1024 + ii * 128 + 128],
                          self.xdT[:, ci * 1024 + qc * 512: ci * 1024 + qc * 512 + 512])
                         for ci in range(8)]
                ev = self.acc_group(ps, pn, pairs, [ev_k, self.ev_xdT])
                ev_e = p.emit("scalar", lambda sc,
                              o=self.kT[:, ii * 1024 + qc * 512: ii * 1024 + qc * 512 + 512],
                              i=ps, c=ii:
                              sc.activation(o, i, AF.Identity,
                                            bias=self.kbcol[:, c:c + 1]),
                              waits=_later(ev, ev_cfree if slot == 0 else None),
                              inc=s_act)
                self.free[(pn,)] = ev_e
                slot += 1
        self.ev_kT = (s_act, s_act.n)
        self.free[("wt", 1)] = self.pe_now()

        ev_v = self.load_wt(1, "av")
        self.ev_wtB = ev_v
        ev_ones = p.emit("gpsimd", lambda g: g.memset(
            self.v1tok[:, 0:8320].rearrange("p (x d) -> p x d", d=65)[:, :, 64:65], 1.0),
            waits=_later(self.free.get(("vtok",)), self.free.get(("vtok2",))),
            inc=s_pool)
        for kti in range(8):
            for qc in range(2):
                ps = self.ps_a if slot % 2 == 0 else self.ps_b
                pn = "ps_a" if slot % 2 == 0 else "ps_b"
                pairs = [(self.xdT[:, ci * 1024 + kti * 128: ci * 1024 + kti * 128 + 128],
                          self.wtB[:, ci * 1024 + qc * 512: ci * 1024 + qc * 512 + 512])
                         for ci in range(8)]
                pairs.append((self.onesB, self.vb1[:, qc * 512:(qc + 1) * 512]))
                ev = self.acc_group(ps, pn, pairs, [ev_v])
                base = kti * 1040 + qc * 520
                dest = self.v1tok[:, base:base + 520].rearrange(
                    "p (h d) -> p h d", d=65)[:, :, 0:64]
                ev_e = p.emit("vector", lambda v, d=dest,
                              i=ps.rearrange("p (h d) -> p h d", d=64):
                              v.tensor_copy(d, i),
                              waits=_later(ev, ev_ones), inc=s_dve)
                self.free[(pn,)] = ev_e
                slot += 1
        self.ev_v1 = (s_dve, s_dve.n)
        self.free[("wt", 1)] = self.pe_now()

    def kv2_phase(self, s):
        p = self.p
        s_act, s_dve, s_pool = (self.sems[n] for n in ("s_act", "s_dve", "s_pool"))
        ev_k = self.ev_wtB
        ev_cfree = self.pe_now()
        slot = 0
        for hc in range(8):
            for qc in range(4):
                ps = self.ps_a if slot % 2 == 0 else self.ps_b
                pn = "ps_a" if slot % 2 == 0 else "ps_b"
                pairs = [(self.wtB[:, ci * 512 + hc * 64: ci * 512 + hc * 64 + 64],
                          self.xnT[:, ci * S + qc * 512: ci * S + qc * 512 + 512])
                         for ci in range(8)]
                ev = self.acc_group(ps, pn, pairs, [ev_k, self.ev_xnT], rows=64)
                ev_e = p.emit("scalar", lambda sc,
                              o=self.k2T[0:64, hc * 2048 + qc * 512:
                                         hc * 2048 + qc * 512 + 512],
                              i=ps[0:64, :], c=hc:
                              sc.activation(o, i, AF.Identity,
                                            bias=self.k2bcol[(c % 2) * 64:(c % 2) * 64 + 64,
                                                             c // 2:c // 2 + 1]),
                              waits=_later(ev, ev_cfree if slot == 0 else None),
                              inc=s_act)
                self.free[(pn,)] = ev_e
                slot += 1
        self.ev_k2 = (s_act, s_act.n)
        self.free[("wt", 1)] = self.pe_now()

        ev_v = self.load_wt(1, "fv")
        self.ev_wtB = ev_v
        ev_ones = p.emit("gpsimd", lambda g: g.memset(
            self.v2tok[:, 0:8448].rearrange("p (x d) -> p x d", d=33)[:, :, 32:33], 1.0),
            waits=_later(self.pe_now()), inc=s_pool)
        for kti in range(16):
            ps = self.ps_a if slot % 2 == 0 else self.ps_b
            pn = "ps_a" if slot % 2 == 0 else "ps_b"
            pairs = [(self.xnT[:, ci * S + kti * 128: ci * S + kti * 128 + 128],
                      self.wtB[:, ci * 512: ci * 512 + 512])
                     for ci in range(8)]
            pairs.append((self.onesB, self.v2b[:, 0:512]))
            ev = self.acc_group(ps, pn, pairs, [ev_v])
            base = kti * 528
            dest = self.v2tok[:, base:base + 528].rearrange(
                "p (h d) -> p h d", d=33)[:, :, 0:32]
            ev_e = p.emit("vector", lambda v, d=dest,
                          i=ps.rearrange("p (h d) -> p h d", d=32):
                          v.tensor_copy(d, i),
                          waits=_later(ev, ev_ones), inc=s_dve)
            self.free[(pn,)] = ev_e
            slot += 1
        self.ev_v2 = (s_dve, s_dve.n)
        self.free[("wt", 1)] = self.pe_now()

    def q_proj(self, s, qh, branch):
        p = self.p
        s_act = self.sems["s_act"]
        qfree = self.free.get(("qbuf",))
        slot = 0
        for ii in range(8):
            for qc in range(2):
                ps = self.ps_a if slot % 2 == 0 else self.ps_b
                pn = "ps_a" if slot % 2 == 0 else "ps_b"
                pairs = [(self.wtA[:, ci * 1024 + ii * 128: ci * 1024 + ii * 128 + 128],
                          self.xnT[:, ci * S + qh * 1024 + qc * 512:
                                   ci * S + qh * 1024 + qc * 512 + 512])
                         for ci in range(8)]
                ev = self.acc_group(ps, pn, pairs, [self.ev_wtA, self.ev_xnT])
                o0 = ii * 1024 + qc * 512
                if branch == 1:
                    ev_e = p.emit("scalar", lambda sc, o=self.qT[:, o0:o0 + 512],
                                  i=ps, c=ii:
                                  sc.activation(o, i, AF.Identity,
                                                bias=self.qbcol[:, c:c + 1]),
                                  waits=_later(ev, qfree if slot == 0 else None),
                                  inc=s_act)
                else:
                    p.emit("scalar", lambda sc, o=self.q2c[0:32, o0:o0 + 512],
                           i=ps[0:32, :], c=ii:
                           sc.activation(o, i, AF.Identity,
                                         bias=self.q2bcol[0:32, c:c + 1]),
                           waits=_later(ev, qfree if slot == 0 else None), inc=s_act)
                    ev_e = p.emit("scalar", lambda sc, o=self.q2c[32:64, o0:o0 + 512],
                                  i=ps[64:96, :], c=ii:
                                  sc.activation(o, i, AF.Identity,
                                                bias=self.q2bcol[64:96, c:c + 1]),
                                  inc=s_act)
                self.free[(pn,)] = ev_e
                slot += 1
        self.ev_q = (s_act, s_act.n)

    def attn_phase(self, s, qh, branch):
        p = self.p
        s_pe, s_act, s_dve = (self.sems[n] for n in ("s_pe", "s_act", "s_dve"))
        if branch == 1:
            n_kti, dv, sc = 8, 64, SC1
            kv_ready = _later(self.ev_kT, self.ev_v1, self.ev_q)
        else:
            n_kti, dv, sc = 16, 32, SC2
            kv_ready = _later(self.ev_k2, self.ev_v2, self.ev_q)
        oa_free = self.free.get(("oa",))

        def slices(h, kti, qc):
            if branch == 1:
                lt = self.kT[(h % 2) * 64:(h % 2) * 64 + 64,
                             (h // 2) * 1024 + kti * 128:
                             (h // 2) * 1024 + kti * 128 + 128]
                rt = self.qT[(h % 2) * 64:(h % 2) * 64 + 64,
                             (h // 2) * 1024 + qc * 512:
                             (h // 2) * 1024 + qc * 512 + 512]
                vt = self.v1tok[:, kti * 1040 + h * 65: kti * 1040 + h * 65 + 65]
            else:
                lt = self.k2T[(h % 2) * 32:(h % 2) * 32 + 32,
                              (h // 2) * 2048 + kti * 128:
                              (h // 2) * 2048 + kti * 128 + 128]
                rt = self.q2c[(h % 2) * 32:(h % 2) * 32 + 32,
                              (h // 2) * 1024 + qc * 512:
                              (h // 2) * 1024 + qc * 512 + 512]
                vt = self.v2tok[:, kti * 528 + h * 33: kti * 528 + h * 33 + 33]
            return lt, rt, vt

        for hp in range(H // 2):
            h_pair = (2 * hp, 2 * hp + 1)
            ev_av_last = {0: None, 1: None}
            for kti in range(n_kti):
                for qc in range(2):
                    for st_i in (0, 1):
                        h = h_pair[st_i]
                        ps_s = self.ps_a if st_i == 0 else self.ps_b
                        pn = "ps_a" if st_i == 0 else "ps_b"
                        lt, rt, vt = slices(h, kti, qc)
                        w = _later(self.free.get((pn,)),
                                   *(kv_ready if (hp == 0 and kti == 0 and qc == 0)
                                     else []))
                        ev_mm = p.emit("tensor", lambda t, l=lt, r=rt, o=ps_s:
                                       t.matmul(o, l, r, start=True, stop=True),
                                       waits=w, inc=s_pe)
                        slot = st_i * 2 + qc
                        ab = self.attb[:, slot * 512: slot * 512 + 512]
                        ev_x = p.emit("scalar", lambda scl, o=ab, i=ps_s, f=sc:
                                      scl.activation(o, i, AF.Exp, scale=f),
                                      waits=_later(ev_mm,
                                                   self.free.get(("attb4", slot)),
                                                   self.free.get(("attb", 0))
                                                   if (hp == 0 and kti == 0
                                                       and qc == 0) else None,
                                                   self.free.get(("attb", 1))
                                                   if (hp == 0 and kti == 0
                                                       and qc == 0) else None),
                                      inc=s_act)
                        self.free[(pn,)] = ev_x
                        if st_i == 0:
                            po = self.ps_big[0:dv + 1, qc * 512: qc * 512 + 512]
                            okeys = [("ps_big",)]
                        else:
                            po = (self.ps_c if qc == 0 else self.ps_d)[0:dv + 1, 0:512]
                            okeys = [("ps_c",), ("ps_d",)]
                        okw = [self.free.get(k) for k in okeys] if kti == 0 else []
                        ev_av = p.emit("tensor", lambda t, l=vt, r=ab, o=po,
                                       st=(kti == 0), sp=(kti == n_kti - 1):
                                       t.matmul(o, l, r, start=st, stop=sp),
                                       waits=_later(ev_x, *okw), inc=s_pe)
                        self.free[("attb4", slot)] = ev_av
                        ev_av_last[st_i] = ev_av
            for st_i in (0, 1):
                h = h_pair[st_i]
                if st_i == 0:
                    srcs = [self.ps_big[0:dv + 1, 0:512],
                            self.ps_big[0:dv + 1, 512:1024]]
                    bcs = [self.ps_big[0:dv, 0:512], self.ps_big[0:dv, 512:1024]]
                else:
                    srcs = [self.ps_c[0:dv + 1, 0:512], self.ps_d[0:dv + 1, 0:512]]
                    bcs = [self.ps_c[0:dv, 0:512], self.ps_d[0:dv, 0:512]]
                ev_ev = None
                for qc in range(2):
                    ev_ev = p.emit("vector", lambda v, i=srcs[qc],
                                   o=self.osb[0:dv + 1, qc * 512:qc * 512 + 512]:
                                   v.tensor_copy(o, i),
                                   waits=_later(ev_av_last[st_i],
                                                self.free.get(("osb",))
                                                if qc == 0 else None),
                                   inc=s_dve)
                ev_r = p.emit("vector", lambda v, r=dv:
                              v.reciprocal(self.recip[0:1, 0:1024],
                                           self.osb[r:r + 1, 0:1024]),
                              waits=[ev_ev], inc=s_dve)
                ev_bc = None
                for qc in range(2):
                    ev_bc = p.emit("tensor", lambda t, o=bcs[qc], q=qc, r=dv:
                                   t.matmul(o, self.onesF[0:1, 0:r],
                                            self.recip[0:1, q * 512:q * 512 + 512],
                                            start=True, stop=True),
                                   waits=[ev_r], inc=s_pe)
                ev_m = None
                for qc in range(2):
                    if branch == 1:
                        dst = self.oT[(h % 2) * 64:(h % 2) * 64 + 64,
                                      (h // 2) * 1024 + qc * 512:
                                      (h // 2) * 1024 + qc * 512 + 512]
                    else:
                        dst = self.aoT[(h % 2) * 32:(h % 2) * 32 + 32,
                                       (h // 2) * 1024 + qc * 512:
                                       (h // 2) * 1024 + qc * 512 + 512]
                    ev_m = p.emit("vector", lambda v, d=dst,
                                  a=self.osb[0:dv, qc * 512:qc * 512 + 512],
                                  b=bcs[qc]:
                                  v.tensor_mul(d, a, b),
                                  waits=_later(ev_bc, oa_free
                                               if (hp == 0 and qc == 0) else None),
                                  inc=s_dve)
                self.free[("osb",)] = ev_m
                if st_i == 0:
                    self.free[("ps_big",)] = ev_m
                else:
                    self.free[("ps_c",)] = ev_m
                    self.free[("ps_d",)] = ev_m
        if branch == 1:
            self.ev_oT = (s_dve, s_dve.n)
        else:
            self.ev_aoT = (s_dve, s_dve.n)

    def out_proj(self, s, qh):
        p = self.p
        s_act = self.sems["s_act"]
        scr_rd = _later(self.free.get(("outscr_rd0",)), self.free.get(("outscr_rd1",)))
        for dc in range(8):
            half = dc % 2
            sd_sp = self.sems["sd_sp0"] if half == 0 else self.sems["sd_sp1"]
            ev_e = None
            for qc in range(2):
                pn = "ps_a" if qc == 0 else "ps_b"
                ps = self.ps_a if qc == 0 else self.ps_b
                pairs = [(self.wtB[:, ci * 1024 + dc * 128: ci * 1024 + dc * 128 + 128],
                          self.oT[:, ci * 1024 + qc * 512: ci * 1024 + qc * 512 + 512])
                         for ci in range(8)]
                ev = self.acc_group(ps, pn, pairs, [self.ev_wtB, self.ev_oT])
                ev_e = p.emit("scalar", lambda sc,
                              o=self.stg[:, half * 1024 + qc * 512:
                                         half * 1024 + qc * 512 + 512],
                              i=ps, c=dc:
                              sc.activation(o, i, AF.Identity,
                                            bias=self.outbcol[:, c:c + 1]),
                              waits=_later(ev, self.free.get(("stg", half))
                                           if qc == 0 else None),
                              inc=s_act)
                self.free[(pn,)] = ev_e
            ev_sp = p.emit("sync", lambda sy, d=dc, q=qh:
                           sy.dma_start(self.outscr[d * 128:(d + 1) * 128,
                                                    q * 1024:(q + 1) * 1024],
                                        self.stg[:, (d % 2) * 1024:(d % 2) * 1024 + 1024]),
                           waits=_later(ev_e, *(scr_rd if dc < 2 else [])),
                           inc=sd_sp, inc_n=16)
            self.free[("stg", half)] = ev_sp

    def mla_proj(self, s, qh):
        p = self.p
        s_act = self.sems["s_act"]
        slot = 0
        for dc in range(8):
            for qc in range(2):
                ps = self.ps_a if slot % 2 == 0 else self.ps_b
                pn = "ps_a" if slot % 2 == 0 else "ps_b"
                pairs = [(self.foB[0:64, kc * 1024 + dc * 128: kc * 1024 + dc * 128 + 128],
                          self.aoT[0:64, kc * 1024 + qc * 512: kc * 1024 + qc * 512 + 512])
                         for kc in range(8)]
                ev = self.acc_group(ps, pn, pairs, [self.ev_foB, self.ev_aoT])
                ev_e = p.emit("scalar", lambda sc,
                              o=self.mlaT[:, dc * S + qh * 1024 + qc * 512:
                                          dc * S + qh * 1024 + qc * 512 + 512],
                              i=ps, c=dc:
                              sc.activation(o, i, AF.Identity,
                                            bias=self.obcol[:, c:c + 1]),
                              waits=[ev], inc=s_act)
                self.free[(pn,)] = ev_e
                slot += 1
        self.ev_mla = (s_act, s_act.n)

    def router_phase(self, s, qh):
        p = self.p
        s_pe, s_act, s_dve = (self.sems[n] for n in ("s_pe", "s_act", "s_dve"))
        ev = None
        for qc in range(2):
            pairs = [(self.routw[:, ci * 8: ci * 8 + 8],
                      self.mlaT[:, ci * S + qh * 1024 + qc * 512:
                                ci * S + qh * 1024 + qc * 512 + 512])
                     for ci in range(8)]
            ev = self.acc_group(self.ps_big, "ps_big" if qc == 0 else "__skip",
                                pairs,
                                [self.ev_mla, self.ev_misc_ld] if qc == 0 else [],
                                rows=8, cols=512, col0=qc * 512)
        ev_l = p.emit("scalar", lambda sc:
                      sc.activation(self.probsL[0:8, 0:1024],
                                    self.ps_big[0:8, 0:1024],
                                    AF.Identity, bias=self.routb[0:8, 0:1]),
                      waits=[ev], inc=s_act)
        self.free[("ps_big",)] = ev_l
        for ti in range(8):
            po = (ti % 2) * 512
            ev_tr = p.emit("tensor", lambda t, o=po,
                           i=self.probsL[0:8, ti * 128:(ti + 1) * 128]:
                           t.transpose(self.ps_big[0:128, o:o + 8], i,
                                       self.identF[0:8, 0:8]),
                           waits=_later(ev_l, self.free.get(("rps", ti % 2)),
                                        self.free.get(("ps_big",)) if ti < 2 else None),
                           inc=s_pe)
            pl = self.ps_big[0:128, po:po + 8]
            ev1 = p.emit("vector", lambda v, i=pl: v.reduce_max(
                self.t0, i, axis=AX, negate=True), waits=[ev_tr], inc=s_dve)
            ev2 = p.emit("scalar", lambda sc, i=pl: sc.activation(
                self.combF, i, AF.Exp, bias=self.t0), waits=[ev1], inc=s_act)
            self.free[("rps", ti % 2)] = ev2
            ev = ev2
            for fn in [
                    lambda v: v.reduce_sum(self.t1, self.combF, axis=AX),
                    lambda v: v.reciprocal(self.t1, self.t1),
                    lambda v: v.tensor_scalar_mul(self.combF, self.combF,
                                                  self.t1[:, 0:1]),
                    lambda v: v.reduce_max(self.t0, self.combF, axis=AX),
                    lambda v: v.tensor_single_scalar(self.eq8, self.combF,
                                                     self.t0[:, 0:1], ALU.is_equal),
                    lambda v: v.scalar_tensor_tensor(self.eq8, self.eq8, -1e30,
                                                     self.combF, op0=ALU.mult,
                                                     op1=ALU.add),
                    lambda v: v.reduce_max(self.t2, self.eq8, axis=AX),
                    lambda v: v.tensor_single_scalar(self.eq8, self.combF,
                                                     self.t2[:, 0:1], ALU.is_ge),
                    lambda v: v.tensor_add(self.t3, self.t0, self.t2),
                    lambda v: v.tensor_scalar_add(self.t3, self.t3, 1e-8),
                    lambda v: v.reciprocal(self.t3, self.t3),
                    lambda v: v.tensor_scalar_mul(self.t3, self.t3, 0.3),
                    lambda v: v.tensor_mul(self.combF, self.combF, self.eq8),
                    lambda v: v.tensor_scalar_mul(self.combF, self.combF,
                                                  self.t3[:, 0:1])]:
                ev = p.emit("vector", fn, waits=[ev], inc=s_dve)
            ev4 = p.emit("vector", lambda v: v.tensor_copy(self.combB, self.combF),
                         waits=_later(ev, self.free.get(("combB",))), inc=s_dve)
            ev_t2 = p.emit("tensor", lambda t, o=(ti % 2) * 1024:
                           t.transpose(self.ps_tr[0:8, o:o + 128], self.combB,
                                       self.identB),
                           waits=_later(ev4, self.free.get(("ctr", ti % 2)),
                                        self.free.get(("ps_tr", ti % 2))
                                        if ti < 2 else None),
                           inc=s_pe)
            self.free[("combB",)] = ev_t2
            ev5 = p.emit("vector", lambda v, o=(ti % 2) * 1024, t=ti:
                         v.tensor_copy(self.combT[0:8, qh * 1024 + t * 128:
                                                  qh * 1024 + (t + 1) * 128],
                                       self.ps_tr[0:8, o:o + 128]),
                         waits=[ev_t2], inc=s_dve)
            self.free[("ctr", ti % 2)] = ev5
            self.free[("ps_tr", ti % 2)] = ev5
        self.ev_comb = (s_dve, s_dve.n)
        self.free[("ps_big",)] = (s_act, s_act.n)

    def moe_phase(self, s):
        p = self.p
        s_pe, s_act, s_dve, s_pool = (self.sems[n] for n in
                                      ("s_pe", "s_act", "s_dve", "s_pool"))
        sd_w1, sd_w2 = self.sems["sd_w1"], self.sems["sd_w2"]
        ev_sp = [(self.sems["sd_sp0"], self.sems["sd_sp0"].n),
                 (self.sems["sd_sp1"], self.sems["sd_sp1"].n)]
        ev_fin_prev = _later(self.free.get(("outscr_rd0",)),
                             self.free.get(("outscr_rd1",)))
        ev_b2_done = self.pe_now()

        e_list = [int(t) for t in os.environ.get("KDBG_ELIST", "").split(",")] \
            if os.environ.get("KDBG_ELIST") else list(range(E))
        qc_list = [int(t) for t in os.environ.get("KDBG_QCLIST", "").split(",")] \
            if os.environ.get("KDBG_QCLIST") else list(range(4))
        for ei, e in enumerate(e_list):
            ev_w1 = self.ev_w1[e]
            ev_w2 = self.ev_w2[e]
            ev_bc_e = None
            for half in range(2):
                evb = None
                for qc in range(2):
                    evb = p.emit("tensor", lambda t, hh=half, q=qc, ee=e:
                                 t.matmul(self.ps_big[0:128, q * 512:q * 512 + 512],
                                          self.selc[0:8, ee * 128:(ee + 1) * 128],
                                          self.combT[0:8, hh * 1024 + q * 512:
                                                     hh * 1024 + q * 512 + 512],
                                          start=True, stop=True),
                                 waits=_later(self.ev_comb,
                                              self.free.get(("ps_big",))
                                              if qc == 0 else None,
                                              self.ev_small if (e == 0 and half == 0
                                                                and qc == 0) else None),
                                 inc=s_pe)
                ev_bc_e = p.emit("vector", lambda v, hh=half:
                                 v.tensor_copy(self.bcast[:, hh * 1024:(hh + 1) * 1024],
                                               self.ps_big[0:128, 0:1024]),
                                 waits=_later(evb, ev_b2_done
                                              if (e == 0 and half == 0) else None),
                                 inc=s_dve)
                self.free[("ps_big",)] = ev_bc_e
            for qc in qc_list:
                ev_h2_prev = self.free.get(("h1g",))
                ev_scale = {}
                for fi in range(16):
                    sl = (fi + int(os.environ.get("KDBG_SLSWAP", "0"))) % 2
                    ps = self.ps_a if sl == 0 else self.ps_b
                    pn = "ps_a" if sl == 0 else "ps_b"
                    pairs = [(self.w1s[:, ci * 2048 + fi * 128: ci * 2048 + fi * 128 + 128],
                              self.mlaT[:, ci * S + qc * 512: ci * S + qc * 512 + 512])
                             for ci in range(8)]
                    ev = self.acc_group(ps, pn, pairs,
                                        [ev_w1, self.ev_mla]
                                        if (fi == 0 and qc == 0) else [])
                    if os.environ.get("KDBG_H1PS") and fi == 0:
                        p.emit("vector", lambda v, i=ps:
                               v.tensor_copy(self.out_st[:, 0:512], i),
                               waits=[ev], inc=s_dve)
                        self._h1ps_done = True
                        return
                    hslice = self.h1g[:, fi * 512:(fi + 1) * 512]
                    gb = self.attb[:, (fi % 2) * 512:(fi % 2) * 512 + 512]
                    ev_g = p.emit("scalar", lambda sc, o=gb, i=ps, f=fi, ee=e:
                                  sc.activation(o, i, AF.Gelu,
                                                bias=self.b1col[:, f * 8 + ee:
                                                                f * 8 + ee + 1]),
                                  waits=_later(ev, ev_h2_prev,
                                               self.free.get(("gb", fi % 2))),
                                  inc=s_act)
                    self.free[(pn,)] = ev_g
                    ev_scale[fi] = p.emit("vector", lambda v, o=hslice, i=gb, q=qc:
                                          v.tensor_mul(o, i,
                                                       self.bcast[:, q * 512:
                                                                  q * 512 + 512]),
                                          waits=_later(ev_g, ev_bc_e), inc=s_dve)
                    self.free[("gb", fi % 2)] = ev_scale[fi]
                ev_h2_last = None
                for dc in range(8):
                    sl = dc % 2
                    psh = self.ps_c if sl == 0 else self.ps_d
                    pnh = "ps_c" if sl == 0 else "ps_d"
                    ev = None
                    for fi in range(16):
                        w = []
                        if fi == 0:
                            w = _later(self.free.get((pnh,)),
                                       ev_w2 if (dc == 0 and qc == qc_list[0])
                                       else None)
                        if dc == 0:
                            w = _later(*(w + [ev_scale[fi]]))
                        ev = p.emit("tensor", lambda t,
                                    l=self.w2s[:, fi * 1024 + dc * 128:
                                               fi * 1024 + dc * 128 + 128],
                                    r=self.h1g[:, fi * 512:(fi + 1) * 512],
                                    st=(fi == 0),
                                    sp=(fi == 15 and (ei != 0 or
                                        bool(os.environ.get("KDBG_NOB")))), o=psh:
                                    t.matmul(o, l, r, start=st, stop=sp),
                                    waits=w,
                                    inc=(s_pe if (fi == 15 and (ei != 0 or
                                         bool(os.environ.get("KDBG_NOB"))))
                                         else None))
                    if ei == 0 and not os.environ.get("KDBG_NOB"):
                        ev = p.emit("tensor", lambda t, o=psh, d=dc, q=qc:
                                    t.matmul(o, self.b2s[0:8, d * 128:(d + 1) * 128],
                                             self.combT[0:8, q * 512:(q + 1) * 512],
                                             start=False, stop=True),
                                    inc=s_pe)
                    elif ei == 0:
                        pass
                    ev_h2_last = ev
                    half = (dc // 2) % 2
                    sd_ac = self.sems["sd_ac0"] if half == 0 else self.sems["sd_ac1"]
                    ev_e = p.emit("vector", lambda v,
                                  o=self.stg[:, half * 1024 + (dc % 2) * 512:
                                             half * 1024 + (dc % 2) * 512 + 512],
                                  i=psh[0:128, 0:512]:
                                  v.tensor_copy(o, i),
                                  waits=_later(ev,
                                               self.free.get(("stg2", half))
                                               if dc % 2 == 0 else None,
                                               self.free.get(("stg", half))
                                               if (e == 0 and qc == 0 and dc < 2)
                                               else None),
                                  inc=s_dve)
                    self.free[(pnh,)] = ev_e
                    if dc % 2 == 1:
                        dst = self.outscr.rearrange("(dc p) t -> p dc t", p=128)[
                            :, dc - 1:dc + 1, qc * 512:qc * 512 + 512]
                        src = self.stg[:, half * 1024: half * 1024 + 1024].rearrange(
                            "p (a t) -> p a t", t=512)
                        if ei == 0:
                            w = _later(ev_e, *ev_sp, ev_fin_prev[0]
                                       if ev_fin_prev else None,
                                       ev_fin_prev[1] if ev_fin_prev and
                                       len(ev_fin_prev) > 1 else None)
                        else:
                            w = [ev_e]
                        ev_ac = p.emit("gpsimd", lambda g, d=dst, i=src:
                                       g.dma_start(d, i, accum_op=ALU.add),
                                       waits=w, inc=sd_ac, inc_n=16)
                        self.free[("stg2", half)] = ev_ac
                self.free[("h1g",)] = ev_h2_last
            if ei + 1 < len(e_list):
                nxt = e_list[ei + 1]
                self.ev_w1[nxt] = p.emit(
                    "sync", lambda sy, ee=nxt:
                    sy.dma_start(self.w1s.rearrange("p (ci f) -> p ci f", f=2048),
                                 self.w1_b[ee].rearrange(
                        "(ci p) f -> p ci f", p=128)),
                    waits=_later(self.pe_now(), self.ev_cast[f"w1_{nxt}"]),
                    inc=sd_w1, inc_n=16)
                self.ev_w2[nxt] = p.emit(
                    "sync", lambda sy, ee=nxt:
                    sy.dma_start(self.w2s.rearrange("p (fi d) -> p fi d", d=1024),
                                 self.w2_b[ee].rearrange(
                        "(fi p) d -> p fi d", p=128)),
                    waits=_later(self.pe_now(), self.ev_cast[f"w2_{nxt}"]),
                    inc=sd_w2, inc_n=16)
        self.ev_acc = [(self.sems["sd_ac0"], self.sems["sd_ac0"].n),
                       (self.sems["sd_ac1"], self.sems["sd_ac1"].n)]
        self.free[("stg", 0)] = self.free.get(("stg2", 0))
        self.free[("stg", 1)] = self.free.get(("stg2", 1))
        pe_end = self.pe_now()
        self.free[("wtb",)] = pe_end
        self.free[("C",)] = pe_end
        self.free[("qbuf",)] = pe_end
        self.free[("xnT_w",)] = pe_end
        self.free[("vtok",)] = pe_end
        self.free[("vtok2",)] = (s_pool, s_pool.n)

    def final_phase(self, s):
        p = self.p
        s_pe, s_dve = self.sems["s_pe"], self.sems["s_dve"]
        for ti in range(16):
            b = ti % 2
            sd_f = self.sems["sd_f0"] if b == 0 else self.sems["sd_f1"]
            sd_x = self.sems["sd_x0"] if b == 0 else self.sems["sd_x1"]
            fin = self.fin_in[:, b * 1024:(b + 1) * 1024]
            src = self.outscr.rearrange("(dc p) (ti t) -> p ti dc t",
                                        p=128, t=128)[:, ti]
            ev_f = p.emit("sync", lambda sy, o=fin, i=src:
                          sy.dma_start(o, i),
                          waits=_later(*self.ev_acc, self.free.get(("fin", b))),
                          inc=sd_f, inc_n=16)
            xb = self.x_any[:, b * 1024:(b + 1) * 1024]
            ev_x = p.emit("sync", lambda sy, o=xb, ss=s, t=ti: sy.dma_start(
                o, self.x_in[ss, t * 128:(t + 1) * 128, :]),
                waits=_later(self.free.get(("x_any", b))), inc=sd_x, inc_n=16)
            evt = None
            for dc in range(8):
                if b == 0:
                    po = self.ps_big[:, dc * 128:(dc + 1) * 128]
                    pkeys = [("ps_big",)]
                else:
                    pt = self.ps_c if dc < 4 else self.ps_d
                    po = pt[:, (dc % 4) * 128:(dc % 4) * 128 + 128]
                    pkeys = [("ps_c",), ("ps_d",)]
                w = _later(ev_f,
                           *([self.free.get(k) for k in pkeys] if dc == 0 else []))
                evt = p.emit("tensor", lambda t, o=po,
                             i=fin[:, dc * 128:(dc + 1) * 128]:
                             t.transpose(o, i, self.identF),
                             waits=w, inc=(s_pe if dc == 7 else None))
            self.free[("fin", b)] = evt
            ev_o = None
            for hc in range(2):
                xp = self.out_st[:, b * 1024 + hc * 512: b * 1024 + hc * 512 + 512]
                ev1 = p.emit("vector", lambda v, o=xp,
                             i=xb[:, hc * 512:hc * 512 + 512], t=ti:
                             v.tensor_scalar(o, i, self.rstd1[:, t:t + 1],
                                             self.nmr[:, t:t + 1],
                                             op0=ALU.mult, op1=ALU.add),
                             waits=_later(ev_x, self.free.get(("out_st", b))
                                          if hc == 0 else None),
                             inc=s_dve)
                if b == 0:
                    pa = self.ps_big[:, hc * 512: hc * 512 + 512]
                else:
                    pa = (self.ps_c if hc == 0 else self.ps_d)[:, 0:512]
                ev_o = p.emit("vector", lambda v, o=xp, i=pa:
                              v.tensor_add(o, o, i),
                              waits=_later(evt, ev1), inc=s_dve)
            if b == 0:
                self.free[("ps_big",)] = ev_o
            else:
                self.free[("ps_c",)] = ev_o
                self.free[("ps_d",)] = ev_o
            self.free[("x_any", b)] = ev_o
            sd_st = self.sems["sd_st0"] if b == 0 else self.sems["sd_st1"]
            ev_st = p.emit("sync", lambda sy, ss=s, t=ti,
                           o=self.out_st[:, (ti % 2) * 1024:(ti % 2) * 1024 + 1024]:
                           sy.dma_start(self.y_out[ss, t * 128:(t + 1) * 128, :], o),
                           waits=[ev_o], inc=sd_st, inc_n=16)
            self.free[("out_st", b)] = ev_st
        self.free[("outscr_rd0",)] = (self.sems["sd_f0"], self.sems["sd_f0"].n)
        self.free[("outscr_rd1",)] = (self.sems["sd_f1"], self.sems["sd_f1"].n)
        self.free[("stg", 0)] = self.pe_now()
        self.free[("stg", 1)] = self.pe_now()

    def _dbg_dump(self, dumps):
        p = self.p
        sd_st = self.sems["sd_st0"]
        waits = []
        for nm in ("s_pe", "s_act", "s_dve", "s_pool"):
            waits.append((self.sems[nm], self.sems[nm].n))
        for nm in ("sd_wa", "sd_wb", "sd_w1", "sd_w2", "sd_x0", "sd_x1",
                   "sd_f0", "sd_f1", "sd_sp0", "sd_sp1", "sd_st0", "sd_st1",
                   "sd_ac0", "sd_ac1", "sd_g", "sd_misc"):
            waits.append((self.sems[nm], self.sems[nm].n))
        first = True
        for nm, ap, shape, dt in dumps:
            out = self.nc.dram_tensor("dbg_" + nm, shape, dt,
                                      kind="ExternalOutput").ap()
            p.emit("sync", lambda sy, o=out, i=ap: sy.dma_start(o, i),
                   waits=waits if first else (), inc=sd_st, inc_n=16)
            first = False
        p.emit("sync", None, waits=[(sd_st, sd_st.n)])

    def build_debug(self, stage):
        self.setup()
        p = self.p
        self.phase0()
        self.ev_w1 = {}
        self.ev_w2 = {}
        s = 0
        dumps = []
        while True:
            if stage == "bias":
                dumps = [("biasc", self.biasc, [128, 224], F32),
                         ("braw", self.braw, [8, 2048], F32)]
                break
            self.ev_wtA = self.load_wt(0, "aq",
                                       extra_waits=[self.free.get(("wtb",))])
            self.ev_wtB = self.load_wt(1, "ak",
                                       extra_waits=[self.free.get(("wtb",))])
            self.ln_phase(s)
            if stage == "ln":
                dumps = [("xnT", self.xnT, [128, 8 * S], BF16),
                         ("negm", self.negm, [128, 16], F32),
                         ("rstd", self.rstd, [128, 16], F32)]
                break
            self.conv_phase(s)
            self.kv1_phase(s)
            if stage == "kv1":
                dumps = [("xdT", self.xdT, [128, 8192], BF16),
                         ("kT", self.kT, [128, 8192], BF16),
                         ("v1tok", self.v1tok, [128, 8320], BF16)]
                break
            self.ev_wtB = self.load_wt(1, "ao")
            self.q_proj(s, 0, 1)
            if stage == "q1":
                dumps = [("qT", self.qT, [128, 8192], BF16)]
                break
            self.attn_phase(s, 0, 1)
            if stage == "att1":
                dumps = [("oT", self.oT, [128, 8192], BF16),
                         ("osb", self.osb, [128, 1024], F32)]
                break
            self.free[("qbuf",)] = self.pe_now()
            self.out_proj(s, 0)
            self.free[("oa",)] = self.pe_now()
            if stage == "outp":
                dumps = [("oscr", self.outscr, [D, S], F32)]
                break
            self.q_proj(s, 1, 1)
            self.attn_phase(s, 1, 1)
            self.free[("qbuf",)] = self.pe_now()
            self.out_proj(s, 1)
            self.free[("oa",)] = self.pe_now()
            self.ev_wtA = self.load_wt(0, "fq", extra_waits=[self.pe_now()])
            self.ev_wtB = self.load_wt(1, "fk", extra_waits=[self.pe_now()])
            self.kv2_phase(s)
            if stage == "kv2":
                dumps = [("k2T", self.k2T, [64, 16384], BF16),
                         ("v2tok", self.v2tok, [128, 8448], BF16)]
                break
            self.ev_foB = self.load_wt(1, "fo")
            self.q_proj(s, 0, 2)
            if stage == "q2":
                dumps = [("q2c", self.q2c, [64, 8192], BF16)]
                break
            self.attn_phase(s, 0, 2)
            if stage == "att2":
                dumps = [("aoT", self.aoT, [64, 8192], BF16)]
                break
            self.free[("qbuf",)] = self.pe_now()
            self.mla_proj(s, 0)
            self.free[("oa",)] = self.pe_now()
            if stage == "mla":
                dumps = [("mlaT", self.mlaT, [128, 8 * S], BF16)]
                break
            self.router_phase(s, 0)
            if stage == "rout":
                dumps = [("combT", self.combT, [8, S], BF16),
                         ("probsL", self.probsL, [8, 1024], F32),
                         ("combF", self.combF, [128, 8], F32),
                         ("eq8", self.eq8, [128, 8], F32),
                         ("stats", self.stats, [128, 80], F32),
                         ("combB", self.combB, [128, 8], BF16)]
                break
            self.q_proj(s, 1, 2)
            self.attn_phase(s, 1, 2)
            self.free[("qbuf",)] = self.pe_now()
            self.mla_proj(s, 1)
            self.free[("oa",)] = self.pe_now()
            self.router_phase(s, 1)
            e0 = int(os.environ.get("KDBG_ELIST", "0").split(",")[0])
            self.ev_w1[e0] = p.emit("sync", lambda sy, ee=e0: sy.dma_start(
                self.w1s.rearrange("p (ci f) -> p ci f", f=2048),
                self.w1_b[ee].rearrange("(ci p) f -> p ci f", p=128)),
                waits=_later(self.ev_cast[f"w1_{e0}"], self.pe_now()),
                inc=self.sems["sd_w1"], inc_n=16)
            self.ev_w2[e0] = p.emit("sync", lambda sy, ee=e0: sy.dma_start(
                self.w2s.rearrange("p (fi d) -> p fi d", d=1024),
                self.w2_b[ee].rearrange("(fi p) d -> p fi d", p=128)),
                waits=_later(self.ev_cast[f"w2_{e0}"], self.pe_now()),
                inc=self.sems["sd_w2"], inc_n=16)
            self.moe_phase(s)
            if stage == "moe" and os.environ.get("KDBG_H1PS"):
                dumps = [("h1ps", self.out_st, [128, 2048], F32),
                         ("mlaT", self.mlaT, [128, 8 * S], BF16),
                         ("w1s", self.w1s, [128, 16384], BF16)]
                break
            if stage == "moe":
                dumps = [("oscr", self.outscr, [D, S], F32),
                         ("combT", self.combT, [8, S], BF16),
                         ("mlaT", self.mlaT, [128, 8 * S], BF16),
                         ("h1g", self.h1g, [128, 8192], BF16),
                         ("bcast", self.bcast, [128, 2048], BF16),
                         ("w1s", self.w1s, [128, 16384], BF16),
                         ("w2s", self.w2s, [128, 16384], BF16),
                         ("b2s", self.b2s, [8, D], BF16),
                         ("attb", self.attb, [128, 2048], BF16)]
                break
            self.final_phase(s)
            dumps = [("y0", self.y_out[0], [S, D], F32)]
            break
        self._dbg_dump(dumps)
        with self.nc.Block() as block:
            block.tensor(lambda t: self.p.run("tensor", t))
            block.vector(lambda v: self.p.run("vector", v))
            block.scalar(lambda sc: self.p.run("scalar", sc))
            block.gpsimd(lambda g: self.p.run("gpsimd", g))
            block.sync(lambda sy: self.p.run("sync", sy))
        return self.nc

    def build(self):
        self.setup()
        p = self.p
        self.phase0()
        self.ev_w1 = {}
        self.ev_w2 = {}
        for s in range(NSEQ):
            self.ev_wtA = self.load_wt(0, "aq",
                                       extra_waits=[self.free.get(("wtb",))])
            self.ev_wtB = self.load_wt(1, "ak",
                                       extra_waits=[self.free.get(("wtb",))])
            self.ln_phase(s)
            self.conv_phase(s)
            self.kv1_phase(s)
            self.ev_wtB = self.load_wt(1, "ao")
            for qh in range(2):
                self.q_proj(s, qh, 1)
                self.attn_phase(s, qh, 1)
                self.free[("qbuf",)] = self.pe_now()
                self.out_proj(s, qh)
                self.free[("oa",)] = self.pe_now()
            self.ev_wtA = self.load_wt(0, "fq", extra_waits=[self.pe_now()])
            self.ev_wtB = self.load_wt(1, "fk", extra_waits=[self.pe_now()])
            self.kv2_phase(s)
            self.ev_foB = self.load_wt(1, "fo")
            for qh in range(2):
                self.q_proj(s, qh, 2)
                self.attn_phase(s, qh, 2)
                self.free[("qbuf",)] = self.pe_now()
                self.mla_proj(s, qh)
                self.free[("oa",)] = self.pe_now()
                self.router_phase(s, qh)
            e0 = int(os.environ.get("KDBG_ELIST", "0").split(",")[0])
            self.ev_w1[e0] = p.emit("sync", lambda sy, ee=e0: sy.dma_start(
                self.w1s.rearrange("p (ci f) -> p ci f", f=2048),
                self.w1_b[ee].rearrange("(ci p) f -> p ci f", p=128)),
                waits=_later(self.ev_cast[f"w1_{e0}"], self.pe_now()),
                inc=self.sems["sd_w1"], inc_n=16)
            self.ev_w2[e0] = p.emit("sync", lambda sy, ee=e0: sy.dma_start(
                self.w2s.rearrange("p (fi d) -> p fi d", d=1024),
                self.w2_b[ee].rearrange("(fi p) d -> p fi d", p=128)),
                waits=_later(self.ev_cast[f"w2_{e0}"], self.pe_now()),
                inc=self.sems["sd_w2"], inc_n=16)
            self.moe_phase(s)
            self.final_phase(s)
        p.emit("sync", None, waits=[(self.sems["sd_st0"], self.sems["sd_st0"].n),
                                    (self.sems["sd_st1"], self.sems["sd_st1"].n)])

        with self.nc.Block() as block:
            block.tensor(lambda t: self.p.run("tensor", t))
            block.vector(lambda v: self.p.run("vector", v))
            block.scalar(lambda sc: self.p.run("scalar", sc))
            block.gpsimd(lambda g: self.p.run("gpsimd", g))
            block.sync(lambda sy: self.p.run("sync", sy))
        return self.nc


_built = None


def _get_built():
    global _built
    if _built is None:
        _built = Build().build()
    return _built


def kernel(**inputs):
    nc = _get_built()
    x = np.ascontiguousarray(inputs["L0_attn_x"])
    shards = np.split(x, 8, axis=0)
    weights = {k: np.ascontiguousarray(v) for k, v in inputs.items()
               if k != "L0_attn_x"}
    in_maps = []
    for c in range(8):
        m = {"x": shards[c]}
        m.update(weights)
        in_maps.append(m)
    res = run_bass_kernel_spmd(nc, in_maps, core_ids=list(range(8)))
    return np.concatenate([r["y"] for r in res.results], axis=0)


# version 03
# speedup vs baseline: 1.0789x; 1.0726x over previous
import math
import os
import numpy as np
from contextlib import ExitStack

import concourse.bass as bass
import concourse.mybir as mybir
from concourse.bass_utils import run_bass_kernel_spmd

F32 = mybir.dt.float32
BF16 = mybir.dt.bfloat16
AF = mybir.ActivationFunctionType
ALU = mybir.AluOpType
AX = mybir.AxisListType.X

NSEQ = 4
S = 2048
D = 1024
H = 16
KPH = 32
LAT = 512
E = 8
DFF = 2048
SC1 = 1.0 / 8.0
SC2 = 1.0 / math.sqrt(KPH)

ENGINES = ("tensor", "vector", "scalar", "gpsimd", "sync")


class _Sem:
    __slots__ = ("h", "n", "name")

    def __init__(self, h, name):
        self.h = h
        self.n = 0
        self.name = name


class _Prog:
    def __init__(self):
        self.ops = {e: [] for e in ENGINES}

    def emit(self, eng, fn, waits=(), inc=None, inc_n=1):
        w = tuple((s, v) for (s, v) in waits if s is not None and v > 0)
        ev = None
        if inc is not None:
            inc.n += inc_n
            ev = (inc, inc.n)
        self.ops[eng].append((w, fn, inc, inc_n if inc else 0))
        return ev

    def run(self, eng_name, eng):
        seen = {}
        for w, fn, inc, n in self.ops[eng_name]:
            for (s, v) in w:
                if seen.get(s.name, 0) >= v:
                    continue
                seen[s.name] = v
                eng.wait_ge(s.h, v)
            ins = fn(eng) if fn is not None else None
            if inc is not None:
                ins.then_inc(inc.h, n)


def _later(*evs):
    best = {}
    for ev in evs:
        if ev is None:
            continue
        s, v = ev
        if s is None or v <= 0:
            continue
        if best.get(s.name, (None, 0))[1] < v:
            best[s.name] = (s, v)
    return list(best.values())


class Build:
    def __init__(self):
        self.nc = bass.Bass("TRN2", target_bir_lowering=False)
        self.p = _Prog()
        self.ctx = ExitStack()
        self.sems = {}
        self.free = {}

    def sem(self, name):
        if name not in self.sems:
            h = self.ctx.enter_context(self.nc.semaphore(name))
            self.sems[name] = _Sem(h, name)
        return self.sems[name]

    def pe_now(self):
        return (self.sems["s_pe"], self.sems["s_pe"].n)

    def setup(self):
        nc = self.nc
        self.x_in = nc.dram_tensor("x", [NSEQ, S, D], F32, kind="ExternalInput").ap()
        wn = lambda n, shp: nc.dram_tensor(n, shp, F32, kind="ExternalInput").ap()
        self.aqw = wn("L0_attn_q_w", [D, D]);   self.aqb = wn("L0_attn_q_b", [D])
        self.akw = wn("L0_attn_k_w", [D, D]);   self.akb = wn("L0_attn_k_b", [D])
        self.avw = wn("L0_attn_v_w", [D, D]);   self.avb = wn("L0_attn_v_b", [D])
        self.acw = wn("L0_attn_conv_w", [D, 1, 3]); self.acb = wn("L0_attn_conv_b", [D])
        self.aow = wn("L0_attn_out_w", [D, D]); self.aob = wn("L0_attn_out_b", [D])
        self.fqw = wn("L0_ffn_q_w", [D, D]);    self.fqb = wn("L0_ffn_q_b", [D])
        self.fkw = wn("L0_ffn_k_w", [D, LAT]);  self.fkb = wn("L0_ffn_k_b", [LAT])
        self.fvw = wn("L0_ffn_v_w", [D, LAT]);  self.fvb = wn("L0_ffn_v_b", [LAT])
        self.fow = wn("L0_ffn_o_w", [D, D]);    self.fob = wn("L0_ffn_o_b", [D])
        self.frw = wn("L0_ffn_router_w", [D, E]); self.frb = wn("L0_ffn_router_b", [E])
        self.w1 = wn("L0_ffn_w1", [E, D, DFF]); self.b1 = wn("L0_ffn_b1", [E, DFF])
        self.w2 = wn("L0_ffn_w2", [E, DFF, D]); self.b2 = wn("L0_ffn_b2", [E, D])
        self.y_out = nc.dram_tensor("y", [NSEQ, S, D], F32, kind="ExternalOutput").ap()

        ds = lambda n, shp, dt: nc.dram_tensor(n, shp, dt, kind="Internal").ap()
        self.aqw_b = ds("aqw_b", [D, D], BF16)
        self.akw_b = ds("akw_b", [D, D], BF16)
        self.avw_b = ds("avw_b", [D, D], BF16)
        self.aow_b = ds("aow_b", [D, D], BF16)
        self.fqw_b = ds("fqw_b", [D, D], BF16)
        self.fkw_b = ds("fkw_b", [D, LAT], BF16)
        self.fvw_b = ds("fvw_b", [D, LAT], BF16)
        self.fow_b = ds("fow_b", [LAT, D], BF16)
        self.w1_b = ds("w1_b", [E, D, DFF], BF16)
        self.w2_b = ds("w2_b", [E, DFF, D], BF16)
        self.outscr = ds("outscr", [D, S], F32)

        sel = np.zeros((E, E * 128), dtype=np.float32)
        for e in range(E):
            sel[e, e * 128:(e + 1) * 128] = 1.0
        import ml_dtypes
        self.selc_d = nc.inline_tensor(sel.astype(ml_dtypes.bfloat16), name="selcd").ap()

        arena = nc.alloc_sbuf_tensor("arena", [128, 206848], mybir.dt.uint8)
        base = nc.lookup_mloc(arena).addr
        self._cur = base
        self._limit = base + 206848

        def at(name, shape, dt, off=None):
            nbytes = shape[1] * (2 if dt == BF16 else 4)
            if off is None:
                off = self._cur
                self._cur = off + ((nbytes + 31) // 32) * 32
            assert off + nbytes <= self._limit, (name, off + nbytes - base)
            return nc.alloc_sbuf_tensor_at(name, shape, dt, offset=off).ap(), off

        self.xnT, off_A = at("xnT", [128, 8 * S], BF16)
        self.mlaT, _ = at("mlaT", [128, 8 * S], BF16, off=off_A)
        self.qT, off_B = at("qT", [128, 8192], BF16)
        self.q2c, _ = at("q2c", [64, 8192], BF16, off=off_B)
        self.h1g, _ = at("h1g", [128, 8192], BF16, off=off_B)
        self.xdT, off_C = at("xdT", [128, 8192], BF16)
        self.kT, _ = at("kT", [128, 8192], BF16)
        self.k2T, _ = at("k2T", [64, 16384], BF16, off=off_C)
        self.w2s, _ = at("w2s", [128, 16384], BF16, off=off_C)
        self.v1tok, off_D = at("v1tok", [128, 8320], BF16)
        self.v2tok, _ = at("v2tok", [128, 8448], BF16, off=off_D)
        self.bcast, _ = at("bcast", [128, 2048], BF16, off=off_D)
        self._cur = max(self._cur, off_D + 16896)
        self.oT, off_E = at("oT", [128, 8192], BF16)
        self.aoT, _ = at("aoT", [64, 8192], BF16, off=off_E)
        self.w1s, off_F = at("w1s", [128, 16384], BF16)
        self.wtA, _ = at("wtA", [128, 8192], BF16, off=off_F)
        self.wtB, _ = at("wtB", [128, 8192], BF16, off=off_F + 16384)
        self.foB, _ = at("foB", [64, 8192], BF16, off=off_F + 16384)
        self.attb, _ = at("attb", [128, 2048], BF16)
        self.identB, _ = at("identB", [128, 128], BF16)
        self.onesB, _ = at("onesB", [1, 128], BF16)
        self.selc, _ = at("selc", [8, E * 128], BF16)
        self.b2s, _ = at("b2s", [8, D], BF16)
        self.combT, _ = at("combT", [8, S], BF16)
        self.vb1, _ = at("vb1", [1, D], BF16)
        self.v2b, _ = at("v2b", [1, LAT], BF16)
        self.routw, _ = at("routw", [128, 64], BF16)
        self.combB, _ = at("combB", [128, 8], BF16)
        self.stg, off_I = at("stg", [128, 2048], F32)
        self.braw, _ = at("braw", [8, 2048], F32, off=off_I)
        self.osb, _ = at("osb", [128, 1024], F32)
        self.sums, off_K = at("sums", [1, 1024], F32)
        self.probsL, _ = at("probsL", [8, 1024], F32, off=off_K)
        self._cur = max(self._cur, off_K + 4096)
        self.recip, _ = at("recip", [1, 1024], F32)
        self.x_any, _ = at("x_any", [128, 2048], F32)
        self.fin_in, _ = at("fin_in", [128, 2048], F32, off=off_I)
        self.out_st, off_N = at("out_st", [128, 2048], F32)
        self.sq_scr, _ = at("sq_scr", [128, 1024], F32, off=off_N)
        self.identF, _ = at("identF", [128, 128], F32)
        self.onesF, _ = at("onesF", [1, 128], F32)
        self.stats, _ = at("stats", [128, 80], F32)
        self.biasc, _ = at("biasc", [128, 224], F32)
        self.routb, _ = at("routb", [8, 1], F32)
        self.combF, _ = at("combF", [128, 8], F32)
        self.eq8, _ = at("eq8", [128, 8], F32)
        self.constv, _ = at("constv", [128, 8], F32)
        assert self._cur <= self._limit, self._cur - base

        bc = self.biasc
        self.qbcol = bc[:, 0:8]
        self.kbcol = bc[:, 8:16]
        self.q2bcol = bc[:, 16:24]
        self.k2bcol = bc[:, 24:28]
        self.obcol = bc[:, 28:36]
        self.outbcol = bc[:, 36:44]
        self.convbcol = bc[:, 44:52]
        self.convw_j = [bc[:, 52 + 8 * j:60 + 8 * j] for j in range(3)]
        self.b1col = bc[:, 76:204]
        st = self.stats
        self.negm = st[:, 0:16]
        self.rstd = st[:, 16:32]
        self.rstd1 = st[:, 32:48]
        self.nmr = st[:, 48:64]
        self.varsum = st[:, 64:65]
        self.t0 = st[:, 65:66]
        self.t1 = st[:, 66:67]
        self.t2 = st[:, 67:68]
        self.t3 = st[:, 68:69]

        self.ps_s0 = self.ctx.enter_context(nc.psum_tensor("ps_s0", [128, 1024], F32)).ap()
        self.ps_s1 = self.ctx.enter_context(nc.psum_tensor("ps_s1", [128, 1024], F32)).ap()
        self.ps_big = self.ctx.enter_context(nc.psum_tensor("ps_big", [128, 1024], F32)).ap()
        self.ps_tr = self.ctx.enter_context(nc.psum_tensor("ps_tr", [128, 2048], BF16)).ap()
        self.ps_a = self.ps_s0[:, 0:512]
        self.ps_b = self.ps_s0[:, 512:1024]
        self.ps_c = self.ps_s1[:, 0:512]
        self.ps_d = self.ps_s1[:, 512:1024]

        for n in ("sd_wa", "sd_wb", "sd_w1", "sd_w2", "sd_x0", "sd_x1",
                  "sd_f0", "sd_f1", "sd_sp0", "sd_sp1", "sd_st0", "sd_st1",
                  "sd_ac0", "sd_ac1", "sd_g", "sd_misc",
                  "s_pe", "s_act", "s_dve", "s_pool"):
            self.sem(n)

    def acc_group(self, ps, pname, pairs, waits0, rows=128, cols=512, col0=0):
        p, s_pe = self.p, self.sems["s_pe"]
        n = len(pairs)
        ev = None
        for i, (lt, rt) in enumerate(pairs):
            w = []
            if i == 0:
                w = _later(self.free.get((pname,)), *waits0)
            ev = p.emit("tensor",
                        lambda t, l=lt, r=rt, st=(i == 0), sp=(i == n - 1),
                        o=ps[0:rows, col0:col0 + cols]:
                        t.matmul(o, l, r, start=st, stop=sp),
                        waits=w, inc=(s_pe if i == n - 1 else None))
        return ev

    def phase0(self):
        p = self.p
        sd_g, sd_misc = self.sems["sd_g"], self.sems["sd_misc"]
        s_pe, s_dve, s_pool = (self.sems[n] for n in ("s_pe", "s_dve", "s_pool"))

        casts = [("aq", self.aqw_b, self.aqw), ("ak", self.akw_b, self.akw),
                 ("av", self.avw_b, self.avw), ("ao", self.aow_b, self.aow),
                 ("fq", self.fqw_b, self.fqw), ("fk", self.fkw_b, self.fkw),
                 ("fv", self.fvw_b, self.fvw), ("fo", self.fow_b, self.fow[0:LAT, :])]
        for e in range(E):
            casts.append((f"w1_{e}", self.w1_b[e], self.w1[e]))
            casts.append((f"w2_{e}", self.w2_b[e], self.w2[e]))
        self.ev_cast = {}
        for nm, out_ap, in_ap in casts:
            self.ev_cast[nm] = p.emit(
                "gpsimd", lambda g, o=out_ap, i=in_ap: g.dma_start(o, i),
                inc=sd_g, inc_n=16)
        self.ev_cast_all = (sd_g, sd_g.n)

        def mkident(g, ident, size):
            g.memset(ident, 0.0)
            return g.affine_select(out=ident, in_=ident, compare_op=ALU.not_equal,
                                   fill=1.0, base=0, pattern=[[-1, size]],
                                   channel_multiplier=1)
        p.emit("gpsimd", lambda g: mkident(g, self.identB, 128))
        p.emit("gpsimd", lambda g: mkident(g, self.identF, 128))
        p.emit("gpsimd", lambda g: g.memset(self.onesB, 1.0))
        for i, val in enumerate([1e-5, 1.0 / D, SC1, SC2]):
            ap = self.constv[:, i:i + 1]
            p.emit("gpsimd", lambda g, o=ap, v=val: g.memset(o, v))
            self.nc.const_aps.register(F32, val, ap)
        p.emit("gpsimd", lambda g: g.memset(self.onesF, 1.0), inc=s_pool)
        self.ev_consts = (s_pool, s_pool.n)

        for out_ap, in_ap, slow in [
                (self.selc, self.selc_d, False),
                (self.b2s, self.b2, False),
                (self.vb1, self.avb.rearrange("(o d) -> o d", o=1), False),
                (self.v2b, self.fvb.rearrange("(o d) -> o d", o=1), False),
                (self.routw.rearrange("p (ci e) -> p ci e", e=8),
                 self.frw.rearrange("(ci p) e -> p ci e", p=128), True)]:
            p.emit("gpsimd", lambda g, o=out_ap, i=in_ap, sl=slow:
                   g.dma_start(o, i, allow_slow_non_contiguous=sl), inc=sd_g, inc_n=16)
        self.ev_small = (sd_g, sd_g.n)

        ev_copy2 = None
        ev_copy1 = None
        slot = [0]

        def tr_step(src_ap, rows, dest_ap, ev_ld):
            nonlocal ev_copy1, ev_copy2
            po = (slot[0] % 2) * 512
            w = _later(ev_ld, self.ev_consts, ev_copy2)
            ev_tr = p.emit("tensor", lambda t, s=src_ap, r=rows, o=po:
                           t.transpose(self.ps_big[0:128, o:o + r], s,
                                       self.identF[0:r, 0:r]),
                           waits=w, inc=s_pe)
            ev_c = p.emit("vector", lambda v, d=dest_ap, o=po, r=rows:
                          v.tensor_copy(d, self.ps_big[0:128, o:o + r]),
                          waits=[ev_tr], inc=s_dve)
            ev_copy2, ev_copy1 = ev_copy1, ev_c
            slot[0] += 1

        jobs = [(self.aqb, 8, self.qbcol), (self.akb, 8, self.kbcol),
                (self.fqb, 8, self.q2bcol), (self.fkb, 4, self.k2bcol),
                (self.fob, 8, self.obcol), (self.aob, 8, self.outbcol),
                (self.acb, 8, self.convbcol)]
        for vec, rows, dest in jobs:
            src = self.braw[0:rows, 0:128]
            ev_ld = p.emit("sync", lambda sy, o=src,
                           i=vec.rearrange("(a p) -> a p", p=128): sy.dma_start(o, i),
                           waits=[self.pe_now()], inc=sd_misc, inc_n=16)
            tr_step(src, rows, dest[:, 0:rows], ev_ld)
        b1raw = self.braw[0:8, 0:2048]
        ev_ld = p.emit("sync", lambda sy: sy.dma_start(b1raw, self.b1),
                       waits=[self.pe_now()], inc=sd_misc, inc_n=16)
        for fi in range(16):
            tr_step(b1raw[0:8, fi * 128:(fi + 1) * 128], 8,
                    self.b1col[:, fi * 8:fi * 8 + 8], ev_ld)
        cwraw = self.braw[0:8, 0:384]
        ev_ld = p.emit("sync", lambda sy: sy.dma_start(
            cwraw, self.acw.rearrange("(a p) one j -> a (p one j)", a=8)),
            waits=[self.pe_now()], inc=sd_misc, inc_n=16)
        for j in range(3):
            cw2 = self.braw[0:8, 1024 + j * 128: 1024 + (j + 1) * 128]
            ev_cp = p.emit("vector", lambda v, o=cw2, i=cwraw[0:8, j:384:3]:
                           v.tensor_copy(o, i), waits=[ev_ld], inc=s_dve)
            tr_step(cw2, 8, self.convw_j[j], ev_cp)
        p.emit("sync", lambda sy: sy.dma_start(
            self.routb, self.frb.rearrange("(e o) -> e o", o=1)),
            waits=[self.pe_now()], inc=sd_misc, inc_n=16)
        self.ev_bias_done = ev_copy1
        self.ev_misc_ld = (sd_misc, sd_misc.n)
        self.ev_p0_pe = self.pe_now()
        self.free[("ps_big",)] = ev_copy1
        self.free[("stg", 0)] = self.ev_p0_pe
        self.free[("stg", 1)] = self.ev_p0_pe

    def load_wt(self, half, kind, extra_waits=()):
        p = self.p
        sem = self.sems["sd_wa"] if half == 0 else self.sems["sd_wb"]
        dst_t = self.wtA if half == 0 else self.wtB
        waits = _later(self.free.get(("wt", half)), self.ev_cast_all, *extra_waits)
        ev = None
        if kind in ("aq", "ak", "av", "ao"):
            src = {"aq": self.aqw_b, "ak": self.akw_b,
                   "av": self.avw_b, "ao": self.aow_b}[kind]
            for ci in range(8):
                ev = p.emit("scalar",
                            lambda sc, d=dst_t[:, ci * 1024:(ci + 1) * 1024],
                            i=src[:, ci * 128:(ci + 1) * 128]:
                            sc.dma_start(d, i, transpose=True),
                            waits=waits if ci == 0 else (), inc=sem, inc_n=16)
        elif kind == "fq":
            ev = p.emit("sync", lambda sy,
                        d=dst_t[:, 0:8192].rearrange("p (ci o) -> p ci o", o=1024):
                        sy.dma_start(
                d, self.fqw_b.rearrange("(ci p) o -> p ci o", p=128)),
                waits=waits, inc=sem, inc_n=16)
        elif kind in ("fk", "fv"):
            src = self.fkw_b if kind == "fk" else self.fvw_b
            ev = p.emit("sync", lambda sy,
                        d=dst_t[:, 0:4096].rearrange("p (ci o) -> p ci o", o=512),
                        s=src: sy.dma_start(
                d, s.rearrange("(ci p) o -> p ci o", p=128)),
                waits=waits, inc=sem, inc_n=16)
        elif kind == "fo":
            ev = p.emit("sync", lambda sy,
                        d=self.foB.rearrange("p (kc o) -> p kc o", o=1024):
                        sy.dma_start(
                d, self.fow_b.rearrange("(kc p) o -> p kc o", p=64)),
                waits=waits, inc=sem, inc_n=16)
        return ev

    def ln_phase(self, s):
        p = self.p
        s_pe, s_act, s_dve = (self.sems[n] for n in ("s_pe", "s_act", "s_dve"))
        for ti in range(16):
            b = ti % 2
            sd_x = self.sems["sd_x0"] if b == 0 else self.sems["sd_x1"]
            xb = self.x_any[:, b * 1024:(b + 1) * 1024]
            ev_x = p.emit("sync", lambda sy, o=xb, ss=s, t=ti: sy.dma_start(
                o, self.x_in[ss, t * 128:(t + 1) * 128, :]),
                waits=_later(self.free.get(("x_any", b))), inc=sd_x, inc_n=16)
            nm = self.negm[:, ti:ti + 1]
            rs = self.rstd[:, ti:ti + 1]
            ev = p.emit("vector", lambda v, o=nm, i=xb:
                        v.reduce_sum(o, i, axis=AX, negate=True),
                        waits=[ev_x], inc=s_dve)
            ev = p.emit("vector", lambda v, o=nm:
                        v.tensor_scalar_mul(o, o, 1.0 / D), waits=[ev], inc=s_dve)
            ev = p.emit("scalar", lambda sc, i=xb, n=nm:
                        sc.activation(self.sq_scr, i, AF.Square, bias=n,
                                      accum_out=self.varsum),
                        waits=_later(ev, self.free.get(("out_st", 0))
                                     if ti == 0 else None,
                                     self.ev_consts if ti == 0 else None),
                        inc=s_act)
            ev = p.emit("scalar", lambda sc, r=rs:
                        sc.activation(r, self.varsum, AF.Sqrt, scale=1.0 / D,
                                      bias=1e-5),
                        waits=[ev], inc=s_act)
            ev = p.emit("vector", lambda v, r=rs: v.reciprocal(r, r),
                        waits=[ev], inc=s_dve)
            p.emit("vector", lambda v, t=ti: v.tensor_scalar_add(
                self.rstd1[:, t:t + 1], self.rstd[:, t:t + 1], 1.0),
                waits=[ev], inc=s_dve)
            p.emit("vector", lambda v, t=ti: v.tensor_mul(
                self.nmr[:, t:t + 1], self.negm[:, t:t + 1], self.rstd[:, t:t + 1]),
                inc=s_dve)
            xn_tok = self.attb[:, b * 1024:(b + 1) * 1024]
            wv = _later(ev, self.free.get(("attb", b)),
                        self.free.get(("attb_s", 0)) if (b == 0 and ti == 0) else None,
                        self.free.get(("attb_s", 1)) if (b == 0 and ti == 0) else None)
            ev = p.emit("vector", lambda v, o=xn_tok, i=xb, t=ti:
                        v.tensor_scalar(o, i, self.negm[:, t:t + 1],
                                        self.rstd[:, t:t + 1],
                                        op0=ALU.add, op1=ALU.mult),
                        waits=wv, inc=s_dve)
            self.free[("x_any", b)] = ev
            ho = b * 1024
            evt = None
            for k in range(8):
                wts = _later(ev, self.ev_p0_pe,
                             self.free.get(("ps_tr", b)) if k == 0 else None,
                             self.free.get(("xnT_w",)) if (ti == 0 and k == 0) else None)
                evt = p.emit("tensor", lambda t, kk=k, o=ho, i=xn_tok:
                             t.transpose(self.ps_tr[:, o + kk * 128:o + (kk + 1) * 128],
                                         i[:, kk * 128:(kk + 1) * 128], self.identB),
                             waits=wts, inc=(s_pe if k == 7 else None))
            self.free[("attb", b)] = evt
            dest = self.xnT.rearrange("p (c t) -> p c t", t=S)[:, :, ti * 128:(ti + 1) * 128]
            ev_e = p.emit("vector", lambda v, d=dest, o=ho:
                          v.tensor_copy(d, self.ps_tr[:, o:o + 1024]
                                        .rearrange("p (c t) -> p c t", t=128)),
                          waits=[evt], inc=s_dve)
            self.free[("ps_tr", b)] = ev_e
        self.ev_xnT = (s_dve, s_dve.n)

    def conv_phase(self, s):
        p = self.p
        s_pool = self.sems["s_pool"]
        s_dve = self.sems["s_dve"]
        tA = self.x_any[:, 0:1024]
        tB = self.x_any[:, 1024:2048]
        ev = None
        for ci in range(8):
            xn = self.xnT[:, ci * S:(ci + 1) * S]
            xd = self.xdT[:, ci * 1024:(ci + 1) * 1024]
            w0 = _later(self.ev_xnT, self.ev_bias_done,
                        self.free.get(("C",)) if ci == 0 else None,
                        self.free.get(("x_any", 0)) if ci == 0 else None,
                        self.free.get(("x_any", 1)) if ci == 0 else None)
            ev = p.emit("vector", lambda v, o=tB[:, 1:1024], i=xn[:, 1:2046:2], c=ci:
                        v.tensor_scalar_mul(o, i, self.convw_j[0][:, c:c + 1]),
                        waits=w0, inc=s_dve)
            for step in [
                    lambda v, o=tA[:, 1:1024], i=xn[:, 2:2047:2], c=ci:
                    v.scalar_tensor_tensor(o, i, self.convw_j[1][:, c:c + 1],
                                           tB[:, 1:1024], op0=ALU.mult, op1=ALU.add),
                    lambda v, o=tB[:, 1:1024], i=xn[:, 3:2048:2], c=ci:
                    v.scalar_tensor_tensor(o, i, self.convw_j[2][:, c:c + 1],
                                           tA[:, 1:1024], op0=ALU.mult, op1=ALU.add),
                    lambda v, o=tA[:, 0:1], i=xn[:, 0:1], c=ci:
                    v.tensor_scalar_mul(o, i, self.convw_j[1][:, c:c + 1]),
                    lambda v, o=tB[:, 0:1], i=xn[:, 1:2], c=ci:
                    v.scalar_tensor_tensor(o, i, self.convw_j[2][:, c:c + 1],
                                           tA[:, 0:1], op0=ALU.mult, op1=ALU.add),
                    lambda v, o=xd, c=ci:
                    v.tensor_scalar_add(o, tB[:, 0:1024], self.convbcol[:, c:c + 1])]:
                ev = p.emit("vector", step, waits=[ev], inc=s_dve)
        self.free[("x_any", 0)] = ev
        self.free[("x_any", 1)] = ev
        self.ev_xdT = ev

    def kv1_phase(self, s):
        p = self.p
        s_act, s_dve, s_pool = (self.sems[n] for n in ("s_act", "s_dve", "s_pool"))
        ev_k = self.ev_wtB
        ev_cfree = self.free.get(("C",))
        slot = 0
        for ii in range(8):
            for qc in range(2):
                ps = self.ps_a if slot % 2 == 0 else self.ps_b
                pn = "ps_a" if slot % 2 == 0 else "ps_b"
                pairs = [(self.wtB[:, ci * 1024 + ii * 128: ci * 1024 + ii * 128 + 128],
                          self.xdT[:, ci * 1024 + qc * 512: ci * 1024 + qc * 512 + 512])
                         for ci in range(8)]
                ev = self.acc_group(ps, pn, pairs, [ev_k, self.ev_xdT])
                ev_e = p.emit("scalar", lambda sc,
                              o=self.kT[:, ii * 1024 + qc * 512: ii * 1024 + qc * 512 + 512],
                              i=ps, c=ii:
                              sc.activation(o, i, AF.Identity,
                                            bias=self.kbcol[:, c:c + 1]),
                              waits=_later(ev, ev_cfree if slot == 0 else None),
                              inc=s_act)
                self.free[(pn,)] = ev_e
                slot += 1
        self.ev_kT = (s_act, s_act.n)
        self.free[("wt", 1)] = self.pe_now()

        ev_v = self.load_wt(1, "av")
        self.ev_wtB = ev_v
        ev_ones = p.emit("gpsimd", lambda g: g.memset(
            self.v1tok[:, 0:8320].rearrange("p (x d) -> p x d", d=65)[:, :, 64:65], 1.0),
            waits=_later(self.free.get(("vtok",)), self.free.get(("vtok2",))),
            inc=s_pool)
        for kti in range(8):
            for qc in range(2):
                ps = self.ps_a if slot % 2 == 0 else self.ps_b
                pn = "ps_a" if slot % 2 == 0 else "ps_b"
                pairs = [(self.xdT[:, ci * 1024 + kti * 128: ci * 1024 + kti * 128 + 128],
                          self.wtB[:, ci * 1024 + qc * 512: ci * 1024 + qc * 512 + 512])
                         for ci in range(8)]
                pairs.append((self.onesB, self.vb1[:, qc * 512:(qc + 1) * 512]))
                ev = self.acc_group(ps, pn, pairs, [ev_v])
                base = kti * 1040 + qc * 520
                dest = self.v1tok[:, base:base + 520].rearrange(
                    "p (h d) -> p h d", d=65)[:, :, 0:64]
                ev_e = p.emit("vector", lambda v, d=dest,
                              i=ps.rearrange("p (h d) -> p h d", d=64):
                              v.tensor_copy(d, i),
                              waits=_later(ev, ev_ones), inc=s_dve)
                self.free[(pn,)] = ev_e
                slot += 1
        self.ev_v1 = (s_dve, s_dve.n)
        self.free[("wt", 1)] = self.pe_now()

    def kv2_phase(self, s):
        p = self.p
        s_act, s_dve, s_pool = (self.sems[n] for n in ("s_act", "s_dve", "s_pool"))
        ev_k = self.ev_wtB
        ev_cfree = self.pe_now()
        slot = 0
        for hc in range(8):
            for qc in range(4):
                ps = self.ps_a if slot % 2 == 0 else self.ps_b
                pn = "ps_a" if slot % 2 == 0 else "ps_b"
                pairs = [(self.wtB[:, ci * 512 + hc * 64: ci * 512 + hc * 64 + 64],
                          self.xnT[:, ci * S + qc * 512: ci * S + qc * 512 + 512])
                         for ci in range(8)]
                ev = self.acc_group(ps, pn, pairs, [ev_k, self.ev_xnT], rows=64)
                ev_e = p.emit("scalar", lambda sc,
                              o=self.k2T[0:64, hc * 2048 + qc * 512:
                                         hc * 2048 + qc * 512 + 512],
                              i=ps[0:64, :], c=hc:
                              sc.activation(o, i, AF.Identity,
                                            bias=self.k2bcol[(c % 2) * 64:(c % 2) * 64 + 64,
                                                             c // 2:c // 2 + 1]),
                              waits=_later(ev, ev_cfree if slot == 0 else None),
                              inc=s_act)
                self.free[(pn,)] = ev_e
                slot += 1
        self.ev_k2 = (s_act, s_act.n)
        self.free[("wt", 1)] = self.pe_now()

        ev_v = self.load_wt(1, "fv")
        self.ev_wtB = ev_v
        ev_ones = p.emit("gpsimd", lambda g: g.memset(
            self.v2tok[:, 0:8448].rearrange("p (x d) -> p x d", d=33)[:, :, 32:33], 1.0),
            waits=_later(self.pe_now()), inc=s_pool)
        for kti in range(16):
            ps = self.ps_a if slot % 2 == 0 else self.ps_b
            pn = "ps_a" if slot % 2 == 0 else "ps_b"
            pairs = [(self.xnT[:, ci * S + kti * 128: ci * S + kti * 128 + 128],
                      self.wtB[:, ci * 512: ci * 512 + 512])
                     for ci in range(8)]
            pairs.append((self.onesB, self.v2b[:, 0:512]))
            ev = self.acc_group(ps, pn, pairs, [ev_v])
            base = kti * 528
            dest = self.v2tok[:, base:base + 528].rearrange(
                "p (h d) -> p h d", d=33)[:, :, 0:32]
            ev_e = p.emit("vector", lambda v, d=dest,
                          i=ps.rearrange("p (h d) -> p h d", d=32):
                          v.tensor_copy(d, i),
                          waits=_later(ev, ev_ones), inc=s_dve)
            self.free[(pn,)] = ev_e
            slot += 1
        self.ev_v2 = (s_dve, s_dve.n)
        self.free[("wt", 1)] = self.pe_now()

    def q_proj(self, s, qh, branch):
        p = self.p
        s_act = self.sems["s_act"]
        qfree = self.free.get(("qbuf",))
        slot = 0
        for ii in range(8):
            for qc in range(2):
                ps = self.ps_a if slot % 2 == 0 else self.ps_b
                pn = "ps_a" if slot % 2 == 0 else "ps_b"
                pairs = [(self.wtA[:, ci * 1024 + ii * 128: ci * 1024 + ii * 128 + 128],
                          self.xnT[:, ci * S + qh * 1024 + qc * 512:
                                   ci * S + qh * 1024 + qc * 512 + 512])
                         for ci in range(8)]
                ev = self.acc_group(ps, pn, pairs, [self.ev_wtA, self.ev_xnT])
                o0 = ii * 1024 + qc * 512
                if branch == 1:
                    ev_e = p.emit("scalar", lambda sc, o=self.qT[:, o0:o0 + 512],
                                  i=ps, c=ii:
                                  sc.activation(o, i, AF.Identity,
                                                bias=self.qbcol[:, c:c + 1]),
                                  waits=_later(ev, qfree if slot == 0 else None),
                                  inc=s_act)
                else:
                    p.emit("scalar", lambda sc, o=self.q2c[0:32, o0:o0 + 512],
                           i=ps[0:32, :], c=ii:
                           sc.activation(o, i, AF.Identity,
                                         bias=self.q2bcol[0:32, c:c + 1]),
                           waits=_later(ev, qfree if slot == 0 else None), inc=s_act)
                    ev_e = p.emit("scalar", lambda sc, o=self.q2c[32:64, o0:o0 + 512],
                                  i=ps[64:96, :], c=ii:
                                  sc.activation(o, i, AF.Identity,
                                                bias=self.q2bcol[64:96, c:c + 1]),
                                  inc=s_act)
                self.free[(pn,)] = ev_e
                slot += 1
        self.ev_q = (s_act, s_act.n)

    def attn_phase(self, s, qh, branch):
        p = self.p
        s_pe, s_act, s_dve = (self.sems[n] for n in ("s_pe", "s_act", "s_dve"))
        if branch == 1:
            n_kti, dv, sc = 8, 64, SC1
            kv_ready = _later(self.ev_kT, self.ev_v1, self.ev_q)
        else:
            n_kti, dv, sc = 16, 32, SC2
            kv_ready = _later(self.ev_k2, self.ev_v2, self.ev_q)
        oa_free = self.free.get(("oa",))
        first_w = _later(*(kv_ready + [self.free.get(("ps_a",)),
                                       self.free.get(("ps_b",)),
                                       self.free.get(("ps_c",)),
                                       self.free.get(("ps_d",))]))

        def slices(h, kti, qc):
            if branch == 1:
                lt = self.kT[(h % 2) * 64:(h % 2) * 64 + 64,
                             (h // 2) * 1024 + kti * 128:
                             (h // 2) * 1024 + kti * 128 + 128]
                rt = self.qT[(h % 2) * 64:(h % 2) * 64 + 64,
                             (h // 2) * 1024 + qc * 512:
                             (h // 2) * 1024 + qc * 512 + 512]
                vt = self.v1tok[:, kti * 1040 + h * 65: kti * 1040 + h * 65 + 65]
            else:
                lt = self.k2T[(h % 2) * 32:(h % 2) * 32 + 32,
                              (h // 2) * 2048 + kti * 128:
                              (h // 2) * 2048 + kti * 128 + 128]
                rt = self.q2c[(h % 2) * 32:(h % 2) * 32 + 32,
                              (h // 2) * 1024 + qc * 512:
                              (h // 2) * 1024 + qc * 512 + 512]
                vt = self.v2tok[:, kti * 528 + h * 33: kti * 528 + h * 33 + 33]
            return lt, rt, vt

        sstep = 0
        for h in range(H):
            ev_av_last = None
            for kp in range(n_kti // 2):
                for qc in range(2):
                    buf = sstep % 2
                    psS = self.ps_s0 if buf == 0 else self.ps_s1
                    pnS = ("psS", buf)
                    k0, k1 = 2 * kp, 2 * kp + 1
                    lt0, rt0, vt0 = slices(h, k0, qc)
                    lt1, rt1, vt1 = slices(h, k1, qc)
                    w = _later(self.free.get(pnS),
                               *( [first_w[i] for i in range(len(first_w))]
                                  if (h == 0 and kp == 0 and qc == 0) else []))
                    p.emit("tensor", lambda t, l=lt0, r=rt0, o=psS[:, 0:512]:
                           t.matmul(o, l, r, start=True, stop=True), waits=w)
                    ev_mm = p.emit("tensor", lambda t, l=lt1, r=rt1,
                                   o=psS[:, 512:1024]:
                                   t.matmul(o, l, r, start=True, stop=True),
                                   inc=s_pe)
                    ab = self.attb[:, buf * 1024:(buf + 1) * 1024]
                    ev_x = p.emit("scalar", lambda scl, o=ab, i=psS, f=sc:
                                  scl.activation(o, i, AF.Exp, scale=f),
                                  waits=_later(ev_mm,
                                               self.free.get(("attb4", buf)),
                                               self.free.get(("attb", buf))
                                               if (h == 0 and kp == 0 and qc == 0)
                                               else None),
                                  inc=s_act)
                    self.free[pnS] = ev_x
                    po = self.ps_big[0:dv + 1, qc * 512: qc * 512 + 512]
                    okw = [self.free.get(("ps_big",))] if (kp == 0 and qc == 0) else []
                    p.emit("tensor", lambda t, l=vt0, r=ab[:, 0:512], o=po,
                           st=(kp == 0):
                           t.matmul(o, l, r, start=st, stop=False),
                           waits=_later(ev_x, *okw))
                    ev_av = p.emit("tensor", lambda t, l=vt1, r=ab[:, 512:1024],
                                   o=po, sp=(kp == n_kti // 2 - 1):
                                   t.matmul(o, l, r, start=False, stop=sp),
                                   inc=s_pe)
                    self.free[("attb4", buf)] = ev_av
                    ev_av_last = ev_av
                    sstep += 1
            srcs = [self.ps_big[0:dv + 1, 0:512], self.ps_big[0:dv + 1, 512:1024]]
            bcs = [self.ps_big[0:dv, 0:512], self.ps_big[0:dv, 512:1024]]
            ev_ev = None
            for qc in range(2):
                ev_ev = p.emit("vector", lambda v, i=srcs[qc],
                               o=self.osb[0:dv + 1, qc * 512:qc * 512 + 512]:
                               v.tensor_copy(o, i),
                               waits=_later(ev_av_last, self.free.get(("osb",))
                                            if qc == 0 else None),
                               inc=s_dve)
            ev_r = p.emit("vector", lambda v, r=dv:
                          v.reciprocal(self.recip[0:1, 0:1024],
                                       self.osb[r:r + 1, 0:1024]),
                          waits=[ev_ev], inc=s_dve)
            ev_bc = None
            for qc in range(2):
                ev_bc = p.emit("tensor", lambda t, o=bcs[qc], q=qc, r=dv:
                               t.matmul(o, self.onesF[0:1, 0:r],
                                        self.recip[0:1, q * 512:q * 512 + 512],
                                        start=True, stop=True),
                               waits=[ev_r], inc=s_pe)
            ev_m = None
            for qc in range(2):
                if branch == 1:
                    dst = self.oT[(h % 2) * 64:(h % 2) * 64 + 64,
                                  (h // 2) * 1024 + qc * 512:
                                  (h // 2) * 1024 + qc * 512 + 512]
                else:
                    dst = self.aoT[(h % 2) * 32:(h % 2) * 32 + 32,
                                   (h // 2) * 1024 + qc * 512:
                                   (h // 2) * 1024 + qc * 512 + 512]
                ev_m = p.emit("vector", lambda v, d=dst,
                              a=self.osb[0:dv, qc * 512:qc * 512 + 512], b=bcs[qc]:
                              v.tensor_mul(d, a, b),
                              waits=_later(ev_bc, oa_free
                                           if (h == 0 and qc == 0) else None),
                              inc=s_dve)
            self.free[("osb",)] = ev_m
            self.free[("ps_big",)] = ev_m
        end_ev = (s_act, s_act.n)
        for k in (("ps_a",), ("ps_b",), ("ps_c",), ("ps_d",)):
            self.free[k] = _later(self.free.get(k), end_ev)[0] \
                if self.free.get(k) else end_ev
        self.free[("ps_a",)] = end_ev
        self.free[("ps_b",)] = end_ev
        self.free[("ps_c",)] = end_ev
        self.free[("ps_d",)] = end_ev
        if branch == 1:
            self.ev_oT = (s_dve, s_dve.n)
        else:
            self.ev_aoT = (s_dve, s_dve.n)

    def out_proj(self, s, qh):
        p = self.p
        s_act = self.sems["s_act"]
        scr_rd = _later(self.free.get(("outscr_rd0",)), self.free.get(("outscr_rd1",)))
        for dc in range(8):
            half = dc % 2
            sd_sp = self.sems["sd_sp0"] if half == 0 else self.sems["sd_sp1"]
            ev_e = None
            for qc in range(2):
                pn = "ps_a" if qc == 0 else "ps_b"
                ps = self.ps_a if qc == 0 else self.ps_b
                pairs = [(self.wtB[:, ci * 1024 + dc * 128: ci * 1024 + dc * 128 + 128],
                          self.oT[:, ci * 1024 + qc * 512: ci * 1024 + qc * 512 + 512])
                         for ci in range(8)]
                ev = self.acc_group(ps, pn, pairs, [self.ev_wtB, self.ev_oT])
                ev_e = p.emit("scalar", lambda sc,
                              o=self.stg[:, half * 1024 + qc * 512:
                                         half * 1024 + qc * 512 + 512],
                              i=ps, c=dc:
                              sc.activation(o, i, AF.Identity,
                                            bias=self.outbcol[:, c:c + 1]),
                              waits=_later(ev, self.free.get(("stg", half))
                                           if qc == 0 else None),
                              inc=s_act)
                self.free[(pn,)] = ev_e
            ev_sp = p.emit("sync", lambda sy, d=dc, q=qh:
                           sy.dma_start(self.outscr[d * 128:(d + 1) * 128,
                                                    q * 1024:(q + 1) * 1024],
                                        self.stg[:, (d % 2) * 1024:(d % 2) * 1024 + 1024]),
                           waits=_later(ev_e, *(scr_rd if dc < 2 else [])),
                           inc=sd_sp, inc_n=16)
            self.free[("stg", half)] = ev_sp

    def mla_proj(self, s, qh):
        p = self.p
        s_act = self.sems["s_act"]
        slot = 0
        for dc in range(8):
            for qc in range(2):
                ps = self.ps_a if slot % 2 == 0 else self.ps_b
                pn = "ps_a" if slot % 2 == 0 else "ps_b"
                pairs = [(self.foB[0:64, kc * 1024 + dc * 128: kc * 1024 + dc * 128 + 128],
                          self.aoT[0:64, kc * 1024 + qc * 512: kc * 1024 + qc * 512 + 512])
                         for kc in range(8)]
                ev = self.acc_group(ps, pn, pairs, [self.ev_foB, self.ev_aoT])
                ev_e = p.emit("scalar", lambda sc,
                              o=self.mlaT[:, dc * S + qh * 1024 + qc * 512:
                                          dc * S + qh * 1024 + qc * 512 + 512],
                              i=ps, c=dc:
                              sc.activation(o, i, AF.Identity,
                                            bias=self.obcol[:, c:c + 1]),
                              waits=[ev], inc=s_act)
                self.free[(pn,)] = ev_e
                slot += 1
        self.ev_mla = (s_act, s_act.n)

    def router_phase(self, s, qh):
        p = self.p
        s_pe, s_act, s_dve = (self.sems[n] for n in ("s_pe", "s_act", "s_dve"))
        ev = None
        for qc in range(2):
            pairs = [(self.routw[:, ci * 8: ci * 8 + 8],
                      self.mlaT[:, ci * S + qh * 1024 + qc * 512:
                                ci * S + qh * 1024 + qc * 512 + 512])
                     for ci in range(8)]
            ev = self.acc_group(self.ps_big, "ps_big" if qc == 0 else "__skip",
                                pairs,
                                [self.ev_mla, self.ev_misc_ld] if qc == 0 else [],
                                rows=8, cols=512, col0=qc * 512)
        ev_l = p.emit("scalar", lambda sc:
                      sc.activation(self.probsL[0:8, 0:1024],
                                    self.ps_big[0:8, 0:1024],
                                    AF.Identity, bias=self.routb[0:8, 0:1]),
                      waits=[ev], inc=s_act)
        self.free[("ps_big",)] = ev_l
        for ti in range(8):
            po = (ti % 2) * 512
            ev_tr = p.emit("tensor", lambda t, o=po,
                           i=self.probsL[0:8, ti * 128:(ti + 1) * 128]:
                           t.transpose(self.ps_big[0:128, o:o + 8], i,
                                       self.identF[0:8, 0:8]),
                           waits=_later(ev_l, self.free.get(("rps", ti % 2)),
                                        self.free.get(("ps_big",)) if ti < 2 else None),
                           inc=s_pe)
            pl = self.ps_big[0:128, po:po + 8]
            ev1 = p.emit("vector", lambda v, i=pl: v.reduce_max(
                self.t0, i, axis=AX, negate=True), waits=[ev_tr], inc=s_dve)
            ev2 = p.emit("scalar", lambda sc, i=pl: sc.activation(
                self.combF, i, AF.Exp, bias=self.t0), waits=[ev1], inc=s_act)
            self.free[("rps", ti % 2)] = ev2
            ev = ev2
            for fn in [
                    lambda v: v.reduce_sum(self.t1, self.combF, axis=AX),
                    lambda v: v.reciprocal(self.t1, self.t1),
                    lambda v: v.tensor_scalar_mul(self.combF, self.combF,
                                                  self.t1[:, 0:1]),
                    lambda v: v.reduce_max(self.t0, self.combF, axis=AX),
                    lambda v: v.tensor_single_scalar(self.eq8, self.combF,
                                                     self.t0[:, 0:1], ALU.is_equal),
                    lambda v: v.scalar_tensor_tensor(self.eq8, self.eq8, -1e30,
                                                     self.combF, op0=ALU.mult,
                                                     op1=ALU.add),
                    lambda v: v.reduce_max(self.t2, self.eq8, axis=AX),
                    lambda v: v.tensor_single_scalar(self.eq8, self.combF,
                                                     self.t2[:, 0:1], ALU.is_ge),
                    lambda v: v.tensor_add(self.t3, self.t0, self.t2),
                    lambda v: v.tensor_scalar_add(self.t3, self.t3, 1e-8),
                    lambda v: v.reciprocal(self.t3, self.t3),
                    lambda v: v.tensor_scalar_mul(self.t3, self.t3, 0.3),
                    lambda v: v.tensor_mul(self.combF, self.combF, self.eq8),
                    lambda v: v.tensor_scalar_mul(self.combF, self.combF,
                                                  self.t3[:, 0:1])]:
                ev = p.emit("vector", fn, waits=[ev], inc=s_dve)
            ev4 = p.emit("vector", lambda v: v.tensor_copy(self.combB, self.combF),
                         waits=_later(ev, self.free.get(("combB",))), inc=s_dve)
            ev_t2 = p.emit("tensor", lambda t, o=(ti % 2) * 1024:
                           t.transpose(self.ps_tr[0:8, o:o + 128], self.combB,
                                       self.identB),
                           waits=_later(ev4, self.free.get(("ctr", ti % 2)),
                                        self.free.get(("ps_tr", ti % 2))
                                        if ti < 2 else None),
                           inc=s_pe)
            self.free[("combB",)] = ev_t2
            ev5 = p.emit("vector", lambda v, o=(ti % 2) * 1024, t=ti:
                         v.tensor_copy(self.combT[0:8, qh * 1024 + t * 128:
                                                  qh * 1024 + (t + 1) * 128],
                                       self.ps_tr[0:8, o:o + 128]),
                         waits=[ev_t2], inc=s_dve)
            self.free[("ctr", ti % 2)] = ev5
            self.free[("ps_tr", ti % 2)] = ev5
        self.ev_comb = (s_dve, s_dve.n)
        self.free[("ps_big",)] = (s_act, s_act.n)

    def moe_phase(self, s):
        p = self.p
        s_pe, s_act, s_dve, s_pool = (self.sems[n] for n in
                                      ("s_pe", "s_act", "s_dve", "s_pool"))
        sd_w1, sd_w2 = self.sems["sd_w1"], self.sems["sd_w2"]
        ev_sp = [(self.sems["sd_sp0"], self.sems["sd_sp0"].n),
                 (self.sems["sd_sp1"], self.sems["sd_sp1"].n)]
        ev_fin_prev = _later(self.free.get(("outscr_rd0",)),
                             self.free.get(("outscr_rd1",)))
        ev_b2_done = self.pe_now()

        e_list = [int(t) for t in os.environ.get("KDBG_ELIST", "").split(",")] \
            if os.environ.get("KDBG_ELIST") else list(range(E))
        qc_list = [int(t) for t in os.environ.get("KDBG_QCLIST", "").split(",")] \
            if os.environ.get("KDBG_QCLIST") else list(range(4))
        for ei, e in enumerate(e_list):
            ev_w1 = self.ev_w1[e]
            ev_w2 = self.ev_w2[e]
            ev_bc_e = None
            for half in range(2):
                evb = None
                for qc in range(2):
                    evb = p.emit("tensor", lambda t, hh=half, q=qc, ee=e:
                                 t.matmul(self.ps_big[0:128, q * 512:q * 512 + 512],
                                          self.selc[0:8, ee * 128:(ee + 1) * 128],
                                          self.combT[0:8, hh * 1024 + q * 512:
                                                     hh * 1024 + q * 512 + 512],
                                          start=True, stop=True),
                                 waits=_later(self.ev_comb,
                                              self.free.get(("ps_big",))
                                              if qc == 0 else None,
                                              self.ev_small if (e == 0 and half == 0
                                                                and qc == 0) else None),
                                 inc=s_pe)
                ev_bc_e = p.emit("vector", lambda v, hh=half:
                                 v.tensor_copy(self.bcast[:, hh * 1024:(hh + 1) * 1024],
                                               self.ps_big[0:128, 0:1024]),
                                 waits=_later(evb, ev_b2_done
                                              if (e == 0 and half == 0) else None),
                                 inc=s_dve)
                self.free[("ps_big",)] = ev_bc_e
            for qc in qc_list:
                ev_h2_prev = self.free.get(("h1g",))
                ev_scale = {}
                for fi in range(16):
                    sl = (fi + int(os.environ.get("KDBG_SLSWAP", "0"))) % 2
                    ps = self.ps_a if sl == 0 else self.ps_b
                    pn = "ps_a" if sl == 0 else "ps_b"
                    pairs = [(self.w1s[:, ci * 2048 + fi * 128: ci * 2048 + fi * 128 + 128],
                              self.mlaT[:, ci * S + qc * 512: ci * S + qc * 512 + 512])
                             for ci in range(8)]
                    ev = self.acc_group(ps, pn, pairs,
                                        [ev_w1, self.ev_mla]
                                        if (fi == 0 and qc == 0) else [])
                    if os.environ.get("KDBG_H1PS") and fi == 0:
                        p.emit("vector", lambda v, i=ps:
                               v.tensor_copy(self.out_st[:, 0:512], i),
                               waits=[ev], inc=s_dve)
                        self._h1ps_done = True
                        return
                    hslice = self.h1g[:, fi * 512:(fi + 1) * 512]
                    gb = self.attb[:, (fi % 2) * 512:(fi % 2) * 512 + 512]
                    ev_g = p.emit("scalar", lambda sc, o=gb, i=ps, f=fi, ee=e:
                                  sc.activation(o, i, AF.Gelu,
                                                bias=self.b1col[:, f * 8 + ee:
                                                                f * 8 + ee + 1]),
                                  waits=_later(ev, ev_h2_prev,
                                               self.free.get(("gb", fi % 2))),
                                  inc=s_act)
                    self.free[(pn,)] = ev_g
                    ev_scale[fi] = p.emit("vector", lambda v, o=hslice, i=gb, q=qc:
                                          v.tensor_mul(o, i,
                                                       self.bcast[:, q * 512:
                                                                  q * 512 + 512]),
                                          waits=_later(ev_g, ev_bc_e), inc=s_dve)
                    self.free[("gb", fi % 2)] = ev_scale[fi]
                ev_h2_last = None
                for dc in range(8):
                    sl = dc % 2
                    psh = self.ps_c if sl == 0 else self.ps_d
                    pnh = "ps_c" if sl == 0 else "ps_d"
                    ev = None
                    for fi in range(16):
                        w = []
                        if fi == 0:
                            w = _later(self.free.get((pnh,)),
                                       ev_w2 if (dc == 0 and qc == qc_list[0])
                                       else None)
                        if dc == 0:
                            w = _later(*(w + [ev_scale[fi]]))
                        ev = p.emit("tensor", lambda t,
                                    l=self.w2s[:, fi * 1024 + dc * 128:
                                               fi * 1024 + dc * 128 + 128],
                                    r=self.h1g[:, fi * 512:(fi + 1) * 512],
                                    st=(fi == 0),
                                    sp=(fi == 15 and (ei != 0 or
                                        bool(os.environ.get("KDBG_NOB")))), o=psh:
                                    t.matmul(o, l, r, start=st, stop=sp),
                                    waits=w,
                                    inc=(s_pe if (fi == 15 and (ei != 0 or
                                         bool(os.environ.get("KDBG_NOB"))))
                                         else None))
                    if ei == 0 and not os.environ.get("KDBG_NOB"):
                        ev = p.emit("tensor", lambda t, o=psh, d=dc, q=qc:
                                    t.matmul(o, self.b2s[0:8, d * 128:(d + 1) * 128],
                                             self.combT[0:8, q * 512:(q + 1) * 512],
                                             start=False, stop=True),
                                    inc=s_pe)
                    elif ei == 0:
                        pass
                    ev_h2_last = ev
                    half = (dc // 2) % 2
                    sd_ac = self.sems["sd_ac0"] if half == 0 else self.sems["sd_ac1"]
                    ev_e = p.emit("vector", lambda v,
                                  o=self.stg[:, half * 1024 + (dc % 2) * 512:
                                             half * 1024 + (dc % 2) * 512 + 512],
                                  i=psh[0:128, 0:512]:
                                  v.tensor_copy(o, i),
                                  waits=_later(ev,
                                               self.free.get(("stg2", half))
                                               if dc % 2 == 0 else None,
                                               self.free.get(("stg", half))
                                               if (e == 0 and qc == 0 and dc < 2)
                                               else None),
                                  inc=s_dve)
                    self.free[(pnh,)] = ev_e
                    if dc % 2 == 1:
                        dst = self.outscr.rearrange("(dc p) t -> p dc t", p=128)[
                            :, dc - 1:dc + 1, qc * 512:qc * 512 + 512]
                        src = self.stg[:, half * 1024: half * 1024 + 1024].rearrange(
                            "p (a t) -> p a t", t=512)
                        if ei == 0:
                            w = _later(ev_e, *ev_sp, ev_fin_prev[0]
                                       if ev_fin_prev else None,
                                       ev_fin_prev[1] if ev_fin_prev and
                                       len(ev_fin_prev) > 1 else None)
                        else:
                            w = [ev_e]
                        ev_ac = p.emit("gpsimd", lambda g, d=dst, i=src:
                                       g.dma_start(d, i, accum_op=ALU.add),
                                       waits=w, inc=sd_ac, inc_n=16)
                        self.free[("stg2", half)] = ev_ac
                self.free[("h1g",)] = ev_h2_last
            if ei + 1 < len(e_list):
                nxt = e_list[ei + 1]
                self.ev_w1[nxt] = p.emit(
                    "sync", lambda sy, ee=nxt:
                    sy.dma_start(self.w1s.rearrange("p (ci f) -> p ci f", f=2048),
                                 self.w1_b[ee].rearrange(
                        "(ci p) f -> p ci f", p=128)),
                    waits=_later(self.pe_now(), self.ev_cast[f"w1_{nxt}"]),
                    inc=sd_w1, inc_n=16)
                self.ev_w2[nxt] = p.emit(
                    "sync", lambda sy, ee=nxt:
                    sy.dma_start(self.w2s.rearrange("p (fi d) -> p fi d", d=1024),
                                 self.w2_b[ee].rearrange(
                        "(fi p) d -> p fi d", p=128)),
                    waits=_later(self.pe_now(), self.ev_cast[f"w2_{nxt}"]),
                    inc=sd_w2, inc_n=16)
        self.ev_acc = [(self.sems["sd_ac0"], self.sems["sd_ac0"].n),
                       (self.sems["sd_ac1"], self.sems["sd_ac1"].n)]
        self.free[("stg", 0)] = self.free.get(("stg2", 0))
        self.free[("stg", 1)] = self.free.get(("stg2", 1))
        pe_end = self.pe_now()
        self.free[("wtb",)] = pe_end
        self.free[("C",)] = pe_end
        self.free[("qbuf",)] = pe_end
        self.free[("xnT_w",)] = pe_end
        self.free[("vtok",)] = pe_end
        self.free[("vtok2",)] = (s_pool, s_pool.n)

    def final_phase(self, s):
        p = self.p
        s_pe, s_dve = self.sems["s_pe"], self.sems["s_dve"]
        for ti in range(16):
            b = ti % 2
            sd_f = self.sems["sd_f0"] if b == 0 else self.sems["sd_f1"]
            sd_x = self.sems["sd_x0"] if b == 0 else self.sems["sd_x1"]
            fin = self.fin_in[:, b * 1024:(b + 1) * 1024]
            src = self.outscr.rearrange("(dc p) (ti t) -> p ti dc t",
                                        p=128, t=128)[:, ti]
            ev_f = p.emit("sync", lambda sy, o=fin, i=src:
                          sy.dma_start(o, i),
                          waits=_later(*self.ev_acc, self.free.get(("fin", b))),
                          inc=sd_f, inc_n=16)
            xb = self.x_any[:, b * 1024:(b + 1) * 1024]
            ev_x = p.emit("sync", lambda sy, o=xb, ss=s, t=ti: sy.dma_start(
                o, self.x_in[ss, t * 128:(t + 1) * 128, :]),
                waits=_later(self.free.get(("x_any", b))), inc=sd_x, inc_n=16)
            evt = None
            for dc in range(8):
                if b == 0:
                    po = self.ps_big[:, dc * 128:(dc + 1) * 128]
                    pkeys = [("ps_big",)]
                else:
                    pt = self.ps_c if dc < 4 else self.ps_d
                    po = pt[:, (dc % 4) * 128:(dc % 4) * 128 + 128]
                    pkeys = [("ps_c",), ("ps_d",)]
                w = _later(ev_f,
                           *([self.free.get(k) for k in pkeys] if dc == 0 else []))
                evt = p.emit("tensor", lambda t, o=po,
                             i=fin[:, dc * 128:(dc + 1) * 128]:
                             t.transpose(o, i, self.identF),
                             waits=w, inc=(s_pe if dc == 7 else None))
            self.free[("fin", b)] = evt
            ev_o = None
            for hc in range(2):
                xp = self.out_st[:, b * 1024 + hc * 512: b * 1024 + hc * 512 + 512]
                ev1 = p.emit("vector", lambda v, o=xp,
                             i=xb[:, hc * 512:hc * 512 + 512], t=ti:
                             v.tensor_scalar(o, i, self.rstd1[:, t:t + 1],
                                             self.nmr[:, t:t + 1],
                                             op0=ALU.mult, op1=ALU.add),
                             waits=_later(ev_x, self.free.get(("out_st", b))
                                          if hc == 0 else None),
                             inc=s_dve)
                if b == 0:
                    pa = self.ps_big[:, hc * 512: hc * 512 + 512]
                else:
                    pa = (self.ps_c if hc == 0 else self.ps_d)[:, 0:512]
                ev_o = p.emit("vector", lambda v, o=xp, i=pa:
                              v.tensor_add(o, o, i),
                              waits=_later(evt, ev1), inc=s_dve)
            if b == 0:
                self.free[("ps_big",)] = ev_o
            else:
                self.free[("ps_c",)] = ev_o
                self.free[("ps_d",)] = ev_o
            self.free[("x_any", b)] = ev_o
            sd_st = self.sems["sd_st0"] if b == 0 else self.sems["sd_st1"]
            ev_st = p.emit("sync", lambda sy, ss=s, t=ti,
                           o=self.out_st[:, (ti % 2) * 1024:(ti % 2) * 1024 + 1024]:
                           sy.dma_start(self.y_out[ss, t * 128:(t + 1) * 128, :], o),
                           waits=[ev_o], inc=sd_st, inc_n=16)
            self.free[("out_st", b)] = ev_st
        self.free[("outscr_rd0",)] = (self.sems["sd_f0"], self.sems["sd_f0"].n)
        self.free[("outscr_rd1",)] = (self.sems["sd_f1"], self.sems["sd_f1"].n)
        self.free[("stg", 0)] = self.pe_now()
        self.free[("stg", 1)] = self.pe_now()

    def _dbg_dump(self, dumps):
        p = self.p
        sd_st = self.sems["sd_st0"]
        waits = []
        for nm in ("s_pe", "s_act", "s_dve", "s_pool"):
            waits.append((self.sems[nm], self.sems[nm].n))
        for nm in ("sd_wa", "sd_wb", "sd_w1", "sd_w2", "sd_x0", "sd_x1",
                   "sd_f0", "sd_f1", "sd_sp0", "sd_sp1", "sd_st0", "sd_st1",
                   "sd_ac0", "sd_ac1", "sd_g", "sd_misc"):
            waits.append((self.sems[nm], self.sems[nm].n))
        first = True
        for nm, ap, shape, dt in dumps:
            out = self.nc.dram_tensor("dbg_" + nm, shape, dt,
                                      kind="ExternalOutput").ap()
            p.emit("sync", lambda sy, o=out, i=ap: sy.dma_start(o, i),
                   waits=waits if first else (), inc=sd_st, inc_n=16)
            first = False
        p.emit("sync", None, waits=[(sd_st, sd_st.n)])

    def build_debug(self, stage):
        self.setup()
        p = self.p
        self.phase0()
        self.ev_w1 = {}
        self.ev_w2 = {}
        s = 0
        dumps = []
        while True:
            if stage == "bias":
                dumps = [("biasc", self.biasc, [128, 224], F32),
                         ("braw", self.braw, [8, 2048], F32)]
                break
            self.ev_wtA = self.load_wt(0, "aq",
                                       extra_waits=[self.free.get(("wtb",))])
            self.ev_wtB = self.load_wt(1, "ak",
                                       extra_waits=[self.free.get(("wtb",))])
            self.ln_phase(s)
            if stage == "ln":
                dumps = [("xnT", self.xnT, [128, 8 * S], BF16),
                         ("negm", self.negm, [128, 16], F32),
                         ("rstd", self.rstd, [128, 16], F32)]
                break
            self.conv_phase(s)
            self.kv1_phase(s)
            if stage == "kv1":
                dumps = [("xdT", self.xdT, [128, 8192], BF16),
                         ("kT", self.kT, [128, 8192], BF16),
                         ("v1tok", self.v1tok, [128, 8320], BF16)]
                break
            self.ev_wtB = self.load_wt(1, "ao")
            self.q_proj(s, 0, 1)
            if stage == "q1":
                dumps = [("qT", self.qT, [128, 8192], BF16)]
                break
            self.attn_phase(s, 0, 1)
            if stage == "att1":
                dumps = [("oT", self.oT, [128, 8192], BF16),
                         ("osb", self.osb, [128, 1024], F32)]
                break
            self.free[("qbuf",)] = self.pe_now()
            self.out_proj(s, 0)
            self.free[("oa",)] = self.pe_now()
            if stage == "outp":
                dumps = [("oscr", self.outscr, [D, S], F32)]
                break
            self.q_proj(s, 1, 1)
            self.attn_phase(s, 1, 1)
            self.free[("qbuf",)] = self.pe_now()
            self.out_proj(s, 1)
            self.free[("oa",)] = self.pe_now()
            self.ev_wtA = self.load_wt(0, "fq", extra_waits=[self.pe_now()])
            self.ev_wtB = self.load_wt(1, "fk", extra_waits=[self.pe_now()])
            self.kv2_phase(s)
            if stage == "kv2":
                dumps = [("k2T", self.k2T, [64, 16384], BF16),
                         ("v2tok", self.v2tok, [128, 8448], BF16)]
                break
            self.ev_foB = self.load_wt(1, "fo")
            self.q_proj(s, 0, 2)
            if stage == "q2":
                dumps = [("q2c", self.q2c, [64, 8192], BF16)]
                break
            self.attn_phase(s, 0, 2)
            if stage == "att2":
                dumps = [("aoT", self.aoT, [64, 8192], BF16)]
                break
            self.free[("qbuf",)] = self.pe_now()
            self.mla_proj(s, 0)
            self.free[("oa",)] = self.pe_now()
            if stage == "mla":
                dumps = [("mlaT", self.mlaT, [128, 8 * S], BF16)]
                break
            self.router_phase(s, 0)
            if stage == "rout":
                dumps = [("combT", self.combT, [8, S], BF16),
                         ("probsL", self.probsL, [8, 1024], F32),
                         ("combF", self.combF, [128, 8], F32),
                         ("eq8", self.eq8, [128, 8], F32),
                         ("stats", self.stats, [128, 80], F32),
                         ("combB", self.combB, [128, 8], BF16)]
                break
            self.q_proj(s, 1, 2)
            self.attn_phase(s, 1, 2)
            self.free[("qbuf",)] = self.pe_now()
            self.mla_proj(s, 1)
            self.free[("oa",)] = self.pe_now()
            self.router_phase(s, 1)
            e0 = int(os.environ.get("KDBG_ELIST", "0").split(",")[0])
            self.ev_w1[e0] = p.emit("sync", lambda sy, ee=e0: sy.dma_start(
                self.w1s.rearrange("p (ci f) -> p ci f", f=2048),
                self.w1_b[ee].rearrange("(ci p) f -> p ci f", p=128)),
                waits=_later(self.ev_cast[f"w1_{e0}"], self.pe_now()),
                inc=self.sems["sd_w1"], inc_n=16)
            self.ev_w2[e0] = p.emit("sync", lambda sy, ee=e0: sy.dma_start(
                self.w2s.rearrange("p (fi d) -> p fi d", d=1024),
                self.w2_b[ee].rearrange("(fi p) d -> p fi d", p=128)),
                waits=_later(self.ev_cast[f"w2_{e0}"], self.pe_now()),
                inc=self.sems["sd_w2"], inc_n=16)
            self.moe_phase(s)
            if stage == "moe" and os.environ.get("KDBG_H1PS"):
                dumps = [("h1ps", self.out_st, [128, 2048], F32),
                         ("mlaT", self.mlaT, [128, 8 * S], BF16),
                         ("w1s", self.w1s, [128, 16384], BF16)]
                break
            if stage == "moe":
                dumps = [("oscr", self.outscr, [D, S], F32),
                         ("combT", self.combT, [8, S], BF16),
                         ("mlaT", self.mlaT, [128, 8 * S], BF16),
                         ("h1g", self.h1g, [128, 8192], BF16),
                         ("bcast", self.bcast, [128, 2048], BF16),
                         ("w1s", self.w1s, [128, 16384], BF16),
                         ("w2s", self.w2s, [128, 16384], BF16),
                         ("b2s", self.b2s, [8, D], BF16),
                         ("attb", self.attb, [128, 2048], BF16)]
                break
            self.final_phase(s)
            dumps = [("y0", self.y_out[0], [S, D], F32)]
            break
        self._dbg_dump(dumps)
        with self.nc.Block() as block:
            block.tensor(lambda t: self.p.run("tensor", t))
            block.vector(lambda v: self.p.run("vector", v))
            block.scalar(lambda sc: self.p.run("scalar", sc))
            block.gpsimd(lambda g: self.p.run("gpsimd", g))
            block.sync(lambda sy: self.p.run("sync", sy))
        return self.nc

    def build(self):
        self.setup()
        p = self.p
        self.phase0()
        self.ev_w1 = {}
        self.ev_w2 = {}
        for s in range(NSEQ):
            self.ev_wtA = self.load_wt(0, "aq",
                                       extra_waits=[self.free.get(("wtb",))])
            self.ev_wtB = self.load_wt(1, "ak",
                                       extra_waits=[self.free.get(("wtb",))])
            self.ln_phase(s)
            self.conv_phase(s)
            self.kv1_phase(s)
            self.ev_wtB = self.load_wt(1, "ao")
            for qh in range(2):
                self.q_proj(s, qh, 1)
                self.attn_phase(s, qh, 1)
                self.free[("qbuf",)] = self.pe_now()
                self.out_proj(s, qh)
                self.free[("oa",)] = self.pe_now()
            self.ev_wtA = self.load_wt(0, "fq", extra_waits=[self.pe_now()])
            self.ev_wtB = self.load_wt(1, "fk", extra_waits=[self.pe_now()])
            self.kv2_phase(s)
            self.ev_foB = self.load_wt(1, "fo")
            for qh in range(2):
                self.q_proj(s, qh, 2)
                self.attn_phase(s, qh, 2)
                self.free[("qbuf",)] = self.pe_now()
                self.mla_proj(s, qh)
                self.free[("oa",)] = self.pe_now()
                self.router_phase(s, qh)
            e0 = int(os.environ.get("KDBG_ELIST", "0").split(",")[0])
            self.ev_w1[e0] = p.emit("sync", lambda sy, ee=e0: sy.dma_start(
                self.w1s.rearrange("p (ci f) -> p ci f", f=2048),
                self.w1_b[ee].rearrange("(ci p) f -> p ci f", p=128)),
                waits=_later(self.ev_cast[f"w1_{e0}"], self.pe_now()),
                inc=self.sems["sd_w1"], inc_n=16)
            self.ev_w2[e0] = p.emit("sync", lambda sy, ee=e0: sy.dma_start(
                self.w2s.rearrange("p (fi d) -> p fi d", d=1024),
                self.w2_b[ee].rearrange("(fi p) d -> p fi d", p=128)),
                waits=_later(self.ev_cast[f"w2_{e0}"], self.pe_now()),
                inc=self.sems["sd_w2"], inc_n=16)
            self.moe_phase(s)
            self.final_phase(s)
        p.emit("sync", None, waits=[(self.sems["sd_st0"], self.sems["sd_st0"].n),
                                    (self.sems["sd_st1"], self.sems["sd_st1"].n)])

        with self.nc.Block() as block:
            block.tensor(lambda t: self.p.run("tensor", t))
            block.vector(lambda v: self.p.run("vector", v))
            block.scalar(lambda sc: self.p.run("scalar", sc))
            block.gpsimd(lambda g: self.p.run("gpsimd", g))
            block.sync(lambda sy: self.p.run("sync", sy))
        return self.nc


_built = None


def _get_built():
    global _built
    if _built is None:
        _built = Build().build()
    return _built


def kernel(**inputs):
    nc = _get_built()
    x = np.ascontiguousarray(inputs["L0_attn_x"])
    shards = np.split(x, 8, axis=0)
    weights = {k: np.ascontiguousarray(v) for k, v in inputs.items()
               if k != "L0_attn_x"}
    in_maps = []
    for c in range(8):
        m = {"x": shards[c]}
        m.update(weights)
        in_maps.append(m)
    res = run_bass_kernel_spmd(nc, in_maps, core_ids=list(range(8)))
    return np.concatenate([r["y"] for r in res.results], axis=0)


# version 04
# speedup vs baseline: 1.0893x; 1.0096x over previous
import math
import os
import numpy as np
from contextlib import ExitStack

import concourse.bass as bass
import concourse.mybir as mybir
from concourse.bass_utils import run_bass_kernel_spmd

F32 = mybir.dt.float32
BF16 = mybir.dt.bfloat16
AF = mybir.ActivationFunctionType
ALU = mybir.AluOpType
AX = mybir.AxisListType.X

NSEQ = 4
S = 2048
D = 1024
H = 16
KPH = 32
LAT = 512
E = 8
DFF = 2048
SC1 = 1.0 / 8.0
SC2 = 1.0 / math.sqrt(KPH)

ENGINES = ("tensor", "vector", "scalar", "gpsimd", "sync")


class _Sem:
    __slots__ = ("h", "n", "name")

    def __init__(self, h, name):
        self.h = h
        self.n = 0
        self.name = name


class _Prog:
    def __init__(self):
        self.ops = {e: [] for e in ENGINES}

    def emit(self, eng, fn, waits=(), inc=None, inc_n=1):
        w = tuple((s, v) for (s, v) in waits if s is not None and v > 0)
        ev = None
        if inc is not None:
            inc.n += inc_n
            ev = (inc, inc.n)
        self.ops[eng].append((w, fn, inc, inc_n if inc else 0))
        return ev

    def run(self, eng_name, eng):
        seen = {}
        for w, fn, inc, n in self.ops[eng_name]:
            for (s, v) in w:
                if seen.get(s.name, 0) >= v:
                    continue
                seen[s.name] = v
                eng.wait_ge(s.h, v)
            ins = fn(eng) if fn is not None else None
            if inc is not None:
                ins.then_inc(inc.h, n)


def _later(*evs):
    best = {}
    for ev in evs:
        if ev is None:
            continue
        s, v = ev
        if s is None or v <= 0:
            continue
        if best.get(s.name, (None, 0))[1] < v:
            best[s.name] = (s, v)
    return list(best.values())


class Build:
    def __init__(self):
        self.nc = bass.Bass("TRN2", target_bir_lowering=False)
        self.p = _Prog()
        self.ctx = ExitStack()
        self.sems = {}
        self.free = {}

    def sem(self, name):
        if name not in self.sems:
            h = self.ctx.enter_context(self.nc.semaphore(name))
            self.sems[name] = _Sem(h, name)
        return self.sems[name]

    def pe_now(self):
        return (self.sems["s_pe"], self.sems["s_pe"].n)

    def setup(self):
        nc = self.nc
        self.x_in = nc.dram_tensor("x", [NSEQ, S, D], F32, kind="ExternalInput").ap()
        wn = lambda n, shp: nc.dram_tensor(n, shp, F32, kind="ExternalInput").ap()
        self.aqw = wn("L0_attn_q_w", [D, D]);   self.aqb = wn("L0_attn_q_b", [D])
        self.akw = wn("L0_attn_k_w", [D, D]);   self.akb = wn("L0_attn_k_b", [D])
        self.avw = wn("L0_attn_v_w", [D, D]);   self.avb = wn("L0_attn_v_b", [D])
        self.acw = wn("L0_attn_conv_w", [D, 1, 3]); self.acb = wn("L0_attn_conv_b", [D])
        self.aow = wn("L0_attn_out_w", [D, D]); self.aob = wn("L0_attn_out_b", [D])
        self.fqw = wn("L0_ffn_q_w", [D, D]);    self.fqb = wn("L0_ffn_q_b", [D])
        self.fkw = wn("L0_ffn_k_w", [D, LAT]);  self.fkb = wn("L0_ffn_k_b", [LAT])
        self.fvw = wn("L0_ffn_v_w", [D, LAT]);  self.fvb = wn("L0_ffn_v_b", [LAT])
        self.fow = wn("L0_ffn_o_w", [D, D]);    self.fob = wn("L0_ffn_o_b", [D])
        self.frw = wn("L0_ffn_router_w", [D, E]); self.frb = wn("L0_ffn_router_b", [E])
        self.w1 = wn("L0_ffn_w1", [E, D, DFF]); self.b1 = wn("L0_ffn_b1", [E, DFF])
        self.w2 = wn("L0_ffn_w2", [E, DFF, D]); self.b2 = wn("L0_ffn_b2", [E, D])
        self.y_out = nc.dram_tensor("y", [NSEQ, S, D], F32, kind="ExternalOutput").ap()

        ds = lambda n, shp, dt: nc.dram_tensor(n, shp, dt, kind="Internal").ap()
        self.aqw_b = ds("aqw_b", [D, D], BF16)
        self.akw_b = ds("akw_b", [D, D], BF16)
        self.avw_b = ds("avw_b", [D, D], BF16)
        self.aow_b = ds("aow_b", [D, D], BF16)
        self.fqw_b = ds("fqw_b", [D, D], BF16)
        self.fkw_b = ds("fkw_b", [D, LAT], BF16)
        self.fvw_b = ds("fvw_b", [D, LAT], BF16)
        self.fow_b = ds("fow_b", [LAT, D], BF16)
        self.w1_b = ds("w1_b", [E, D, DFF], BF16)
        self.w2_b = ds("w2_b", [E, DFF, D], BF16)
        self.outscr = ds("outscr", [D, S], F32)

        sel = np.zeros((E, E * 128), dtype=np.float32)
        for e in range(E):
            sel[e, e * 128:(e + 1) * 128] = 1.0
        import ml_dtypes
        self.selc_d = nc.inline_tensor(sel.astype(ml_dtypes.bfloat16), name="selcd").ap()

        arena = nc.alloc_sbuf_tensor("arena", [128, 206848], mybir.dt.uint8)
        base = nc.lookup_mloc(arena).addr
        self._cur = base
        self._limit = base + 206848

        def at(name, shape, dt, off=None):
            nbytes = shape[1] * (2 if dt == BF16 else 4)
            if off is None:
                off = self._cur
                self._cur = off + ((nbytes + 31) // 32) * 32
            assert off + nbytes <= self._limit, (name, off + nbytes - base)
            return nc.alloc_sbuf_tensor_at(name, shape, dt, offset=off).ap(), off

        self.xnT, off_A = at("xnT", [128, 8 * S], BF16)
        self.mlaT, _ = at("mlaT", [128, 8 * S], BF16, off=off_A)
        self.qT, off_B = at("qT", [128, 8192], BF16)
        self.q2c, _ = at("q2c", [64, 8192], BF16, off=off_B)
        self.h1g, _ = at("h1g", [128, 8192], BF16, off=off_B)
        self.xdT, off_C = at("xdT", [128, 8192], BF16)
        self.kT, _ = at("kT", [128, 8192], BF16)
        self.k2T, _ = at("k2T", [64, 16384], BF16, off=off_C)
        self.w2s, _ = at("w2s", [128, 16384], BF16, off=off_C)
        self.v1tok, off_D = at("v1tok", [128, 8320], BF16)
        self.v2tok, _ = at("v2tok", [128, 8448], BF16, off=off_D)
        self.bcast, _ = at("bcast", [128, 2048], BF16, off=off_D)
        self._cur = max(self._cur, off_D + 16896)
        self.oT, off_E = at("oT", [128, 8192], BF16)
        self.aoT, _ = at("aoT", [64, 8192], BF16, off=off_E)
        self.w1s, off_F = at("w1s", [128, 16384], BF16)
        self.wtA, _ = at("wtA", [128, 8192], BF16, off=off_F)
        self.wtB, _ = at("wtB", [128, 8192], BF16, off=off_F + 16384)
        self.foB, _ = at("foB", [64, 8192], BF16, off=off_F + 16384)
        self.attb, _ = at("attb", [128, 2048], BF16)
        self.identB, _ = at("identB", [128, 128], BF16)
        self.onesB, _ = at("onesB", [1, 128], BF16)
        self.selc, _ = at("selc", [8, E * 128], BF16)
        self.b2s, _ = at("b2s", [8, D], BF16)
        self.combT, _ = at("combT", [8, S], BF16)
        self.vb1, _ = at("vb1", [1, D], BF16)
        self.v2b, _ = at("v2b", [1, LAT], BF16)
        self.routw, _ = at("routw", [128, 64], BF16)
        self.combB, _ = at("combB", [128, 8], BF16)
        self.stg, off_I = at("stg", [128, 2048], F32)
        self.braw, _ = at("braw", [8, 2048], F32, off=off_I)
        self.osb, _ = at("osb", [128, 1024], F32)
        self.sums, off_K = at("sums", [1, 1024], F32)
        self.probsL, _ = at("probsL", [8, 1024], F32, off=off_K)
        self._cur = max(self._cur, off_K + 4096)
        self.recip, _ = at("recip", [1, 1024], F32)
        self.x_any, _ = at("x_any", [128, 2048], F32)
        self.fin_in, _ = at("fin_in", [128, 2048], F32, off=off_I)
        self.out_st, off_N = at("out_st", [128, 2048], F32)
        self.sq_scr, _ = at("sq_scr", [128, 1024], F32, off=off_N)
        self.identF, _ = at("identF", [128, 128], F32)
        self.onesF, _ = at("onesF", [1, 128], F32)
        self.stats, _ = at("stats", [128, 80], F32)
        self.biasc, _ = at("biasc", [128, 224], F32)
        self.routb, _ = at("routb", [8, 1], F32)
        self.combF, _ = at("combF", [128, 8], F32)
        self.eq8, _ = at("eq8", [128, 8], F32)
        self.constv, _ = at("constv", [128, 8], F32)
        assert self._cur <= self._limit, self._cur - base

        bc = self.biasc
        self.qbcol = bc[:, 0:8]
        self.kbcol = bc[:, 8:16]
        self.q2bcol = bc[:, 16:24]
        self.k2bcol = bc[:, 24:28]
        self.obcol = bc[:, 28:36]
        self.outbcol = bc[:, 36:44]
        self.convbcol = bc[:, 44:52]
        self.convw_j = [bc[:, 52 + 8 * j:60 + 8 * j] for j in range(3)]
        self.b1col = bc[:, 76:204]
        st = self.stats
        self.negm = st[:, 0:16]
        self.rstd = st[:, 16:32]
        self.rstd1 = st[:, 32:48]
        self.nmr = st[:, 48:64]
        self.varsum = st[:, 64:65]
        self.t0 = st[:, 65:66]
        self.t1 = st[:, 66:67]
        self.t2 = st[:, 67:68]
        self.t3 = st[:, 68:69]

        self.ps_s0 = self.ctx.enter_context(nc.psum_tensor("ps_s0", [128, 1024], F32)).ap()
        self.ps_s1 = self.ctx.enter_context(nc.psum_tensor("ps_s1", [128, 1024], F32)).ap()
        self.ps_big = self.ctx.enter_context(nc.psum_tensor("ps_big", [128, 1024], F32)).ap()
        self.ps_tr = self.ctx.enter_context(nc.psum_tensor("ps_tr", [128, 2048], BF16)).ap()
        self.ps_a = self.ps_s0[:, 0:512]
        self.ps_b = self.ps_s0[:, 512:1024]
        self.ps_c = self.ps_s1[:, 0:512]
        self.ps_d = self.ps_s1[:, 512:1024]

        for n in ("sd_wa", "sd_wb", "sd_w1", "sd_w2", "sd_x0", "sd_x1",
                  "sd_f0", "sd_f1", "sd_sp0", "sd_sp1", "sd_st0", "sd_st1",
                  "sd_ac0", "sd_ac1", "sd_g", "sd_misc",
                  "s_pe", "s_act", "s_dve", "s_pool"):
            self.sem(n)

    def acc_group(self, ps, pname, pairs, waits0, rows=128, cols=512, col0=0):
        p, s_pe = self.p, self.sems["s_pe"]
        n = len(pairs)
        ev = None
        for i, (lt, rt) in enumerate(pairs):
            w = []
            if i == 0:
                w = _later(self.free.get((pname,)), *waits0)
            ev = p.emit("tensor",
                        lambda t, l=lt, r=rt, st=(i == 0), sp=(i == n - 1),
                        o=ps[0:rows, col0:col0 + cols]:
                        t.matmul(o, l, r, start=st, stop=sp),
                        waits=w, inc=(s_pe if i == n - 1 else None))
        return ev

    def phase0(self):
        p = self.p
        sd_g, sd_misc = self.sems["sd_g"], self.sems["sd_misc"]
        s_pe, s_dve, s_pool = (self.sems[n] for n in ("s_pe", "s_dve", "s_pool"))

        def mkident(g, ident, size):
            g.memset(ident, 0.0)
            return g.affine_select(out=ident, in_=ident, compare_op=ALU.not_equal,
                                   fill=1.0, base=0, pattern=[[-1, size]],
                                   channel_multiplier=1)
        p.emit("gpsimd", lambda g: mkident(g, self.identB, 128))
        p.emit("gpsimd", lambda g: mkident(g, self.identF, 128))
        p.emit("gpsimd", lambda g: g.memset(self.onesB, 1.0))
        for i, val in enumerate([1e-5, 1.0 / D, SC1, SC2]):
            ap = self.constv[:, i:i + 1]
            p.emit("gpsimd", lambda g, o=ap, v=val: g.memset(o, v))
            self.nc.const_aps.register(F32, val, ap)
        p.emit("gpsimd", lambda g: g.memset(self.onesF, 1.0), inc=s_pool)
        self.ev_consts = (s_pool, s_pool.n)

        casts = [("aq", self.aqw_b, self.aqw), ("ak", self.akw_b, self.akw),
                 ("av", self.avw_b, self.avw), ("ao", self.aow_b, self.aow),
                 ("fq", self.fqw_b, self.fqw), ("fk", self.fkw_b, self.fkw),
                 ("fv", self.fvw_b, self.fvw), ("fo", self.fow_b, self.fow[0:LAT, :])]
        for e in range(E):
            casts.append((f"w1_{e}", self.w1_b[e], self.w1[e]))
            casts.append((f"w2_{e}", self.w2_b[e], self.w2[e]))
        self.ev_cast = {}
        for nm, out_ap, in_ap in casts:
            self.ev_cast[nm] = p.emit(
                "gpsimd", lambda g, o=out_ap, i=in_ap: g.dma_start(o, i),
                inc=sd_g, inc_n=16)
        self.ev_cast_all = (sd_g, sd_g.n)

        for out_ap, in_ap, slow in [
                (self.selc, self.selc_d, False),
                (self.b2s, self.b2, False),
                (self.vb1, self.avb.rearrange("(o d) -> o d", o=1), False),
                (self.v2b, self.fvb.rearrange("(o d) -> o d", o=1), False),
                (self.routw.rearrange("p (ci e) -> p ci e", e=8),
                 self.frw.rearrange("(ci p) e -> p ci e", p=128), True)]:
            p.emit("gpsimd", lambda g, o=out_ap, i=in_ap, sl=slow:
                   g.dma_start(o, i, allow_slow_non_contiguous=sl), inc=sd_g, inc_n=16)
        self.ev_small = (sd_g, sd_g.n)

        ev_copy2 = None
        ev_copy1 = None
        slot = [0]

        def tr_step(src_ap, rows, dest_ap, ev_ld):
            nonlocal ev_copy1, ev_copy2
            po = (slot[0] % 2) * 512
            w = _later(ev_ld, self.ev_consts, ev_copy2)
            ev_tr = p.emit("tensor", lambda t, s=src_ap, r=rows, o=po:
                           t.transpose(self.ps_big[0:128, o:o + r], s,
                                       self.identF[0:r, 0:r]),
                           waits=w, inc=s_pe)
            ev_c = p.emit("vector", lambda v, d=dest_ap, o=po, r=rows:
                          v.tensor_copy(d, self.ps_big[0:128, o:o + r]),
                          waits=[ev_tr], inc=s_dve)
            ev_copy2, ev_copy1 = ev_copy1, ev_c
            slot[0] += 1

        jobs = [(self.aqb, 8, self.qbcol), (self.akb, 8, self.kbcol),
                (self.fqb, 8, self.q2bcol), (self.fkb, 4, self.k2bcol),
                (self.fob, 8, self.obcol), (self.aob, 8, self.outbcol),
                (self.acb, 8, self.convbcol)]
        for vec, rows, dest in jobs:
            src = self.braw[0:rows, 0:128]
            ev_ld = p.emit("sync", lambda sy, o=src,
                           i=vec.rearrange("(a p) -> a p", p=128): sy.dma_start(o, i),
                           waits=[self.pe_now()], inc=sd_misc, inc_n=16)
            tr_step(src, rows, dest[:, 0:rows], ev_ld)
        b1raw = self.braw[0:8, 0:2048]
        ev_ld = p.emit("sync", lambda sy: sy.dma_start(b1raw, self.b1),
                       waits=[self.pe_now()], inc=sd_misc, inc_n=16)
        for fi in range(16):
            tr_step(b1raw[0:8, fi * 128:(fi + 1) * 128], 8,
                    self.b1col[:, fi * 8:fi * 8 + 8], ev_ld)
        cwraw = self.braw[0:8, 0:384]
        ev_ld = p.emit("sync", lambda sy: sy.dma_start(
            cwraw, self.acw.rearrange("(a p) one j -> a (p one j)", a=8)),
            waits=[self.pe_now()], inc=sd_misc, inc_n=16)
        for j in range(3):
            cw2 = self.braw[0:8, 1024 + j * 128: 1024 + (j + 1) * 128]
            ev_cp = p.emit("vector", lambda v, o=cw2, i=cwraw[0:8, j:384:3]:
                           v.tensor_copy(o, i), waits=[ev_ld], inc=s_dve)
            tr_step(cw2, 8, self.convw_j[j], ev_cp)
        p.emit("sync", lambda sy: sy.dma_start(
            self.routb, self.frb.rearrange("(e o) -> e o", o=1)),
            waits=[self.pe_now()], inc=sd_misc, inc_n=16)
        self.ev_bias_done = ev_copy1
        self.ev_misc_ld = (sd_misc, sd_misc.n)
        self.ev_p0_pe = self.pe_now()
        self.free[("ps_big",)] = ev_copy1
        self.free[("stg", 0)] = self.ev_p0_pe
        self.free[("stg", 1)] = self.ev_p0_pe

    def load_wt(self, half, kind, extra_waits=()):
        p = self.p
        sem = self.sems["sd_wa"] if half == 0 else self.sems["sd_wb"]
        dst_t = self.wtA if half == 0 else self.wtB
        waits = _later(self.free.get(("wt", half)), self.ev_cast_all, *extra_waits)
        ev = None
        if kind in ("aq", "ak", "av", "ao"):
            src = {"aq": self.aqw_b, "ak": self.akw_b,
                   "av": self.avw_b, "ao": self.aow_b}[kind]
            for ci in range(8):
                ev = p.emit("scalar",
                            lambda sc, d=dst_t[:, ci * 1024:(ci + 1) * 1024],
                            i=src[:, ci * 128:(ci + 1) * 128]:
                            sc.dma_start(d, i, transpose=True),
                            waits=waits if ci == 0 else (), inc=sem, inc_n=16)
        elif kind == "fq":
            ev = p.emit("sync", lambda sy,
                        d=dst_t[:, 0:8192].rearrange("p (ci o) -> p ci o", o=1024):
                        sy.dma_start(
                d, self.fqw_b.rearrange("(ci p) o -> p ci o", p=128)),
                waits=waits, inc=sem, inc_n=16)
        elif kind in ("fk", "fv"):
            src = self.fkw_b if kind == "fk" else self.fvw_b
            ev = p.emit("sync", lambda sy,
                        d=dst_t[:, 0:4096].rearrange("p (ci o) -> p ci o", o=512),
                        s=src: sy.dma_start(
                d, s.rearrange("(ci p) o -> p ci o", p=128)),
                waits=waits, inc=sem, inc_n=16)
        elif kind == "fo":
            ev = p.emit("sync", lambda sy,
                        d=self.foB.rearrange("p (kc o) -> p kc o", o=1024):
                        sy.dma_start(
                d, self.fow_b.rearrange("(kc p) o -> p kc o", p=64)),
                waits=waits, inc=sem, inc_n=16)
        return ev

    def ln_phase(self, s):
        p = self.p
        s_pe, s_act, s_dve = (self.sems[n] for n in ("s_pe", "s_act", "s_dve"))
        for ti in range(16):
            b = ti % 2
            sd_x = self.sems["sd_x0"] if b == 0 else self.sems["sd_x1"]
            xb = self.x_any[:, b * 1024:(b + 1) * 1024]
            ev_x = p.emit("sync", lambda sy, o=xb, ss=s, t=ti: sy.dma_start(
                o, self.x_in[ss, t * 128:(t + 1) * 128, :]),
                waits=_later(self.free.get(("x_any", b))), inc=sd_x, inc_n=16)
            nm = self.negm[:, ti:ti + 1]
            rs = self.rstd[:, ti:ti + 1]
            ev = p.emit("vector", lambda v, o=nm, i=xb:
                        v.reduce_sum(o, i, axis=AX, negate=True),
                        waits=[ev_x], inc=s_dve)
            ev = p.emit("vector", lambda v, o=nm:
                        v.tensor_scalar_mul(o, o, 1.0 / D), waits=[ev], inc=s_dve)
            ev = p.emit("scalar", lambda sc, i=xb, n=nm:
                        sc.activation(self.sq_scr, i, AF.Square, bias=n,
                                      accum_out=self.varsum),
                        waits=_later(ev, self.free.get(("out_st", 0))
                                     if ti == 0 else None,
                                     self.ev_consts if ti == 0 else None),
                        inc=s_act)
            ev = p.emit("scalar", lambda sc, r=rs:
                        sc.activation(r, self.varsum, AF.Sqrt, scale=1.0 / D,
                                      bias=1e-5),
                        waits=[ev], inc=s_act)
            ev = p.emit("vector", lambda v, r=rs: v.reciprocal(r, r),
                        waits=[ev], inc=s_dve)
            p.emit("vector", lambda v, t=ti: v.tensor_scalar_add(
                self.rstd1[:, t:t + 1], self.rstd[:, t:t + 1], 1.0),
                waits=[ev], inc=s_dve)
            p.emit("vector", lambda v, t=ti: v.tensor_mul(
                self.nmr[:, t:t + 1], self.negm[:, t:t + 1], self.rstd[:, t:t + 1]),
                inc=s_dve)
            xn_tok = self.attb[:, b * 1024:(b + 1) * 1024]
            wv = _later(ev, self.free.get(("attb", b)),
                        self.free.get(("attb_s", 0)) if (b == 0 and ti == 0) else None,
                        self.free.get(("attb_s", 1)) if (b == 0 and ti == 0) else None)
            ev = p.emit("vector", lambda v, o=xn_tok, i=xb, t=ti:
                        v.tensor_scalar(o, i, self.negm[:, t:t + 1],
                                        self.rstd[:, t:t + 1],
                                        op0=ALU.add, op1=ALU.mult),
                        waits=wv, inc=s_dve)
            self.free[("x_any", b)] = ev
            ho = b * 1024
            evt = None
            for k in range(8):
                wts = _later(ev, self.ev_p0_pe,
                             self.free.get(("ps_tr", b)) if k == 0 else None,
                             self.free.get(("xnT_w",)) if (ti == 0 and k == 0) else None)
                evt = p.emit("tensor", lambda t, kk=k, o=ho, i=xn_tok:
                             t.transpose(self.ps_tr[:, o + kk * 128:o + (kk + 1) * 128],
                                         i[:, kk * 128:(kk + 1) * 128], self.identB),
                             waits=wts, inc=(s_pe if k == 7 else None))
            self.free[("attb", b)] = evt
            dest = self.xnT.rearrange("p (c t) -> p c t", t=S)[:, :, ti * 128:(ti + 1) * 128]
            ev_e = p.emit("vector", lambda v, d=dest, o=ho:
                          v.tensor_copy(d, self.ps_tr[:, o:o + 1024]
                                        .rearrange("p (c t) -> p c t", t=128)),
                          waits=[evt], inc=s_dve)
            self.free[("ps_tr", b)] = ev_e
        self.ev_xnT = (s_dve, s_dve.n)

    def conv_phase(self, s):
        p = self.p
        s_pool = self.sems["s_pool"]
        s_dve = self.sems["s_dve"]
        tA = self.x_any[:, 0:1024]
        tB = self.x_any[:, 1024:2048]
        ev = None
        for ci in range(8):
            xn = self.xnT[:, ci * S:(ci + 1) * S]
            xd = self.xdT[:, ci * 1024:(ci + 1) * 1024]
            w0 = _later(self.ev_xnT, self.ev_bias_done,
                        self.free.get(("C",)) if ci == 0 else None,
                        self.free.get(("x_any", 0)) if ci == 0 else None,
                        self.free.get(("x_any", 1)) if ci == 0 else None)
            ev = p.emit("vector", lambda v, o=tB[:, 1:1024], i=xn[:, 1:2046:2], c=ci:
                        v.tensor_scalar_mul(o, i, self.convw_j[0][:, c:c + 1]),
                        waits=w0, inc=s_dve)
            for step in [
                    lambda v, o=tA[:, 1:1024], i=xn[:, 2:2047:2], c=ci:
                    v.scalar_tensor_tensor(o, i, self.convw_j[1][:, c:c + 1],
                                           tB[:, 1:1024], op0=ALU.mult, op1=ALU.add),
                    lambda v, o=tB[:, 1:1024], i=xn[:, 3:2048:2], c=ci:
                    v.scalar_tensor_tensor(o, i, self.convw_j[2][:, c:c + 1],
                                           tA[:, 1:1024], op0=ALU.mult, op1=ALU.add),
                    lambda v, o=tA[:, 0:1], i=xn[:, 0:1], c=ci:
                    v.tensor_scalar_mul(o, i, self.convw_j[1][:, c:c + 1]),
                    lambda v, o=tB[:, 0:1], i=xn[:, 1:2], c=ci:
                    v.scalar_tensor_tensor(o, i, self.convw_j[2][:, c:c + 1],
                                           tA[:, 0:1], op0=ALU.mult, op1=ALU.add),
                    lambda v, o=xd, c=ci:
                    v.tensor_scalar_add(o, tB[:, 0:1024], self.convbcol[:, c:c + 1])]:
                ev = p.emit("vector", step, waits=[ev], inc=s_dve)
        self.free[("x_any", 0)] = ev
        self.free[("x_any", 1)] = ev
        self.ev_xdT = ev

    def kv1_phase(self, s):
        p = self.p
        s_act, s_dve, s_pool = (self.sems[n] for n in ("s_act", "s_dve", "s_pool"))
        ev_k = self.ev_wtB
        ev_cfree = self.free.get(("C",))
        slot = 0
        for ii in range(8):
            for qc in range(2):
                ps = self.ps_a if slot % 2 == 0 else self.ps_b
                pn = "ps_a" if slot % 2 == 0 else "ps_b"
                pairs = [(self.wtB[:, ci * 1024 + ii * 128: ci * 1024 + ii * 128 + 128],
                          self.xdT[:, ci * 1024 + qc * 512: ci * 1024 + qc * 512 + 512])
                         for ci in range(8)]
                ev = self.acc_group(ps, pn, pairs, [ev_k, self.ev_xdT])
                ev_e = p.emit("scalar", lambda sc,
                              o=self.kT[:, ii * 1024 + qc * 512: ii * 1024 + qc * 512 + 512],
                              i=ps, c=ii:
                              sc.activation(o, i, AF.Identity,
                                            bias=self.kbcol[:, c:c + 1]),
                              waits=_later(ev, ev_cfree if slot == 0 else None),
                              inc=s_act)
                self.free[(pn,)] = ev_e
                slot += 1
        self.ev_kT = (s_act, s_act.n)
        self.free[("wt", 1)] = self.pe_now()

        ev_v = self.load_wt(1, "av")
        self.ev_wtB = ev_v
        ev_ones = p.emit("gpsimd", lambda g: g.memset(
            self.v1tok[:, 0:8320].rearrange("p (x d) -> p x d", d=65)[:, :, 64:65], 1.0),
            waits=_later(self.free.get(("vtok",)), self.free.get(("vtok2",))),
            inc=s_pool)
        for kti in range(8):
            for qc in range(2):
                ps = self.ps_a if slot % 2 == 0 else self.ps_b
                pn = "ps_a" if slot % 2 == 0 else "ps_b"
                pairs = [(self.xdT[:, ci * 1024 + kti * 128: ci * 1024 + kti * 128 + 128],
                          self.wtB[:, ci * 1024 + qc * 512: ci * 1024 + qc * 512 + 512])
                         for ci in range(8)]
                pairs.append((self.onesB, self.vb1[:, qc * 512:(qc + 1) * 512]))
                ev = self.acc_group(ps, pn, pairs, [ev_v])
                base = kti * 1040 + qc * 520
                dest = self.v1tok[:, base:base + 520].rearrange(
                    "p (h d) -> p h d", d=65)[:, :, 0:64]
                ev_e = p.emit("vector", lambda v, d=dest,
                              i=ps.rearrange("p (h d) -> p h d", d=64):
                              v.tensor_copy(d, i),
                              waits=_later(ev, ev_ones), inc=s_dve)
                self.free[(pn,)] = ev_e
                slot += 1
        self.ev_v1 = (s_dve, s_dve.n)
        self.free[("wt", 1)] = self.pe_now()

    def kv2_phase(self, s):
        p = self.p
        s_act, s_dve, s_pool = (self.sems[n] for n in ("s_act", "s_dve", "s_pool"))
        ev_k = self.ev_wtB
        ev_cfree = self.pe_now()
        slot = 0
        for hc in range(8):
            for qc in range(4):
                ps = self.ps_a if slot % 2 == 0 else self.ps_b
                pn = "ps_a" if slot % 2 == 0 else "ps_b"
                pairs = [(self.wtB[:, ci * 512 + hc * 64: ci * 512 + hc * 64 + 64],
                          self.xnT[:, ci * S + qc * 512: ci * S + qc * 512 + 512])
                         for ci in range(8)]
                ev = self.acc_group(ps, pn, pairs, [ev_k, self.ev_xnT], rows=64)
                ev_e = p.emit("scalar", lambda sc,
                              o=self.k2T[0:64, hc * 2048 + qc * 512:
                                         hc * 2048 + qc * 512 + 512],
                              i=ps[0:64, :], c=hc:
                              sc.activation(o, i, AF.Identity,
                                            bias=self.k2bcol[(c % 2) * 64:(c % 2) * 64 + 64,
                                                             c // 2:c // 2 + 1]),
                              waits=_later(ev, ev_cfree if slot == 0 else None),
                              inc=s_act)
                self.free[(pn,)] = ev_e
                slot += 1
        self.ev_k2 = (s_act, s_act.n)
        self.free[("wt", 1)] = self.pe_now()

        ev_v = self.load_wt(1, "fv")
        self.ev_wtB = ev_v
        ev_ones = p.emit("gpsimd", lambda g: g.memset(
            self.v2tok[:, 0:8448].rearrange("p (x d) -> p x d", d=33)[:, :, 32:33], 1.0),
            waits=_later(self.pe_now()), inc=s_pool)
        for kti in range(16):
            ps = self.ps_a if slot % 2 == 0 else self.ps_b
            pn = "ps_a" if slot % 2 == 0 else "ps_b"
            pairs = [(self.xnT[:, ci * S + kti * 128: ci * S + kti * 128 + 128],
                      self.wtB[:, ci * 512: ci * 512 + 512])
                     for ci in range(8)]
            pairs.append((self.onesB, self.v2b[:, 0:512]))
            ev = self.acc_group(ps, pn, pairs, [ev_v])
            base = kti * 528
            dest = self.v2tok[:, base:base + 528].rearrange(
                "p (h d) -> p h d", d=33)[:, :, 0:32]
            ev_e = p.emit("vector", lambda v, d=dest,
                          i=ps.rearrange("p (h d) -> p h d", d=32):
                          v.tensor_copy(d, i),
                          waits=_later(ev, ev_ones), inc=s_dve)
            self.free[(pn,)] = ev_e
            slot += 1
        self.ev_v2 = (s_dve, s_dve.n)
        self.free[("wt", 1)] = self.pe_now()

    def q_proj(self, s, qh, branch):
        p = self.p
        s_act = self.sems["s_act"]
        qfree = self.free.get(("qbuf",))
        slot = 0
        for ii in range(8):
            for qc in range(2):
                ps = self.ps_a if slot % 2 == 0 else self.ps_b
                pn = "ps_a" if slot % 2 == 0 else "ps_b"
                pairs = [(self.wtA[:, ci * 1024 + ii * 128: ci * 1024 + ii * 128 + 128],
                          self.xnT[:, ci * S + qh * 1024 + qc * 512:
                                   ci * S + qh * 1024 + qc * 512 + 512])
                         for ci in range(8)]
                ev = self.acc_group(ps, pn, pairs, [self.ev_wtA, self.ev_xnT])
                o0 = ii * 1024 + qc * 512
                if branch == 1:
                    ev_e = p.emit("scalar", lambda sc, o=self.qT[:, o0:o0 + 512],
                                  i=ps, c=ii:
                                  sc.activation(o, i, AF.Identity,
                                                bias=self.qbcol[:, c:c + 1]),
                                  waits=_later(ev, qfree if slot == 0 else None),
                                  inc=s_act)
                else:
                    p.emit("scalar", lambda sc, o=self.q2c[0:32, o0:o0 + 512],
                           i=ps[0:32, :], c=ii:
                           sc.activation(o, i, AF.Identity,
                                         bias=self.q2bcol[0:32, c:c + 1]),
                           waits=_later(ev, qfree if slot == 0 else None), inc=s_act)
                    ev_e = p.emit("scalar", lambda sc, o=self.q2c[32:64, o0:o0 + 512],
                                  i=ps[64:96, :], c=ii:
                                  sc.activation(o, i, AF.Identity,
                                                bias=self.q2bcol[64:96, c:c + 1]),
                                  inc=s_act)
                self.free[(pn,)] = ev_e
                slot += 1
        self.ev_q = (s_act, s_act.n)

    def attn_phase(self, s, qh, branch):
        p = self.p
        s_pe, s_act, s_dve = (self.sems[n] for n in ("s_pe", "s_act", "s_dve"))
        if branch == 1:
            n_kti, dv, sc = 8, 64, SC1
            kv_ready = _later(self.ev_kT, self.ev_v1, self.ev_q)
        else:
            n_kti, dv, sc = 16, 32, SC2
            kv_ready = _later(self.ev_k2, self.ev_v2, self.ev_q)
        oa_free = self.free.get(("oa",))
        first_w = _later(*(kv_ready + [self.free.get(("ps_a",)),
                                       self.free.get(("ps_b",)),
                                       self.free.get(("ps_c",)),
                                       self.free.get(("ps_d",))]))

        def slices(h, kti, qc):
            if branch == 1:
                lt = self.kT[(h % 2) * 64:(h % 2) * 64 + 64,
                             (h // 2) * 1024 + kti * 128:
                             (h // 2) * 1024 + kti * 128 + 128]
                rt = self.qT[(h % 2) * 64:(h % 2) * 64 + 64,
                             (h // 2) * 1024 + qc * 512:
                             (h // 2) * 1024 + qc * 512 + 512]
                vt = self.v1tok[:, kti * 1040 + h * 65: kti * 1040 + h * 65 + 65]
            else:
                lt = self.k2T[(h % 2) * 32:(h % 2) * 32 + 32,
                              (h // 2) * 2048 + kti * 128:
                              (h // 2) * 2048 + kti * 128 + 128]
                rt = self.q2c[(h % 2) * 32:(h % 2) * 32 + 32,
                              (h // 2) * 1024 + qc * 512:
                              (h // 2) * 1024 + qc * 512 + 512]
                vt = self.v2tok[:, kti * 528 + h * 33: kti * 528 + h * 33 + 33]
            return lt, rt, vt

        sstep = 0
        for h in range(H):
            ev_av_last = None
            for kp in range(n_kti // 2):
                for qc in range(2):
                    buf = sstep % 2
                    psS = self.ps_s0 if buf == 0 else self.ps_s1
                    pnS = ("psS", buf)
                    k0, k1 = 2 * kp, 2 * kp + 1
                    lt0, rt0, vt0 = slices(h, k0, qc)
                    lt1, rt1, vt1 = slices(h, k1, qc)
                    w = _later(self.free.get(pnS),
                               *( [first_w[i] for i in range(len(first_w))]
                                  if (h == 0 and kp == 0 and qc == 0) else []))
                    p.emit("tensor", lambda t, l=lt0, r=rt0, o=psS[:, 0:512]:
                           t.matmul(o, l, r, start=True, stop=True), waits=w)
                    ev_mm = p.emit("tensor", lambda t, l=lt1, r=rt1,
                                   o=psS[:, 512:1024]:
                                   t.matmul(o, l, r, start=True, stop=True),
                                   inc=s_pe)
                    ab = self.attb[:, buf * 1024:(buf + 1) * 1024]
                    ev_x = p.emit("scalar", lambda scl, o=ab, i=psS, f=sc:
                                  scl.activation(o, i, AF.Exp, scale=f),
                                  waits=_later(ev_mm,
                                               self.free.get(("attb4", buf)),
                                               self.free.get(("attb", buf))
                                               if (h == 0 and kp == 0 and qc == 0)
                                               else None),
                                  inc=s_act)
                    self.free[pnS] = ev_x
                    po = self.ps_big[0:dv + 1, qc * 512: qc * 512 + 512]
                    okw = [self.free.get(("ps_big",))] if (kp == 0 and qc == 0) else []
                    p.emit("tensor", lambda t, l=vt0, r=ab[:, 0:512], o=po,
                           st=(kp == 0):
                           t.matmul(o, l, r, start=st, stop=False),
                           waits=_later(ev_x, *okw))
                    ev_av = p.emit("tensor", lambda t, l=vt1, r=ab[:, 512:1024],
                                   o=po, sp=(kp == n_kti // 2 - 1):
                                   t.matmul(o, l, r, start=False, stop=sp),
                                   inc=s_pe)
                    self.free[("attb4", buf)] = ev_av
                    ev_av_last = ev_av
                    sstep += 1
            srcs = [self.ps_big[0:dv + 1, 0:512], self.ps_big[0:dv + 1, 512:1024]]
            bcs = [self.ps_big[0:dv, 0:512], self.ps_big[0:dv, 512:1024]]
            ev_ev = None
            for qc in range(2):
                ev_ev = p.emit("vector", lambda v, i=srcs[qc],
                               o=self.osb[0:dv + 1, qc * 512:qc * 512 + 512]:
                               v.tensor_copy(o, i),
                               waits=_later(ev_av_last, self.free.get(("osb",))
                                            if qc == 0 else None),
                               inc=s_dve)
            ev_r = p.emit("vector", lambda v, r=dv:
                          v.reciprocal(self.recip[0:1, 0:1024],
                                       self.osb[r:r + 1, 0:1024]),
                          waits=[ev_ev], inc=s_dve)
            ev_bc = None
            for qc in range(2):
                ev_bc = p.emit("tensor", lambda t, o=bcs[qc], q=qc, r=dv:
                               t.matmul(o, self.onesF[0:1, 0:r],
                                        self.recip[0:1, q * 512:q * 512 + 512],
                                        start=True, stop=True),
                               waits=[ev_r], inc=s_pe)
            ev_m = None
            for qc in range(2):
                if branch == 1:
                    dst = self.oT[(h % 2) * 64:(h % 2) * 64 + 64,
                                  (h // 2) * 1024 + qc * 512:
                                  (h // 2) * 1024 + qc * 512 + 512]
                else:
                    dst = self.aoT[(h % 2) * 32:(h % 2) * 32 + 32,
                                   (h // 2) * 1024 + qc * 512:
                                   (h // 2) * 1024 + qc * 512 + 512]
                ev_m = p.emit("vector", lambda v, d=dst,
                              a=self.osb[0:dv, qc * 512:qc * 512 + 512], b=bcs[qc]:
                              v.tensor_mul(d, a, b),
                              waits=_later(ev_bc, oa_free
                                           if (h == 0 and qc == 0) else None),
                              inc=s_dve)
            self.free[("osb",)] = ev_m
            self.free[("ps_big",)] = ev_m
        end_ev = (s_act, s_act.n)
        for k in (("ps_a",), ("ps_b",), ("ps_c",), ("ps_d",)):
            self.free[k] = _later(self.free.get(k), end_ev)[0] \
                if self.free.get(k) else end_ev
        self.free[("ps_a",)] = end_ev
        self.free[("ps_b",)] = end_ev
        self.free[("ps_c",)] = end_ev
        self.free[("ps_d",)] = end_ev
        if branch == 1:
            self.ev_oT = (s_dve, s_dve.n)
        else:
            self.ev_aoT = (s_dve, s_dve.n)

    def out_proj(self, s, qh):
        p = self.p
        s_act = self.sems["s_act"]
        scr_rd = _later(self.free.get(("outscr_rd0",)), self.free.get(("outscr_rd1",)))
        for dc in range(8):
            half = dc % 2
            sd_sp = self.sems["sd_sp0"] if half == 0 else self.sems["sd_sp1"]
            ev_e = None
            for qc in range(2):
                pn = "ps_a" if qc == 0 else "ps_b"
                ps = self.ps_a if qc == 0 else self.ps_b
                pairs = [(self.wtB[:, ci * 1024 + dc * 128: ci * 1024 + dc * 128 + 128],
                          self.oT[:, ci * 1024 + qc * 512: ci * 1024 + qc * 512 + 512])
                         for ci in range(8)]
                ev = self.acc_group(ps, pn, pairs, [self.ev_wtB, self.ev_oT])
                ev_e = p.emit("scalar", lambda sc,
                              o=self.stg[:, half * 1024 + qc * 512:
                                         half * 1024 + qc * 512 + 512],
                              i=ps, c=dc:
                              sc.activation(o, i, AF.Identity,
                                            bias=self.outbcol[:, c:c + 1]),
                              waits=_later(ev, self.free.get(("stg", half))
                                           if qc == 0 else None),
                              inc=s_act)
                self.free[(pn,)] = ev_e
            ev_sp = p.emit("sync", lambda sy, d=dc, q=qh:
                           sy.dma_start(self.outscr[d * 128:(d + 1) * 128,
                                                    q * 1024:(q + 1) * 1024],
                                        self.stg[:, (d % 2) * 1024:(d % 2) * 1024 + 1024]),
                           waits=_later(ev_e, *(scr_rd if dc < 2 else [])),
                           inc=sd_sp, inc_n=16)
            self.free[("stg", half)] = ev_sp

    def mla_proj(self, s, qh):
        p = self.p
        s_act = self.sems["s_act"]
        slot = 0
        for dc in range(8):
            for qc in range(2):
                ps = self.ps_a if slot % 2 == 0 else self.ps_b
                pn = "ps_a" if slot % 2 == 0 else "ps_b"
                pairs = [(self.foB[0:64, kc * 1024 + dc * 128: kc * 1024 + dc * 128 + 128],
                          self.aoT[0:64, kc * 1024 + qc * 512: kc * 1024 + qc * 512 + 512])
                         for kc in range(8)]
                ev = self.acc_group(ps, pn, pairs, [self.ev_foB, self.ev_aoT])
                ev_e = p.emit("scalar", lambda sc,
                              o=self.mlaT[:, dc * S + qh * 1024 + qc * 512:
                                          dc * S + qh * 1024 + qc * 512 + 512],
                              i=ps, c=dc:
                              sc.activation(o, i, AF.Identity,
                                            bias=self.obcol[:, c:c + 1]),
                              waits=[ev], inc=s_act)
                self.free[(pn,)] = ev_e
                slot += 1
        self.ev_mla = (s_act, s_act.n)

    def router_phase(self, s, qh):
        p = self.p
        s_pe, s_act, s_dve = (self.sems[n] for n in ("s_pe", "s_act", "s_dve"))
        ev = None
        for qc in range(2):
            pairs = [(self.routw[:, ci * 8: ci * 8 + 8],
                      self.mlaT[:, ci * S + qh * 1024 + qc * 512:
                                ci * S + qh * 1024 + qc * 512 + 512])
                     for ci in range(8)]
            ev = self.acc_group(self.ps_big, "ps_big" if qc == 0 else "__skip",
                                pairs,
                                [self.ev_mla, self.ev_misc_ld] if qc == 0 else [],
                                rows=8, cols=512, col0=qc * 512)
        ev_l = p.emit("scalar", lambda sc:
                      sc.activation(self.probsL[0:8, 0:1024],
                                    self.ps_big[0:8, 0:1024],
                                    AF.Identity, bias=self.routb[0:8, 0:1]),
                      waits=[ev], inc=s_act)
        self.free[("ps_big",)] = ev_l
        for ti in range(8):
            po = (ti % 2) * 512
            ev_tr = p.emit("tensor", lambda t, o=po,
                           i=self.probsL[0:8, ti * 128:(ti + 1) * 128]:
                           t.transpose(self.ps_big[0:128, o:o + 8], i,
                                       self.identF[0:8, 0:8]),
                           waits=_later(ev_l, self.free.get(("rps", ti % 2)),
                                        self.free.get(("ps_big",)) if ti < 2 else None),
                           inc=s_pe)
            pl = self.ps_big[0:128, po:po + 8]
            ev1 = p.emit("vector", lambda v, i=pl: v.reduce_max(
                self.t0, i, axis=AX, negate=True), waits=[ev_tr], inc=s_dve)
            ev2 = p.emit("scalar", lambda sc, i=pl: sc.activation(
                self.combF, i, AF.Exp, bias=self.t0), waits=[ev1], inc=s_act)
            self.free[("rps", ti % 2)] = ev2
            ev = ev2
            for fn in [
                    lambda v: v.reduce_sum(self.t1, self.combF, axis=AX),
                    lambda v: v.reciprocal(self.t1, self.t1),
                    lambda v: v.tensor_scalar_mul(self.combF, self.combF,
                                                  self.t1[:, 0:1]),
                    lambda v: v.reduce_max(self.t0, self.combF, axis=AX),
                    lambda v: v.tensor_single_scalar(self.eq8, self.combF,
                                                     self.t0[:, 0:1], ALU.is_equal),
                    lambda v: v.scalar_tensor_tensor(self.eq8, self.eq8, -1e30,
                                                     self.combF, op0=ALU.mult,
                                                     op1=ALU.add),
                    lambda v: v.reduce_max(self.t2, self.eq8, axis=AX),
                    lambda v: v.tensor_single_scalar(self.eq8, self.combF,
                                                     self.t2[:, 0:1], ALU.is_ge),
                    lambda v: v.tensor_add(self.t3, self.t0, self.t2),
                    lambda v: v.tensor_scalar_add(self.t3, self.t3, 1e-8),
                    lambda v: v.reciprocal(self.t3, self.t3),
                    lambda v: v.tensor_scalar_mul(self.t3, self.t3, 0.3),
                    lambda v: v.tensor_mul(self.combF, self.combF, self.eq8),
                    lambda v: v.tensor_scalar_mul(self.combF, self.combF,
                                                  self.t3[:, 0:1])]:
                ev = p.emit("vector", fn, waits=[ev], inc=s_dve)
            ev4 = p.emit("vector", lambda v: v.tensor_copy(self.combB, self.combF),
                         waits=_later(ev, self.free.get(("combB",))), inc=s_dve)
            ev_t2 = p.emit("tensor", lambda t, o=(ti % 2) * 1024:
                           t.transpose(self.ps_tr[0:8, o:o + 128], self.combB,
                                       self.identB),
                           waits=_later(ev4, self.free.get(("ctr", ti % 2)),
                                        self.free.get(("ps_tr", ti % 2))
                                        if ti < 2 else None),
                           inc=s_pe)
            self.free[("combB",)] = ev_t2
            ev5 = p.emit("vector", lambda v, o=(ti % 2) * 1024, t=ti:
                         v.tensor_copy(self.combT[0:8, qh * 1024 + t * 128:
                                                  qh * 1024 + (t + 1) * 128],
                                       self.ps_tr[0:8, o:o + 128]),
                         waits=[ev_t2], inc=s_dve)
            self.free[("ctr", ti % 2)] = ev5
            self.free[("ps_tr", ti % 2)] = ev5
        self.ev_comb = (s_dve, s_dve.n)
        self.free[("ps_big",)] = (s_act, s_act.n)

    def moe_phase(self, s):
        p = self.p
        s_pe, s_act, s_dve, s_pool = (self.sems[n] for n in
                                      ("s_pe", "s_act", "s_dve", "s_pool"))
        sd_w1, sd_w2 = self.sems["sd_w1"], self.sems["sd_w2"]
        ev_sp = [(self.sems["sd_sp0"], self.sems["sd_sp0"].n),
                 (self.sems["sd_sp1"], self.sems["sd_sp1"].n)]
        ev_fin_prev = _later(self.free.get(("outscr_rd0",)),
                             self.free.get(("outscr_rd1",)))
        ev_b2_done = self.pe_now()

        e_list = [int(t) for t in os.environ.get("KDBG_ELIST", "").split(",")] \
            if os.environ.get("KDBG_ELIST") else list(range(E))
        qc_list = [int(t) for t in os.environ.get("KDBG_QCLIST", "").split(",")] \
            if os.environ.get("KDBG_QCLIST") else list(range(4))
        for ei, e in enumerate(e_list):
            ev_w1 = self.ev_w1[e]
            ev_w2 = self.ev_w2[e]
            ev_bc_e = None
            for half in range(2):
                evb = None
                for qc in range(2):
                    evb = p.emit("tensor", lambda t, hh=half, q=qc, ee=e:
                                 t.matmul(self.ps_big[0:128, q * 512:q * 512 + 512],
                                          self.selc[0:8, ee * 128:(ee + 1) * 128],
                                          self.combT[0:8, hh * 1024 + q * 512:
                                                     hh * 1024 + q * 512 + 512],
                                          start=True, stop=True),
                                 waits=_later(self.ev_comb,
                                              self.free.get(("ps_big",))
                                              if qc == 0 else None,
                                              self.ev_small if (e == 0 and half == 0
                                                                and qc == 0) else None),
                                 inc=s_pe)
                ev_bc_e = p.emit("vector", lambda v, hh=half:
                                 v.tensor_copy(self.bcast[:, hh * 1024:(hh + 1) * 1024],
                                               self.ps_big[0:128, 0:1024]),
                                 waits=_later(evb, ev_b2_done
                                              if (e == 0 and half == 0) else None),
                                 inc=s_dve)
                self.free[("ps_big",)] = ev_bc_e
            for qc in qc_list:
                ev_h2_prev = self.free.get(("h1g",))
                ev_scale = {}
                for fi in range(16):
                    sl = (fi + int(os.environ.get("KDBG_SLSWAP", "0"))) % 2
                    ps = self.ps_a if sl == 0 else self.ps_b
                    pn = "ps_a" if sl == 0 else "ps_b"
                    pairs = [(self.w1s[:, ci * 2048 + fi * 128: ci * 2048 + fi * 128 + 128],
                              self.mlaT[:, ci * S + qc * 512: ci * S + qc * 512 + 512])
                             for ci in range(8)]
                    ev = self.acc_group(ps, pn, pairs,
                                        [ev_w1, self.ev_mla]
                                        if (fi == 0 and qc == 0) else [])
                    if os.environ.get("KDBG_H1PS") and fi == 0:
                        p.emit("vector", lambda v, i=ps:
                               v.tensor_copy(self.out_st[:, 0:512], i),
                               waits=[ev], inc=s_dve)
                        self._h1ps_done = True
                        return
                    hslice = self.h1g[:, fi * 512:(fi + 1) * 512]
                    gb = self.attb[:, (fi % 2) * 512:(fi % 2) * 512 + 512]
                    ev_g = p.emit("scalar", lambda sc, o=gb, i=ps, f=fi, ee=e:
                                  sc.activation(o, i, AF.Gelu,
                                                bias=self.b1col[:, f * 8 + ee:
                                                                f * 8 + ee + 1]),
                                  waits=_later(ev, ev_h2_prev,
                                               self.free.get(("gb", fi % 2))),
                                  inc=s_act)
                    self.free[(pn,)] = ev_g
                    ev_scale[fi] = p.emit("vector", lambda v, o=hslice, i=gb, q=qc:
                                          v.tensor_mul(o, i,
                                                       self.bcast[:, q * 512:
                                                                  q * 512 + 512]),
                                          waits=_later(ev_g, ev_bc_e), inc=s_dve)
                    self.free[("gb", fi % 2)] = ev_scale[fi]
                ev_h2_last = None
                for dc in range(8):
                    sl = dc % 2
                    psh = self.ps_c if sl == 0 else self.ps_d
                    pnh = "ps_c" if sl == 0 else "ps_d"
                    ev = None
                    for fi in range(16):
                        w = []
                        if fi == 0:
                            w = _later(self.free.get((pnh,)),
                                       ev_w2 if (dc == 0 and qc == qc_list[0])
                                       else None)
                        if dc == 0:
                            w = _later(*(w + [ev_scale[fi]]))
                        ev = p.emit("tensor", lambda t,
                                    l=self.w2s[:, fi * 1024 + dc * 128:
                                               fi * 1024 + dc * 128 + 128],
                                    r=self.h1g[:, fi * 512:(fi + 1) * 512],
                                    st=(fi == 0),
                                    sp=(fi == 15 and (ei != 0 or
                                        bool(os.environ.get("KDBG_NOB")))), o=psh:
                                    t.matmul(o, l, r, start=st, stop=sp),
                                    waits=w,
                                    inc=(s_pe if (fi == 15 and (ei != 0 or
                                         bool(os.environ.get("KDBG_NOB"))))
                                         else None))
                    if ei == 0 and not os.environ.get("KDBG_NOB"):
                        ev = p.emit("tensor", lambda t, o=psh, d=dc, q=qc:
                                    t.matmul(o, self.b2s[0:8, d * 128:(d + 1) * 128],
                                             self.combT[0:8, q * 512:(q + 1) * 512],
                                             start=False, stop=True),
                                    inc=s_pe)
                    elif ei == 0:
                        pass
                    ev_h2_last = ev
                    half = (dc // 2) % 2
                    sd_ac = self.sems["sd_ac0"] if half == 0 else self.sems["sd_ac1"]
                    ev_e = p.emit("vector", lambda v,
                                  o=self.stg[:, half * 1024 + (dc % 2) * 512:
                                             half * 1024 + (dc % 2) * 512 + 512],
                                  i=psh[0:128, 0:512]:
                                  v.tensor_copy(o, i),
                                  waits=_later(ev,
                                               self.free.get(("stg2", half))
                                               if dc % 2 == 0 else None,
                                               self.free.get(("stg", half))
                                               if (e == 0 and qc == 0 and dc < 2)
                                               else None),
                                  inc=s_dve)
                    self.free[(pnh,)] = ev_e
                    if dc % 2 == 1:
                        dst = self.outscr.rearrange("(dc p) t -> p dc t", p=128)[
                            :, dc - 1:dc + 1, qc * 512:qc * 512 + 512]
                        src = self.stg[:, half * 1024: half * 1024 + 1024].rearrange(
                            "p (a t) -> p a t", t=512)
                        if ei == 0:
                            w = _later(ev_e, *ev_sp, ev_fin_prev[0]
                                       if ev_fin_prev else None,
                                       ev_fin_prev[1] if ev_fin_prev and
                                       len(ev_fin_prev) > 1 else None)
                        else:
                            w = [ev_e]
                        ev_ac = p.emit("gpsimd", lambda g, d=dst, i=src:
                                       g.dma_start(d, i, accum_op=ALU.add),
                                       waits=w, inc=sd_ac, inc_n=16)
                        self.free[("stg2", half)] = ev_ac
                self.free[("h1g",)] = ev_h2_last
            if ei + 1 < len(e_list):
                nxt = e_list[ei + 1]
                self.ev_w1[nxt] = p.emit(
                    "sync", lambda sy, ee=nxt:
                    sy.dma_start(self.w1s.rearrange("p (ci f) -> p ci f", f=2048),
                                 self.w1_b[ee].rearrange(
                        "(ci p) f -> p ci f", p=128)),
                    waits=_later(self.pe_now(), self.ev_cast[f"w1_{nxt}"]),
                    inc=sd_w1, inc_n=16)
                self.ev_w2[nxt] = p.emit(
                    "sync", lambda sy, ee=nxt:
                    sy.dma_start(self.w2s.rearrange("p (fi d) -> p fi d", d=1024),
                                 self.w2_b[ee].rearrange(
                        "(fi p) d -> p fi d", p=128)),
                    waits=_later(self.pe_now(), self.ev_cast[f"w2_{nxt}"]),
                    inc=sd_w2, inc_n=16)
        self.ev_acc = [(self.sems["sd_ac0"], self.sems["sd_ac0"].n),
                       (self.sems["sd_ac1"], self.sems["sd_ac1"].n)]
        self.free[("stg", 0)] = self.free.get(("stg2", 0))
        self.free[("stg", 1)] = self.free.get(("stg2", 1))
        pe_end = self.pe_now()
        self.free[("wtb",)] = pe_end
        self.free[("C",)] = pe_end
        self.free[("qbuf",)] = pe_end
        self.free[("xnT_w",)] = pe_end
        self.free[("vtok",)] = pe_end
        self.free[("vtok2",)] = (s_pool, s_pool.n)

    def final_phase(self, s):
        p = self.p
        s_pe, s_dve = self.sems["s_pe"], self.sems["s_dve"]
        for ti in range(16):
            b = ti % 2
            sd_f = self.sems["sd_f0"] if b == 0 else self.sems["sd_f1"]
            sd_x = self.sems["sd_x0"] if b == 0 else self.sems["sd_x1"]
            fin = self.fin_in[:, b * 1024:(b + 1) * 1024]
            src = self.outscr.rearrange("(dc p) (ti t) -> p ti dc t",
                                        p=128, t=128)[:, ti]
            ev_f = p.emit("sync", lambda sy, o=fin, i=src:
                          sy.dma_start(o, i),
                          waits=_later(*self.ev_acc, self.free.get(("fin", b))),
                          inc=sd_f, inc_n=16)
            xb = self.x_any[:, b * 1024:(b + 1) * 1024]
            ev_x = p.emit("sync", lambda sy, o=xb, ss=s, t=ti: sy.dma_start(
                o, self.x_in[ss, t * 128:(t + 1) * 128, :]),
                waits=_later(self.free.get(("x_any", b))), inc=sd_x, inc_n=16)
            evt = None
            for dc in range(8):
                if b == 0:
                    po = self.ps_big[:, dc * 128:(dc + 1) * 128]
                    pkeys = [("ps_big",)]
                else:
                    pt = self.ps_c if dc < 4 else self.ps_d
                    po = pt[:, (dc % 4) * 128:(dc % 4) * 128 + 128]
                    pkeys = [("ps_c",), ("ps_d",)]
                w = _later(ev_f,
                           *([self.free.get(k) for k in pkeys] if dc == 0 else []))
                evt = p.emit("tensor", lambda t, o=po,
                             i=fin[:, dc * 128:(dc + 1) * 128]:
                             t.transpose(o, i, self.identF),
                             waits=w, inc=(s_pe if dc == 7 else None))
            self.free[("fin", b)] = evt
            ev_o = None
            for hc in range(2):
                xp = self.out_st[:, b * 1024 + hc * 512: b * 1024 + hc * 512 + 512]
                ev1 = p.emit("vector", lambda v, o=xp,
                             i=xb[:, hc * 512:hc * 512 + 512], t=ti:
                             v.tensor_scalar(o, i, self.rstd1[:, t:t + 1],
                                             self.nmr[:, t:t + 1],
                                             op0=ALU.mult, op1=ALU.add),
                             waits=_later(ev_x, self.free.get(("out_st", b))
                                          if hc == 0 else None),
                             inc=s_dve)
                if b == 0:
                    pa = self.ps_big[:, hc * 512: hc * 512 + 512]
                else:
                    pa = (self.ps_c if hc == 0 else self.ps_d)[:, 0:512]
                ev_o = p.emit("vector", lambda v, o=xp, i=pa:
                              v.tensor_add(o, o, i),
                              waits=_later(evt, ev1), inc=s_dve)
            if b == 0:
                self.free[("ps_big",)] = ev_o
            else:
                self.free[("ps_c",)] = ev_o
                self.free[("ps_d",)] = ev_o
            self.free[("x_any", b)] = ev_o
            sd_st = self.sems["sd_st0"] if b == 0 else self.sems["sd_st1"]
            ev_st = p.emit("sync", lambda sy, ss=s, t=ti,
                           o=self.out_st[:, (ti % 2) * 1024:(ti % 2) * 1024 + 1024]:
                           sy.dma_start(self.y_out[ss, t * 128:(t + 1) * 128, :], o),
                           waits=[ev_o], inc=sd_st, inc_n=16)
            self.free[("out_st", b)] = ev_st
        self.free[("outscr_rd0",)] = (self.sems["sd_f0"], self.sems["sd_f0"].n)
        self.free[("outscr_rd1",)] = (self.sems["sd_f1"], self.sems["sd_f1"].n)
        self.free[("stg", 0)] = self.pe_now()
        self.free[("stg", 1)] = self.pe_now()

    def _dbg_dump(self, dumps):
        p = self.p
        sd_st = self.sems["sd_st0"]
        waits = []
        for nm in ("s_pe", "s_act", "s_dve", "s_pool"):
            waits.append((self.sems[nm], self.sems[nm].n))
        for nm in ("sd_wa", "sd_wb", "sd_w1", "sd_w2", "sd_x0", "sd_x1",
                   "sd_f0", "sd_f1", "sd_sp0", "sd_sp1", "sd_st0", "sd_st1",
                   "sd_ac0", "sd_ac1", "sd_g", "sd_misc"):
            waits.append((self.sems[nm], self.sems[nm].n))
        first = True
        for nm, ap, shape, dt in dumps:
            out = self.nc.dram_tensor("dbg_" + nm, shape, dt,
                                      kind="ExternalOutput").ap()
            p.emit("sync", lambda sy, o=out, i=ap: sy.dma_start(o, i),
                   waits=waits if first else (), inc=sd_st, inc_n=16)
            first = False
        p.emit("sync", None, waits=[(sd_st, sd_st.n)])

    def build_debug(self, stage):
        self.setup()
        p = self.p
        self.phase0()
        self.ev_w1 = {}
        self.ev_w2 = {}
        s = 0
        dumps = []
        while True:
            if stage == "bias":
                dumps = [("biasc", self.biasc, [128, 224], F32),
                         ("braw", self.braw, [8, 2048], F32)]
                break
            self.ev_wtA = self.load_wt(0, "aq",
                                       extra_waits=[self.free.get(("wtb",))])
            self.ev_wtB = self.load_wt(1, "ak",
                                       extra_waits=[self.free.get(("wtb",))])
            self.ln_phase(s)
            if stage == "ln":
                dumps = [("xnT", self.xnT, [128, 8 * S], BF16),
                         ("negm", self.negm, [128, 16], F32),
                         ("rstd", self.rstd, [128, 16], F32)]
                break
            self.conv_phase(s)
            self.kv1_phase(s)
            if stage == "kv1":
                dumps = [("xdT", self.xdT, [128, 8192], BF16),
                         ("kT", self.kT, [128, 8192], BF16),
                         ("v1tok", self.v1tok, [128, 8320], BF16)]
                break
            self.ev_wtB = self.load_wt(1, "ao")
            self.q_proj(s, 0, 1)
            if stage == "q1":
                dumps = [("qT", self.qT, [128, 8192], BF16)]
                break
            self.attn_phase(s, 0, 1)
            if stage == "att1":
                dumps = [("oT", self.oT, [128, 8192], BF16),
                         ("osb", self.osb, [128, 1024], F32)]
                break
            self.free[("qbuf",)] = self.pe_now()
            self.out_proj(s, 0)
            self.free[("oa",)] = self.pe_now()
            if stage == "outp":
                dumps = [("oscr", self.outscr, [D, S], F32)]
                break
            self.q_proj(s, 1, 1)
            self.attn_phase(s, 1, 1)
            self.free[("qbuf",)] = self.pe_now()
            self.out_proj(s, 1)
            self.free[("oa",)] = self.pe_now()
            self.ev_wtA = self.load_wt(0, "fq", extra_waits=[self.pe_now()])
            self.ev_wtB = self.load_wt(1, "fk", extra_waits=[self.pe_now()])
            self.kv2_phase(s)
            if stage == "kv2":
                dumps = [("k2T", self.k2T, [64, 16384], BF16),
                         ("v2tok", self.v2tok, [128, 8448], BF16)]
                break
            self.ev_foB = self.load_wt(1, "fo")
            self.q_proj(s, 0, 2)
            if stage == "q2":
                dumps = [("q2c", self.q2c, [64, 8192], BF16)]
                break
            self.attn_phase(s, 0, 2)
            if stage == "att2":
                dumps = [("aoT", self.aoT, [64, 8192], BF16)]
                break
            self.free[("qbuf",)] = self.pe_now()
            self.mla_proj(s, 0)
            self.free[("oa",)] = self.pe_now()
            if stage == "mla":
                dumps = [("mlaT", self.mlaT, [128, 8 * S], BF16)]
                break
            self.router_phase(s, 0)
            if stage == "rout":
                dumps = [("combT", self.combT, [8, S], BF16),
                         ("probsL", self.probsL, [8, 1024], F32),
                         ("combF", self.combF, [128, 8], F32),
                         ("eq8", self.eq8, [128, 8], F32),
                         ("stats", self.stats, [128, 80], F32),
                         ("combB", self.combB, [128, 8], BF16)]
                break
            self.q_proj(s, 1, 2)
            self.attn_phase(s, 1, 2)
            self.free[("qbuf",)] = self.pe_now()
            self.mla_proj(s, 1)
            self.free[("oa",)] = self.pe_now()
            self.router_phase(s, 1)
            e0 = int(os.environ.get("KDBG_ELIST", "0").split(",")[0])
            self.ev_w1[e0] = p.emit("sync", lambda sy, ee=e0: sy.dma_start(
                self.w1s.rearrange("p (ci f) -> p ci f", f=2048),
                self.w1_b[ee].rearrange("(ci p) f -> p ci f", p=128)),
                waits=_later(self.ev_cast[f"w1_{e0}"], self.pe_now()),
                inc=self.sems["sd_w1"], inc_n=16)
            self.ev_w2[e0] = p.emit("sync", lambda sy, ee=e0: sy.dma_start(
                self.w2s.rearrange("p (fi d) -> p fi d", d=1024),
                self.w2_b[ee].rearrange("(fi p) d -> p fi d", p=128)),
                waits=_later(self.ev_cast[f"w2_{e0}"], self.pe_now()),
                inc=self.sems["sd_w2"], inc_n=16)
            self.moe_phase(s)
            if stage == "moe" and os.environ.get("KDBG_H1PS"):
                dumps = [("h1ps", self.out_st, [128, 2048], F32),
                         ("mlaT", self.mlaT, [128, 8 * S], BF16),
                         ("w1s", self.w1s, [128, 16384], BF16)]
                break
            if stage == "moe":
                dumps = [("oscr", self.outscr, [D, S], F32),
                         ("combT", self.combT, [8, S], BF16),
                         ("mlaT", self.mlaT, [128, 8 * S], BF16),
                         ("h1g", self.h1g, [128, 8192], BF16),
                         ("bcast", self.bcast, [128, 2048], BF16),
                         ("w1s", self.w1s, [128, 16384], BF16),
                         ("w2s", self.w2s, [128, 16384], BF16),
                         ("b2s", self.b2s, [8, D], BF16),
                         ("attb", self.attb, [128, 2048], BF16)]
                break
            self.final_phase(s)
            dumps = [("y0", self.y_out[0], [S, D], F32)]
            break
        self._dbg_dump(dumps)
        with self.nc.Block() as block:
            block.tensor(lambda t: self.p.run("tensor", t))
            block.vector(lambda v: self.p.run("vector", v))
            block.scalar(lambda sc: self.p.run("scalar", sc))
            block.gpsimd(lambda g: self.p.run("gpsimd", g))
            block.sync(lambda sy: self.p.run("sync", sy))
        return self.nc

    def build(self):
        self.setup()
        p = self.p
        self.phase0()
        self.ev_w1 = {}
        self.ev_w2 = {}
        for s in range(NSEQ):
            self.ev_wtA = self.load_wt(0, "aq",
                                       extra_waits=[self.free.get(("wtb",))])
            self.ev_wtB = self.load_wt(1, "ak",
                                       extra_waits=[self.free.get(("wtb",))])
            self.ln_phase(s)
            self.conv_phase(s)
            self.kv1_phase(s)
            self.ev_wtB = self.load_wt(1, "ao")
            for qh in range(2):
                self.q_proj(s, qh, 1)
                self.attn_phase(s, qh, 1)
                self.free[("qbuf",)] = self.pe_now()
                self.out_proj(s, qh)
                self.free[("oa",)] = self.pe_now()
            self.ev_wtA = self.load_wt(0, "fq", extra_waits=[self.pe_now()])
            self.ev_wtB = self.load_wt(1, "fk", extra_waits=[self.pe_now()])
            self.kv2_phase(s)
            self.ev_foB = self.load_wt(1, "fo")
            for qh in range(2):
                self.q_proj(s, qh, 2)
                self.attn_phase(s, qh, 2)
                self.free[("qbuf",)] = self.pe_now()
                self.mla_proj(s, qh)
                self.free[("oa",)] = self.pe_now()
                self.router_phase(s, qh)
            e0 = int(os.environ.get("KDBG_ELIST", "0").split(",")[0])
            self.ev_w1[e0] = p.emit("sync", lambda sy, ee=e0: sy.dma_start(
                self.w1s.rearrange("p (ci f) -> p ci f", f=2048),
                self.w1_b[ee].rearrange("(ci p) f -> p ci f", p=128)),
                waits=_later(self.ev_cast[f"w1_{e0}"], self.pe_now()),
                inc=self.sems["sd_w1"], inc_n=16)
            self.ev_w2[e0] = p.emit("sync", lambda sy, ee=e0: sy.dma_start(
                self.w2s.rearrange("p (fi d) -> p fi d", d=1024),
                self.w2_b[ee].rearrange("(fi p) d -> p fi d", p=128)),
                waits=_later(self.ev_cast[f"w2_{e0}"], self.pe_now()),
                inc=self.sems["sd_w2"], inc_n=16)
            self.moe_phase(s)
            self.final_phase(s)
        p.emit("sync", None, waits=[(self.sems["sd_st0"], self.sems["sd_st0"].n),
                                    (self.sems["sd_st1"], self.sems["sd_st1"].n)])

        with self.nc.Block() as block:
            block.tensor(lambda t: self.p.run("tensor", t))
            block.vector(lambda v: self.p.run("vector", v))
            block.scalar(lambda sc: self.p.run("scalar", sc))
            block.gpsimd(lambda g: self.p.run("gpsimd", g))
            block.sync(lambda sy: self.p.run("sync", sy))
        return self.nc


_built = None


def _get_built():
    global _built
    if _built is None:
        _built = Build().build()
    return _built


def kernel(**inputs):
    nc = _get_built()
    x = np.ascontiguousarray(inputs["L0_attn_x"])
    shards = np.split(x, 8, axis=0)
    weights = {k: np.ascontiguousarray(v) for k, v in inputs.items()
               if k != "L0_attn_x"}
    in_maps = []
    for c in range(8):
        m = {"x": shards[c]}
        m.update(weights)
        in_maps.append(m)
    res = run_bass_kernel_spmd(nc, in_maps, core_ids=list(range(8)))
    return np.concatenate([r["y"] for r in res.results], axis=0)


# version 05 (final)
# speedup vs baseline: 1.0944x; 1.0046x over previous
import math
import os
import numpy as np
from contextlib import ExitStack

import concourse.bass as bass
import concourse.mybir as mybir
from concourse.bass_utils import run_bass_kernel_spmd

F32 = mybir.dt.float32
BF16 = mybir.dt.bfloat16
AF = mybir.ActivationFunctionType
ALU = mybir.AluOpType
AX = mybir.AxisListType.X

NSEQ = 4
S = 2048
D = 1024
H = 16
KPH = 32
LAT = 512
E = 8
DFF = 2048
SC1 = 1.0 / 8.0
SC2 = 1.0 / math.sqrt(KPH)

ENGINES = ("tensor", "vector", "scalar", "gpsimd", "sync")


class _Sem:
    __slots__ = ("h", "n", "name")

    def __init__(self, h, name):
        self.h = h
        self.n = 0
        self.name = name


class _Prog:
    def __init__(self):
        self.ops = {e: [] for e in ENGINES}

    def emit(self, eng, fn, waits=(), inc=None, inc_n=1):
        w = tuple((s, v) for (s, v) in waits if s is not None and v > 0)
        ev = None
        if inc is not None:
            inc.n += inc_n
            ev = (inc, inc.n)
        self.ops[eng].append((w, fn, inc, inc_n if inc else 0))
        return ev

    def run(self, eng_name, eng):
        seen = {}
        for w, fn, inc, n in self.ops[eng_name]:
            for (s, v) in w:
                if seen.get(s.name, 0) >= v:
                    continue
                seen[s.name] = v
                eng.wait_ge(s.h, v)
            ins = fn(eng) if fn is not None else None
            if inc is not None:
                ins.then_inc(inc.h, n)


def _later(*evs):
    best = {}
    for ev in evs:
        if ev is None:
            continue
        s, v = ev
        if s is None or v <= 0:
            continue
        if best.get(s.name, (None, 0))[1] < v:
            best[s.name] = (s, v)
    return list(best.values())


class Build:
    def __init__(self):
        self.nc = bass.Bass("TRN2", target_bir_lowering=False)
        self.p = _Prog()
        self.ctx = ExitStack()
        self.sems = {}
        self.free = {}

    def sem(self, name):
        if name not in self.sems:
            h = self.ctx.enter_context(self.nc.semaphore(name))
            self.sems[name] = _Sem(h, name)
        return self.sems[name]

    def pe_now(self):
        return (self.sems["s_pe"], self.sems["s_pe"].n)

    def ps4(self, slot):
        i = slot % 4
        ps = (self.ps_a, self.ps_b, self.ps_c, self.ps_d)[i]
        pn = ("ps_a", "ps_b", "ps_c", "ps_d")[i]
        return ps, pn

    def setup(self):
        nc = self.nc
        self.x_in = nc.dram_tensor("x", [NSEQ, S, D], F32, kind="ExternalInput").ap()
        wn = lambda n, shp: nc.dram_tensor(n, shp, F32, kind="ExternalInput").ap()
        self.aqw = wn("L0_attn_q_w", [D, D]);   self.aqb = wn("L0_attn_q_b", [D])
        self.akw = wn("L0_attn_k_w", [D, D]);   self.akb = wn("L0_attn_k_b", [D])
        self.avw = wn("L0_attn_v_w", [D, D]);   self.avb = wn("L0_attn_v_b", [D])
        self.acw = wn("L0_attn_conv_w", [D, 1, 3]); self.acb = wn("L0_attn_conv_b", [D])
        self.aow = wn("L0_attn_out_w", [D, D]); self.aob = wn("L0_attn_out_b", [D])
        self.fqw = wn("L0_ffn_q_w", [D, D]);    self.fqb = wn("L0_ffn_q_b", [D])
        self.fkw = wn("L0_ffn_k_w", [D, LAT]);  self.fkb = wn("L0_ffn_k_b", [LAT])
        self.fvw = wn("L0_ffn_v_w", [D, LAT]);  self.fvb = wn("L0_ffn_v_b", [LAT])
        self.fow = wn("L0_ffn_o_w", [D, D]);    self.fob = wn("L0_ffn_o_b", [D])
        self.frw = wn("L0_ffn_router_w", [D, E]); self.frb = wn("L0_ffn_router_b", [E])
        self.w1 = wn("L0_ffn_w1", [E, D, DFF]); self.b1 = wn("L0_ffn_b1", [E, DFF])
        self.w2 = wn("L0_ffn_w2", [E, DFF, D]); self.b2 = wn("L0_ffn_b2", [E, D])
        self.y_out = nc.dram_tensor("y", [NSEQ, S, D], F32, kind="ExternalOutput").ap()

        ds = lambda n, shp, dt: nc.dram_tensor(n, shp, dt, kind="Internal").ap()
        self.aqw_b = ds("aqw_b", [D, D], BF16)
        self.akw_b = ds("akw_b", [D, D], BF16)
        self.avw_b = ds("avw_b", [D, D], BF16)
        self.aow_b = ds("aow_b", [D, D], BF16)
        self.fqw_b = ds("fqw_b", [D, D], BF16)
        self.fkw_b = ds("fkw_b", [D, LAT], BF16)
        self.fvw_b = ds("fvw_b", [D, LAT], BF16)
        self.fow_b = ds("fow_b", [LAT, D], BF16)
        self.w1_b = ds("w1_b", [E, D, DFF], BF16)
        self.w2_b = ds("w2_b", [E, DFF, D], BF16)
        self.outscr = ds("outscr", [D, S], F32)

        sel = np.zeros((E, E * 128), dtype=np.float32)
        for e in range(E):
            sel[e, e * 128:(e + 1) * 128] = 1.0
        import ml_dtypes
        self.selc_d = nc.inline_tensor(sel.astype(ml_dtypes.bfloat16), name="selcd").ap()

        arena = nc.alloc_sbuf_tensor("arena", [128, 206848], mybir.dt.uint8)
        base = nc.lookup_mloc(arena).addr
        self._cur = base
        self._limit = base + 206848

        def at(name, shape, dt, off=None):
            nbytes = shape[1] * (2 if dt == BF16 else 4)
            if off is None:
                off = self._cur
                self._cur = off + ((nbytes + 31) // 32) * 32
            assert off + nbytes <= self._limit, (name, off + nbytes - base)
            return nc.alloc_sbuf_tensor_at(name, shape, dt, offset=off).ap(), off

        self.xnT, off_A = at("xnT", [128, 8 * S], BF16)
        self.mlaT, _ = at("mlaT", [128, 8 * S], BF16, off=off_A)
        self.qT, off_B = at("qT", [128, 8192], BF16)
        self.q2c, _ = at("q2c", [64, 8192], BF16, off=off_B)
        self.h1g, _ = at("h1g", [128, 8192], BF16, off=off_B)
        self.xdT, off_C = at("xdT", [128, 8192], BF16)
        self.kT, _ = at("kT", [128, 8192], BF16)
        self.k2T, _ = at("k2T", [64, 16384], BF16, off=off_C)
        self.w2s, _ = at("w2s", [128, 16384], BF16, off=off_C)
        self.v1tok, off_D = at("v1tok", [128, 8320], BF16)
        self.v2tok, _ = at("v2tok", [128, 8448], BF16, off=off_D)
        self.bcast, _ = at("bcast", [128, 2048], BF16, off=off_D)
        self._cur = max(self._cur, off_D + 16896)
        self.oT, off_E = at("oT", [128, 8192], BF16)
        self.aoT, _ = at("aoT", [64, 8192], BF16, off=off_E)
        self.w1s, off_F = at("w1s", [128, 16384], BF16)
        self.wtA, _ = at("wtA", [128, 8192], BF16, off=off_F)
        self.wtB, _ = at("wtB", [128, 8192], BF16, off=off_F + 16384)
        self.foB, _ = at("foB", [64, 8192], BF16, off=off_F + 16384)
        self.attb, _ = at("attb", [128, 2048], BF16)
        self.identB, _ = at("identB", [128, 128], BF16)
        self.onesB, _ = at("onesB", [1, 128], BF16)
        self.selc, _ = at("selc", [8, E * 128], BF16)
        self.b2s, _ = at("b2s", [8, D], BF16)
        self.combT, _ = at("combT", [8, S], BF16)
        self.vb1, _ = at("vb1", [1, D], BF16)
        self.v2b, _ = at("v2b", [1, LAT], BF16)
        self.routw, _ = at("routw", [128, 64], BF16)
        self.combB, _ = at("combB", [128, 8], BF16)
        self.stg, off_I = at("stg", [128, 2048], F32)
        self.braw, _ = at("braw", [8, 2048], F32, off=off_I)
        self.osb, _ = at("osb", [128, 1024], F32)
        self.sums, off_K = at("sums", [1, 1024], F32)
        self.probsL, _ = at("probsL", [8, 1024], F32, off=off_K)
        self._cur = max(self._cur, off_K + 4096)
        self.recip, _ = at("recip", [1, 1024], F32)
        self.x_any, _ = at("x_any", [128, 2048], F32)
        self.fin_in, _ = at("fin_in", [128, 2048], F32, off=off_I)
        self.out_st, off_N = at("out_st", [128, 2048], F32)
        self.sq_scr, _ = at("sq_scr", [128, 1024], F32, off=off_N)
        self.identF, _ = at("identF", [128, 128], F32)
        self.onesF, _ = at("onesF", [1, 128], F32)
        self.stats, _ = at("stats", [128, 80], F32)
        self.biasc, _ = at("biasc", [128, 224], F32)
        self.routb, _ = at("routb", [8, 1], F32)
        self.combF, _ = at("combF", [128, 8], F32)
        self.eq8, _ = at("eq8", [128, 8], F32)
        self.constv, _ = at("constv", [128, 8], F32)
        assert self._cur <= self._limit, self._cur - base

        bc = self.biasc
        self.qbcol = bc[:, 0:8]
        self.kbcol = bc[:, 8:16]
        self.q2bcol = bc[:, 16:24]
        self.k2bcol = bc[:, 24:28]
        self.obcol = bc[:, 28:36]
        self.outbcol = bc[:, 36:44]
        self.convbcol = bc[:, 44:52]
        self.convw_j = [bc[:, 52 + 8 * j:60 + 8 * j] for j in range(3)]
        self.b1col = bc[:, 76:204]
        st = self.stats
        self.negm = st[:, 0:16]
        self.rstd = st[:, 16:32]
        self.rstd1 = st[:, 32:48]
        self.nmr = st[:, 48:64]
        self.varsum = st[:, 64:65]
        self.t0 = st[:, 65:66]
        self.t1 = st[:, 66:67]
        self.t2 = st[:, 67:68]
        self.t3 = st[:, 68:69]

        self.ps_s0 = self.ctx.enter_context(nc.psum_tensor("ps_s0", [128, 1024], F32)).ap()
        self.ps_s1 = self.ctx.enter_context(nc.psum_tensor("ps_s1", [128, 1024], F32)).ap()
        self.ps_big = self.ctx.enter_context(nc.psum_tensor("ps_big", [128, 1024], F32)).ap()
        self.ps_tr = self.ctx.enter_context(nc.psum_tensor("ps_tr", [128, 2048], BF16)).ap()
        self.ps_a = self.ps_s0[:, 0:512]
        self.ps_b = self.ps_s0[:, 512:1024]
        self.ps_c = self.ps_s1[:, 0:512]
        self.ps_d = self.ps_s1[:, 512:1024]

        for n in ("sd_wa", "sd_wb", "sd_w1", "sd_w2", "sd_x0", "sd_x1",
                  "sd_f0", "sd_f1", "sd_sp0", "sd_sp1", "sd_st0", "sd_st1",
                  "sd_ac0", "sd_ac1", "sd_g", "sd_misc",
                  "s_pe", "s_act", "s_dve", "s_pool"):
            self.sem(n)

    def acc_group(self, ps, pname, pairs, waits0, rows=128, cols=512, col0=0):
        p, s_pe = self.p, self.sems["s_pe"]
        n = len(pairs)
        ev = None
        for i, (lt, rt) in enumerate(pairs):
            w = []
            if i == 0:
                w = _later(self.free.get((pname,)), *waits0)
            ev = p.emit("tensor",
                        lambda t, l=lt, r=rt, st=(i == 0), sp=(i == n - 1),
                        o=ps[0:rows, col0:col0 + cols]:
                        t.matmul(o, l, r, start=st, stop=sp),
                        waits=w, inc=(s_pe if i == n - 1 else None))
        return ev

    def phase0(self):
        p = self.p
        sd_g, sd_misc = self.sems["sd_g"], self.sems["sd_misc"]
        s_pe, s_dve, s_pool = (self.sems[n] for n in ("s_pe", "s_dve", "s_pool"))

        def mkident(g, ident, size):
            g.memset(ident, 0.0)
            return g.affine_select(out=ident, in_=ident, compare_op=ALU.not_equal,
                                   fill=1.0, base=0, pattern=[[-1, size]],
                                   channel_multiplier=1)
        p.emit("gpsimd", lambda g: mkident(g, self.identB, 128))
        p.emit("gpsimd", lambda g: mkident(g, self.identF, 128))
        p.emit("gpsimd", lambda g: g.memset(self.onesB, 1.0))
        for i, val in enumerate([1e-5, 1.0 / D, SC1, SC2]):
            ap = self.constv[:, i:i + 1]
            p.emit("gpsimd", lambda g, o=ap, v=val: g.memset(o, v))
            self.nc.const_aps.register(F32, val, ap)
        p.emit("gpsimd", lambda g: g.memset(self.onesF, 1.0), inc=s_pool)
        self.ev_consts = (s_pool, s_pool.n)

        casts = [("aq", self.aqw_b, self.aqw), ("ak", self.akw_b, self.akw),
                 ("av", self.avw_b, self.avw), ("ao", self.aow_b, self.aow),
                 ("fq", self.fqw_b, self.fqw), ("fk", self.fkw_b, self.fkw),
                 ("fv", self.fvw_b, self.fvw), ("fo", self.fow_b, self.fow[0:LAT, :])]
        for e in range(E):
            casts.append((f"w1_{e}", self.w1_b[e], self.w1[e]))
            casts.append((f"w2_{e}", self.w2_b[e], self.w2[e]))
        self.ev_cast = {}
        for nm, out_ap, in_ap in casts:
            self.ev_cast[nm] = p.emit(
                "gpsimd", lambda g, o=out_ap, i=in_ap: g.dma_start(o, i),
                inc=sd_g, inc_n=16)
        self.ev_cast_all = (sd_g, sd_g.n)

        for out_ap, in_ap, slow in [
                (self.selc, self.selc_d, False),
                (self.b2s, self.b2, False),
                (self.vb1, self.avb.rearrange("(o d) -> o d", o=1), False),
                (self.v2b, self.fvb.rearrange("(o d) -> o d", o=1), False),
                (self.routw.rearrange("p (ci e) -> p ci e", e=8),
                 self.frw.rearrange("(ci p) e -> p ci e", p=128), True)]:
            p.emit("gpsimd", lambda g, o=out_ap, i=in_ap, sl=slow:
                   g.dma_start(o, i, allow_slow_non_contiguous=sl), inc=sd_g, inc_n=16)
        self.ev_small = (sd_g, sd_g.n)

        ev_copy2 = None
        ev_copy1 = None
        slot = [0]

        def tr_step(src_ap, rows, dest_ap, ev_ld):
            nonlocal ev_copy1, ev_copy2
            po = (slot[0] % 2) * 512
            w = _later(ev_ld, self.ev_consts, ev_copy2)
            ev_tr = p.emit("tensor", lambda t, s=src_ap, r=rows, o=po:
                           t.transpose(self.ps_big[0:128, o:o + r], s,
                                       self.identF[0:r, 0:r]),
                           waits=w, inc=s_pe)
            ev_c = p.emit("vector", lambda v, d=dest_ap, o=po, r=rows:
                          v.tensor_copy(d, self.ps_big[0:128, o:o + r]),
                          waits=[ev_tr], inc=s_dve)
            ev_copy2, ev_copy1 = ev_copy1, ev_c
            slot[0] += 1

        jobs = [(self.aqb, 8, self.qbcol), (self.akb, 8, self.kbcol),
                (self.fqb, 8, self.q2bcol), (self.fkb, 4, self.k2bcol),
                (self.fob, 8, self.obcol), (self.aob, 8, self.outbcol),
                (self.acb, 8, self.convbcol)]
        for vec, rows, dest in jobs:
            src = self.braw[0:rows, 0:128]
            ev_ld = p.emit("sync", lambda sy, o=src,
                           i=vec.rearrange("(a p) -> a p", p=128): sy.dma_start(o, i),
                           waits=[self.pe_now()], inc=sd_misc, inc_n=16)
            tr_step(src, rows, dest[:, 0:rows], ev_ld)
        b1raw = self.braw[0:8, 0:2048]
        ev_ld = p.emit("sync", lambda sy: sy.dma_start(b1raw, self.b1),
                       waits=[self.pe_now()], inc=sd_misc, inc_n=16)
        for fi in range(16):
            tr_step(b1raw[0:8, fi * 128:(fi + 1) * 128], 8,
                    self.b1col[:, fi * 8:fi * 8 + 8], ev_ld)
        cwraw = self.braw[0:8, 0:384]
        ev_ld = p.emit("sync", lambda sy: sy.dma_start(
            cwraw, self.acw.rearrange("(a p) one j -> a (p one j)", a=8)),
            waits=[self.pe_now()], inc=sd_misc, inc_n=16)
        for j in range(3):
            cw2 = self.braw[0:8, 1024 + j * 128: 1024 + (j + 1) * 128]
            ev_cp = p.emit("vector", lambda v, o=cw2, i=cwraw[0:8, j:384:3]:
                           v.tensor_copy(o, i), waits=[ev_ld], inc=s_dve)
            tr_step(cw2, 8, self.convw_j[j], ev_cp)
        p.emit("sync", lambda sy: sy.dma_start(
            self.routb, self.frb.rearrange("(e o) -> e o", o=1)),
            waits=[self.pe_now()], inc=sd_misc, inc_n=16)
        self.ev_bias_done = ev_copy1
        self.ev_misc_ld = (sd_misc, sd_misc.n)
        self.ev_p0_pe = self.pe_now()
        self.free[("ps_big",)] = ev_copy1
        self.free[("stg", 0)] = self.ev_p0_pe
        self.free[("stg", 1)] = self.ev_p0_pe

    def load_wt(self, half, kind, extra_waits=()):
        p = self.p
        sem = self.sems["sd_wa"] if half == 0 else self.sems["sd_wb"]
        dst_t = self.wtA if half == 0 else self.wtB
        waits = _later(self.free.get(("wt", half)), self.ev_cast_all, *extra_waits)
        ev = None
        if kind in ("aq", "ak", "av", "ao"):
            src = {"aq": self.aqw_b, "ak": self.akw_b,
                   "av": self.avw_b, "ao": self.aow_b}[kind]
            for ci in range(8):
                ev = p.emit("scalar",
                            lambda sc, d=dst_t[:, ci * 1024:(ci + 1) * 1024],
                            i=src[:, ci * 128:(ci + 1) * 128]:
                            sc.dma_start(d, i, transpose=True),
                            waits=waits if ci == 0 else (), inc=sem, inc_n=16)
        elif kind == "fq":
            ev = p.emit("sync", lambda sy,
                        d=dst_t[:, 0:8192].rearrange("p (ci o) -> p ci o", o=1024):
                        sy.dma_start(
                d, self.fqw_b.rearrange("(ci p) o -> p ci o", p=128)),
                waits=waits, inc=sem, inc_n=16)
        elif kind in ("fk", "fv"):
            src = self.fkw_b if kind == "fk" else self.fvw_b
            ev = p.emit("sync", lambda sy,
                        d=dst_t[:, 0:4096].rearrange("p (ci o) -> p ci o", o=512),
                        s=src: sy.dma_start(
                d, s.rearrange("(ci p) o -> p ci o", p=128)),
                waits=waits, inc=sem, inc_n=16)
        elif kind == "fo":
            ev = p.emit("sync", lambda sy,
                        d=self.foB.rearrange("p (kc o) -> p kc o", o=1024):
                        sy.dma_start(
                d, self.fow_b.rearrange("(kc p) o -> p kc o", p=64)),
                waits=waits, inc=sem, inc_n=16)
        return ev

    def ln_phase(self, s):
        p = self.p
        s_pe, s_act, s_dve = (self.sems[n] for n in ("s_pe", "s_act", "s_dve"))
        for ti in range(16):
            b = ti % 2
            sd_x = self.sems["sd_x0"] if b == 0 else self.sems["sd_x1"]
            xb = self.x_any[:, b * 1024:(b + 1) * 1024]
            ev_x = p.emit("sync", lambda sy, o=xb, ss=s, t=ti: sy.dma_start(
                o, self.x_in[ss, t * 128:(t + 1) * 128, :]),
                waits=_later(self.free.get(("x_any", b))), inc=sd_x, inc_n=16)
            nm = self.negm[:, ti:ti + 1]
            rs = self.rstd[:, ti:ti + 1]
            ev = p.emit("vector", lambda v, o=nm, i=xb:
                        v.reduce_sum(o, i, axis=AX, negate=True),
                        waits=[ev_x], inc=s_dve)
            ev = p.emit("vector", lambda v, o=nm:
                        v.tensor_scalar_mul(o, o, 1.0 / D), waits=[ev], inc=s_dve)
            ev = p.emit("scalar", lambda sc, i=xb, n=nm:
                        sc.activation(self.sq_scr, i, AF.Square, bias=n,
                                      accum_out=self.varsum),
                        waits=_later(ev, self.free.get(("out_st", 0))
                                     if ti == 0 else None,
                                     self.ev_consts if ti == 0 else None),
                        inc=s_act)
            ev = p.emit("scalar", lambda sc, r=rs:
                        sc.activation(r, self.varsum, AF.Sqrt, scale=1.0 / D,
                                      bias=1e-5),
                        waits=[ev], inc=s_act)
            ev = p.emit("vector", lambda v, r=rs: v.reciprocal(r, r),
                        waits=[ev], inc=s_dve)
            p.emit("vector", lambda v, t=ti: v.tensor_scalar_add(
                self.rstd1[:, t:t + 1], self.rstd[:, t:t + 1], 1.0),
                waits=[ev], inc=s_dve)
            p.emit("vector", lambda v, t=ti: v.tensor_mul(
                self.nmr[:, t:t + 1], self.negm[:, t:t + 1], self.rstd[:, t:t + 1]),
                inc=s_dve)
            xn_tok = self.attb[:, b * 1024:(b + 1) * 1024]
            wv = _later(ev, self.free.get(("attb", b)),
                        self.free.get(("attb_s", 0)) if (b == 0 and ti == 0) else None,
                        self.free.get(("attb_s", 1)) if (b == 0 and ti == 0) else None)
            ev = p.emit("vector", lambda v, o=xn_tok, i=xb, t=ti:
                        v.tensor_scalar(o, i, self.negm[:, t:t + 1],
                                        self.rstd[:, t:t + 1],
                                        op0=ALU.add, op1=ALU.mult),
                        waits=wv, inc=s_dve)
            self.free[("x_any", b)] = ev
            ho = b * 1024
            evt = None
            for k in range(8):
                wts = _later(ev, self.ev_p0_pe,
                             self.free.get(("ps_tr", b)) if k == 0 else None,
                             self.free.get(("xnT_w",)) if (ti == 0 and k == 0) else None)
                evt = p.emit("tensor", lambda t, kk=k, o=ho, i=xn_tok:
                             t.transpose(self.ps_tr[:, o + kk * 128:o + (kk + 1) * 128],
                                         i[:, kk * 128:(kk + 1) * 128], self.identB),
                             waits=wts, inc=(s_pe if k == 7 else None))
            self.free[("attb", b)] = evt
            dest = self.xnT.rearrange("p (c t) -> p c t", t=S)[:, :, ti * 128:(ti + 1) * 128]
            ev_e = p.emit("vector", lambda v, d=dest, o=ho:
                          v.tensor_copy(d, self.ps_tr[:, o:o + 1024]
                                        .rearrange("p (c t) -> p c t", t=128)),
                          waits=[evt], inc=s_dve)
            self.free[("ps_tr", b)] = ev_e
        self.ev_xnT = (s_dve, s_dve.n)

    def conv_phase(self, s):
        p = self.p
        s_pool = self.sems["s_pool"]
        s_dve = self.sems["s_dve"]
        tA = self.x_any[:, 0:1024]
        tB = self.x_any[:, 1024:2048]
        ev = None
        for ci in range(8):
            xn = self.xnT[:, ci * S:(ci + 1) * S]
            xd = self.xdT[:, ci * 1024:(ci + 1) * 1024]
            w0 = _later(self.ev_xnT, self.ev_bias_done,
                        self.free.get(("C",)) if ci == 0 else None,
                        self.free.get(("x_any", 0)) if ci == 0 else None,
                        self.free.get(("x_any", 1)) if ci == 0 else None)
            ev = p.emit("vector", lambda v, o=tB[:, 1:1024], i=xn[:, 1:2046:2], c=ci:
                        v.tensor_scalar_mul(o, i, self.convw_j[0][:, c:c + 1]),
                        waits=w0, inc=s_dve)
            for step in [
                    lambda v, o=tA[:, 1:1024], i=xn[:, 2:2047:2], c=ci:
                    v.scalar_tensor_tensor(o, i, self.convw_j[1][:, c:c + 1],
                                           tB[:, 1:1024], op0=ALU.mult, op1=ALU.add),
                    lambda v, o=tB[:, 1:1024], i=xn[:, 3:2048:2], c=ci:
                    v.scalar_tensor_tensor(o, i, self.convw_j[2][:, c:c + 1],
                                           tA[:, 1:1024], op0=ALU.mult, op1=ALU.add),
                    lambda v, o=tA[:, 0:1], i=xn[:, 0:1], c=ci:
                    v.tensor_scalar_mul(o, i, self.convw_j[1][:, c:c + 1]),
                    lambda v, o=tB[:, 0:1], i=xn[:, 1:2], c=ci:
                    v.scalar_tensor_tensor(o, i, self.convw_j[2][:, c:c + 1],
                                           tA[:, 0:1], op0=ALU.mult, op1=ALU.add),
                    lambda v, o=xd, c=ci:
                    v.tensor_scalar_add(o, tB[:, 0:1024], self.convbcol[:, c:c + 1])]:
                ev = p.emit("vector", step, waits=[ev], inc=s_dve)
        self.free[("x_any", 0)] = ev
        self.free[("x_any", 1)] = ev
        self.ev_xdT = ev

    def kv1_phase(self, s):
        p = self.p
        s_act, s_dve, s_pool = (self.sems[n] for n in ("s_act", "s_dve", "s_pool"))
        ev_k = self.ev_wtB
        ev_cfree = self.free.get(("C",))
        slot = 0
        for ii in range(8):
            for qc in range(2):
                ps, pn = self.ps4(slot)
                pairs = [(self.wtB[:, ci * 1024 + ii * 128: ci * 1024 + ii * 128 + 128],
                          self.xdT[:, ci * 1024 + qc * 512: ci * 1024 + qc * 512 + 512])
                         for ci in range(8)]
                ev = self.acc_group(ps, pn, pairs, [ev_k, self.ev_xdT])
                ev_e = p.emit("scalar", lambda sc,
                              o=self.kT[:, ii * 1024 + qc * 512: ii * 1024 + qc * 512 + 512],
                              i=ps, c=ii:
                              sc.activation(o, i, AF.Identity,
                                            bias=self.kbcol[:, c:c + 1]),
                              waits=_later(ev, ev_cfree if slot == 0 else None),
                              inc=s_act)
                self.free[(pn,)] = ev_e
                slot += 1
        self.ev_kT = (s_act, s_act.n)
        self.free[("wt", 1)] = self.pe_now()

        ev_v = self.load_wt(1, "av")
        self.ev_wtB = ev_v
        ev_ones = p.emit("gpsimd", lambda g: g.memset(
            self.v1tok[:, 0:8320].rearrange("p (x d) -> p x d", d=65)[:, :, 64:65], 1.0),
            waits=_later(self.free.get(("vtok",)), self.free.get(("vtok2",))),
            inc=s_pool)
        for kti in range(8):
            for qc in range(2):
                ps, pn = self.ps4(slot)
                pairs = [(self.xdT[:, ci * 1024 + kti * 128: ci * 1024 + kti * 128 + 128],
                          self.wtB[:, ci * 1024 + qc * 512: ci * 1024 + qc * 512 + 512])
                         for ci in range(8)]
                pairs.append((self.onesB, self.vb1[:, qc * 512:(qc + 1) * 512]))
                ev = self.acc_group(ps, pn, pairs, [ev_v])
                base = kti * 1040 + qc * 520
                dest = self.v1tok[:, base:base + 520].rearrange(
                    "p (h d) -> p h d", d=65)[:, :, 0:64]
                ev_e = p.emit("vector", lambda v, d=dest,
                              i=ps.rearrange("p (h d) -> p h d", d=64):
                              v.tensor_copy(d, i),
                              waits=_later(ev, ev_ones), inc=s_dve)
                self.free[(pn,)] = ev_e
                slot += 1
        self.ev_v1 = (s_dve, s_dve.n)
        self.free[("wt", 1)] = self.pe_now()

    def kv2_phase(self, s):
        p = self.p
        s_act, s_dve, s_pool = (self.sems[n] for n in ("s_act", "s_dve", "s_pool"))
        ev_k = self.ev_wtB
        ev_cfree = self.pe_now()
        slot = 0
        for hc in range(8):
            for qc in range(4):
                ps, pn = self.ps4(slot)
                pairs = [(self.wtB[:, ci * 512 + hc * 64: ci * 512 + hc * 64 + 64],
                          self.xnT[:, ci * S + qc * 512: ci * S + qc * 512 + 512])
                         for ci in range(8)]
                ev = self.acc_group(ps, pn, pairs, [ev_k, self.ev_xnT], rows=64)
                ev_e = p.emit("scalar", lambda sc,
                              o=self.k2T[0:64, hc * 2048 + qc * 512:
                                         hc * 2048 + qc * 512 + 512],
                              i=ps[0:64, :], c=hc:
                              sc.activation(o, i, AF.Identity,
                                            bias=self.k2bcol[(c % 2) * 64:(c % 2) * 64 + 64,
                                                             c // 2:c // 2 + 1]),
                              waits=_later(ev, ev_cfree if slot == 0 else None),
                              inc=s_act)
                self.free[(pn,)] = ev_e
                slot += 1
        self.ev_k2 = (s_act, s_act.n)
        self.free[("wt", 1)] = self.pe_now()

        ev_v = self.load_wt(1, "fv")
        self.ev_wtB = ev_v
        ev_ones = p.emit("gpsimd", lambda g: g.memset(
            self.v2tok[:, 0:8448].rearrange("p (x d) -> p x d", d=33)[:, :, 32:33], 1.0),
            waits=_later(self.pe_now()), inc=s_pool)
        for kti in range(16):
            ps, pn = self.ps4(slot)
            pairs = [(self.xnT[:, ci * S + kti * 128: ci * S + kti * 128 + 128],
                      self.wtB[:, ci * 512: ci * 512 + 512])
                     for ci in range(8)]
            pairs.append((self.onesB, self.v2b[:, 0:512]))
            ev = self.acc_group(ps, pn, pairs, [ev_v])
            base = kti * 528
            dest = self.v2tok[:, base:base + 528].rearrange(
                "p (h d) -> p h d", d=33)[:, :, 0:32]
            ev_e = p.emit("vector", lambda v, d=dest,
                          i=ps.rearrange("p (h d) -> p h d", d=32):
                          v.tensor_copy(d, i),
                          waits=_later(ev, ev_ones), inc=s_dve)
            self.free[(pn,)] = ev_e
            slot += 1
        self.ev_v2 = (s_dve, s_dve.n)
        self.free[("wt", 1)] = self.pe_now()

    def q_proj(self, s, qh, branch):
        p = self.p
        s_act = self.sems["s_act"]
        qfree = self.free.get(("qbuf",))
        slot = 0
        for ii in range(8):
            for qc in range(2):
                ps, pn = self.ps4(slot)
                pairs = [(self.wtA[:, ci * 1024 + ii * 128: ci * 1024 + ii * 128 + 128],
                          self.xnT[:, ci * S + qh * 1024 + qc * 512:
                                   ci * S + qh * 1024 + qc * 512 + 512])
                         for ci in range(8)]
                ev = self.acc_group(ps, pn, pairs, [self.ev_wtA, self.ev_xnT])
                o0 = ii * 1024 + qc * 512
                if branch == 1:
                    ev_e = p.emit("scalar", lambda sc, o=self.qT[:, o0:o0 + 512],
                                  i=ps, c=ii:
                                  sc.activation(o, i, AF.Identity,
                                                bias=self.qbcol[:, c:c + 1]),
                                  waits=_later(ev, qfree if slot == 0 else None),
                                  inc=s_act)
                else:
                    p.emit("scalar", lambda sc, o=self.q2c[0:32, o0:o0 + 512],
                           i=ps[0:32, :], c=ii:
                           sc.activation(o, i, AF.Identity,
                                         bias=self.q2bcol[0:32, c:c + 1]),
                           waits=_later(ev, qfree if slot == 0 else None), inc=s_act)
                    ev_e = p.emit("scalar", lambda sc, o=self.q2c[32:64, o0:o0 + 512],
                                  i=ps[64:96, :], c=ii:
                                  sc.activation(o, i, AF.Identity,
                                                bias=self.q2bcol[64:96, c:c + 1]),
                                  inc=s_act)
                self.free[(pn,)] = ev_e
                slot += 1
        self.ev_q = (s_act, s_act.n)

    def attn_phase(self, s, qh, branch):
        p = self.p
        s_pe, s_act, s_dve = (self.sems[n] for n in ("s_pe", "s_act", "s_dve"))
        if branch == 1:
            n_kti, dv, sc = 8, 64, SC1
            kv_ready = _later(self.ev_kT, self.ev_v1, self.ev_q)
        else:
            n_kti, dv, sc = 16, 32, SC2
            kv_ready = _later(self.ev_k2, self.ev_v2, self.ev_q)
        oa_free = self.free.get(("oa",))
        first_w = _later(*(kv_ready + [self.free.get(("ps_a",)),
                                       self.free.get(("ps_b",)),
                                       self.free.get(("ps_c",)),
                                       self.free.get(("ps_d",))]))

        def slices(h, kti, qc):
            if branch == 1:
                lt = self.kT[(h % 2) * 64:(h % 2) * 64 + 64,
                             (h // 2) * 1024 + kti * 128:
                             (h // 2) * 1024 + kti * 128 + 128]
                rt = self.qT[(h % 2) * 64:(h % 2) * 64 + 64,
                             (h // 2) * 1024 + qc * 512:
                             (h // 2) * 1024 + qc * 512 + 512]
                vt = self.v1tok[:, kti * 1040 + h * 65: kti * 1040 + h * 65 + 65]
            else:
                lt = self.k2T[(h % 2) * 32:(h % 2) * 32 + 32,
                              (h // 2) * 2048 + kti * 128:
                              (h // 2) * 2048 + kti * 128 + 128]
                rt = self.q2c[(h % 2) * 32:(h % 2) * 32 + 32,
                              (h // 2) * 1024 + qc * 512:
                              (h // 2) * 1024 + qc * 512 + 512]
                vt = self.v2tok[:, kti * 528 + h * 33: kti * 528 + h * 33 + 33]
            return lt, rt, vt

        sstep = 0
        for h in range(H):
            ev_av_last = None
            for kp in range(n_kti // 2):
                for qc in range(2):
                    buf = sstep % 2
                    psS = self.ps_s0 if buf == 0 else self.ps_s1
                    pnS = ("psS", buf)
                    k0, k1 = 2 * kp, 2 * kp + 1
                    lt0, rt0, vt0 = slices(h, k0, qc)
                    lt1, rt1, vt1 = slices(h, k1, qc)
                    w = _later(self.free.get(pnS),
                               *( [first_w[i] for i in range(len(first_w))]
                                  if (h == 0 and kp == 0 and qc == 0) else []))
                    p.emit("tensor", lambda t, l=lt0, r=rt0, o=psS[:, 0:512]:
                           t.matmul(o, l, r, start=True, stop=True), waits=w)
                    ev_mm = p.emit("tensor", lambda t, l=lt1, r=rt1,
                                   o=psS[:, 512:1024]:
                                   t.matmul(o, l, r, start=True, stop=True),
                                   inc=s_pe)
                    ab = self.attb[:, buf * 1024:(buf + 1) * 1024]
                    ev_x = p.emit("scalar", lambda scl, o=ab, i=psS, f=sc:
                                  scl.activation(o, i, AF.Exp, scale=f),
                                  waits=_later(ev_mm,
                                               self.free.get(("attb4", buf)),
                                               self.free.get(("attb", buf))
                                               if (h == 0 and kp == 0 and qc == 0)
                                               else None),
                                  inc=s_act)
                    self.free[pnS] = ev_x
                    po = self.ps_big[0:dv + 1, qc * 512: qc * 512 + 512]
                    okw = [self.free.get(("ps_big",))] if (kp == 0 and qc == 0) else []
                    p.emit("tensor", lambda t, l=vt0, r=ab[:, 0:512], o=po,
                           st=(kp == 0):
                           t.matmul(o, l, r, start=st, stop=False),
                           waits=_later(ev_x, *okw))
                    ev_av = p.emit("tensor", lambda t, l=vt1, r=ab[:, 512:1024],
                                   o=po, sp=(kp == n_kti // 2 - 1):
                                   t.matmul(o, l, r, start=False, stop=sp),
                                   inc=s_pe)
                    self.free[("attb4", buf)] = ev_av
                    ev_av_last = ev_av
                    sstep += 1
            srcs = [self.ps_big[0:dv + 1, 0:512], self.ps_big[0:dv + 1, 512:1024]]
            bcs = [self.ps_big[0:dv, 0:512], self.ps_big[0:dv, 512:1024]]
            ev_ev = None
            for qc in range(2):
                ev_ev = p.emit("vector", lambda v, i=srcs[qc],
                               o=self.osb[0:dv + 1, qc * 512:qc * 512 + 512]:
                               v.tensor_copy(o, i),
                               waits=_later(ev_av_last, self.free.get(("osb",))
                                            if qc == 0 else None),
                               inc=s_dve)
            ev_r = p.emit("vector", lambda v, r=dv:
                          v.reciprocal(self.recip[0:1, 0:1024],
                                       self.osb[r:r + 1, 0:1024]),
                          waits=[ev_ev], inc=s_dve)
            ev_bc = None
            for qc in range(2):
                ev_bc = p.emit("tensor", lambda t, o=bcs[qc], q=qc, r=dv:
                               t.matmul(o, self.onesF[0:1, 0:r],
                                        self.recip[0:1, q * 512:q * 512 + 512],
                                        start=True, stop=True),
                               waits=[ev_r], inc=s_pe)
            ev_m = None
            for qc in range(2):
                if branch == 1:
                    dst = self.oT[(h % 2) * 64:(h % 2) * 64 + 64,
                                  (h // 2) * 1024 + qc * 512:
                                  (h // 2) * 1024 + qc * 512 + 512]
                else:
                    dst = self.aoT[(h % 2) * 32:(h % 2) * 32 + 32,
                                   (h // 2) * 1024 + qc * 512:
                                   (h // 2) * 1024 + qc * 512 + 512]
                ev_m = p.emit("vector", lambda v, d=dst,
                              a=self.osb[0:dv, qc * 512:qc * 512 + 512], b=bcs[qc]:
                              v.tensor_mul(d, a, b),
                              waits=_later(ev_bc, oa_free
                                           if (h == 0 and qc == 0) else None),
                              inc=s_dve)
            self.free[("osb",)] = ev_m
            self.free[("ps_big",)] = ev_m
        end_ev = (s_act, s_act.n)
        for k in (("ps_a",), ("ps_b",), ("ps_c",), ("ps_d",)):
            self.free[k] = _later(self.free.get(k), end_ev)[0] \
                if self.free.get(k) else end_ev
        self.free[("ps_a",)] = end_ev
        self.free[("ps_b",)] = end_ev
        self.free[("ps_c",)] = end_ev
        self.free[("ps_d",)] = end_ev
        if branch == 1:
            self.ev_oT = (s_dve, s_dve.n)
        else:
            self.ev_aoT = (s_dve, s_dve.n)

    def out_proj(self, s, qh):
        p = self.p
        s_act = self.sems["s_act"]
        scr_rd = _later(self.free.get(("outscr_rd0",)), self.free.get(("outscr_rd1",)))
        for dc in range(8):
            half = dc % 2
            sd_sp = self.sems["sd_sp0"] if half == 0 else self.sems["sd_sp1"]
            ev_e = None
            for qc in range(2):
                ps, pn = self.ps4(dc * 2 + qc)
                pairs = [(self.wtB[:, ci * 1024 + dc * 128: ci * 1024 + dc * 128 + 128],
                          self.oT[:, ci * 1024 + qc * 512: ci * 1024 + qc * 512 + 512])
                         for ci in range(8)]
                ev = self.acc_group(ps, pn, pairs, [self.ev_wtB, self.ev_oT])
                ev_e = p.emit("scalar", lambda sc,
                              o=self.stg[:, half * 1024 + qc * 512:
                                         half * 1024 + qc * 512 + 512],
                              i=ps, c=dc:
                              sc.activation(o, i, AF.Identity,
                                            bias=self.outbcol[:, c:c + 1]),
                              waits=_later(ev, self.free.get(("stg", half))
                                           if qc == 0 else None),
                              inc=s_act)
                self.free[(pn,)] = ev_e
            ev_sp = p.emit("sync", lambda sy, d=dc, q=qh:
                           sy.dma_start(self.outscr[d * 128:(d + 1) * 128,
                                                    q * 1024:(q + 1) * 1024],
                                        self.stg[:, (d % 2) * 1024:(d % 2) * 1024 + 1024]),
                           waits=_later(ev_e, *(scr_rd if dc < 2 else [])),
                           inc=sd_sp, inc_n=16)
            self.free[("stg", half)] = ev_sp

    def mla_proj(self, s, qh):
        p = self.p
        s_act = self.sems["s_act"]
        slot = 0
        for dc in range(8):
            for qc in range(2):
                ps, pn = self.ps4(slot)
                pairs = [(self.foB[0:64, kc * 1024 + dc * 128: kc * 1024 + dc * 128 + 128],
                          self.aoT[0:64, kc * 1024 + qc * 512: kc * 1024 + qc * 512 + 512])
                         for kc in range(8)]
                ev = self.acc_group(ps, pn, pairs, [self.ev_foB, self.ev_aoT])
                ev_e = p.emit("scalar", lambda sc,
                              o=self.mlaT[:, dc * S + qh * 1024 + qc * 512:
                                          dc * S + qh * 1024 + qc * 512 + 512],
                              i=ps, c=dc:
                              sc.activation(o, i, AF.Identity,
                                            bias=self.obcol[:, c:c + 1]),
                              waits=[ev], inc=s_act)
                self.free[(pn,)] = ev_e
                slot += 1
        self.ev_mla = (s_act, s_act.n)

    def router_phase(self, s, qh):
        p = self.p
        s_pe, s_act, s_dve = (self.sems[n] for n in ("s_pe", "s_act", "s_dve"))
        ev = None
        for qc in range(2):
            pairs = [(self.routw[:, ci * 8: ci * 8 + 8],
                      self.mlaT[:, ci * S + qh * 1024 + qc * 512:
                                ci * S + qh * 1024 + qc * 512 + 512])
                     for ci in range(8)]
            ev = self.acc_group(self.ps_big, "ps_big" if qc == 0 else "__skip",
                                pairs,
                                [self.ev_mla, self.ev_misc_ld] if qc == 0 else [],
                                rows=8, cols=512, col0=qc * 512)
        ev_l = p.emit("scalar", lambda sc:
                      sc.activation(self.probsL[0:8, 0:1024],
                                    self.ps_big[0:8, 0:1024],
                                    AF.Identity, bias=self.routb[0:8, 0:1]),
                      waits=[ev], inc=s_act)
        self.free[("ps_big",)] = ev_l
        for ti in range(8):
            po = (ti % 2) * 512
            ev_tr = p.emit("tensor", lambda t, o=po,
                           i=self.probsL[0:8, ti * 128:(ti + 1) * 128]:
                           t.transpose(self.ps_big[0:128, o:o + 8], i,
                                       self.identF[0:8, 0:8]),
                           waits=_later(ev_l, self.free.get(("rps", ti % 2)),
                                        self.free.get(("ps_big",)) if ti < 2 else None),
                           inc=s_pe)
            pl = self.ps_big[0:128, po:po + 8]
            ev1 = p.emit("vector", lambda v, i=pl: v.reduce_max(
                self.t0, i, axis=AX, negate=True), waits=[ev_tr], inc=s_dve)
            ev2 = p.emit("scalar", lambda sc, i=pl: sc.activation(
                self.combF, i, AF.Exp, bias=self.t0), waits=[ev1], inc=s_act)
            self.free[("rps", ti % 2)] = ev2
            ev = ev2
            for fn in [
                    lambda v: v.reduce_sum(self.t1, self.combF, axis=AX),
                    lambda v: v.reciprocal(self.t1, self.t1),
                    lambda v: v.tensor_scalar_mul(self.combF, self.combF,
                                                  self.t1[:, 0:1]),
                    lambda v: v.reduce_max(self.t0, self.combF, axis=AX),
                    lambda v: v.tensor_single_scalar(self.eq8, self.combF,
                                                     self.t0[:, 0:1], ALU.is_equal),
                    lambda v: v.scalar_tensor_tensor(self.eq8, self.eq8, -1e30,
                                                     self.combF, op0=ALU.mult,
                                                     op1=ALU.add),
                    lambda v: v.reduce_max(self.t2, self.eq8, axis=AX),
                    lambda v: v.tensor_single_scalar(self.eq8, self.combF,
                                                     self.t2[:, 0:1], ALU.is_ge),
                    lambda v: v.tensor_add(self.t3, self.t0, self.t2),
                    lambda v: v.tensor_scalar_add(self.t3, self.t3, 1e-8),
                    lambda v: v.reciprocal(self.t3, self.t3),
                    lambda v: v.tensor_scalar_mul(self.t3, self.t3, 0.3),
                    lambda v: v.tensor_mul(self.combF, self.combF, self.eq8),
                    lambda v: v.tensor_scalar_mul(self.combF, self.combF,
                                                  self.t3[:, 0:1])]:
                ev = p.emit("vector", fn, waits=[ev], inc=s_dve)
            ev4 = p.emit("vector", lambda v: v.tensor_copy(self.combB, self.combF),
                         waits=_later(ev, self.free.get(("combB",))), inc=s_dve)
            ev_t2 = p.emit("tensor", lambda t, o=(ti % 2) * 1024:
                           t.transpose(self.ps_tr[0:8, o:o + 128], self.combB,
                                       self.identB),
                           waits=_later(ev4, self.free.get(("ctr", ti % 2)),
                                        self.free.get(("ps_tr", ti % 2))
                                        if ti < 2 else None),
                           inc=s_pe)
            self.free[("combB",)] = ev_t2
            ev5 = p.emit("vector", lambda v, o=(ti % 2) * 1024, t=ti:
                         v.tensor_copy(self.combT[0:8, qh * 1024 + t * 128:
                                                  qh * 1024 + (t + 1) * 128],
                                       self.ps_tr[0:8, o:o + 128]),
                         waits=[ev_t2], inc=s_dve)
            self.free[("ctr", ti % 2)] = ev5
            self.free[("ps_tr", ti % 2)] = ev5
        self.ev_comb = (s_dve, s_dve.n)
        self.free[("ps_big",)] = (s_act, s_act.n)

    def moe_phase(self, s):
        p = self.p
        s_pe, s_act, s_dve, s_pool = (self.sems[n] for n in
                                      ("s_pe", "s_act", "s_dve", "s_pool"))
        sd_w1, sd_w2 = self.sems["sd_w1"], self.sems["sd_w2"]
        ev_sp = [(self.sems["sd_sp0"], self.sems["sd_sp0"].n),
                 (self.sems["sd_sp1"], self.sems["sd_sp1"].n)]
        ev_fin_prev = _later(self.free.get(("outscr_rd0",)),
                             self.free.get(("outscr_rd1",)))
        ev_b2_done = self.pe_now()

        e_list = [int(t) for t in os.environ.get("KDBG_ELIST", "").split(",")] \
            if os.environ.get("KDBG_ELIST") else list(range(E))
        qc_list = [int(t) for t in os.environ.get("KDBG_QCLIST", "").split(",")] \
            if os.environ.get("KDBG_QCLIST") else list(range(4))
        for ei, e in enumerate(e_list):
            ev_w1 = self.ev_w1[e]
            ev_w2 = self.ev_w2[e]
            ev_bc_e = None
            for half in range(2):
                evb = None
                for qc in range(2):
                    evb = p.emit("tensor", lambda t, hh=half, q=qc, ee=e:
                                 t.matmul(self.ps_big[0:128, q * 512:q * 512 + 512],
                                          self.selc[0:8, ee * 128:(ee + 1) * 128],
                                          self.combT[0:8, hh * 1024 + q * 512:
                                                     hh * 1024 + q * 512 + 512],
                                          start=True, stop=True),
                                 waits=_later(self.ev_comb,
                                              self.free.get(("ps_big",))
                                              if qc == 0 else None,
                                              self.ev_small if (e == 0 and half == 0
                                                                and qc == 0) else None),
                                 inc=s_pe)
                ev_bc_e = p.emit("vector", lambda v, hh=half:
                                 v.tensor_copy(self.bcast[:, hh * 1024:(hh + 1) * 1024],
                                               self.ps_big[0:128, 0:1024]),
                                 waits=_later(evb, ev_b2_done
                                              if (e == 0 and half == 0) else None),
                                 inc=s_dve)
                self.free[("ps_big",)] = ev_bc_e
            for qc in qc_list:
                ev_h2_prev = self.free.get(("h1g",))
                ev_scale = {}
                for fi in range(16):
                    sl = (fi + int(os.environ.get("KDBG_SLSWAP", "0"))) % 2
                    ps = self.ps_a if sl == 0 else self.ps_b
                    pn = "ps_a" if sl == 0 else "ps_b"
                    pairs = [(self.w1s[:, ci * 2048 + fi * 128: ci * 2048 + fi * 128 + 128],
                              self.mlaT[:, ci * S + qc * 512: ci * S + qc * 512 + 512])
                             for ci in range(8)]
                    ev = self.acc_group(ps, pn, pairs,
                                        [ev_w1, self.ev_mla]
                                        if (fi == 0 and qc == 0) else [])
                    if os.environ.get("KDBG_H1PS") and fi == 0:
                        p.emit("vector", lambda v, i=ps:
                               v.tensor_copy(self.out_st[:, 0:512], i),
                               waits=[ev], inc=s_dve)
                        self._h1ps_done = True
                        return
                    hslice = self.h1g[:, fi * 512:(fi + 1) * 512]
                    gb = self.attb[:, (fi % 2) * 512:(fi % 2) * 512 + 512]
                    ev_g = p.emit("scalar", lambda sc, o=gb, i=ps, f=fi, ee=e:
                                  sc.activation(o, i, AF.Gelu,
                                                bias=self.b1col[:, f * 8 + ee:
                                                                f * 8 + ee + 1]),
                                  waits=_later(ev, ev_h2_prev,
                                               self.free.get(("gb", fi % 2))),
                                  inc=s_act)
                    self.free[(pn,)] = ev_g
                    ev_scale[fi] = p.emit("vector", lambda v, o=hslice, i=gb, q=qc:
                                          v.tensor_mul(o, i,
                                                       self.bcast[:, q * 512:
                                                                  q * 512 + 512]),
                                          waits=_later(ev_g, ev_bc_e), inc=s_dve)
                    self.free[("gb", fi % 2)] = ev_scale[fi]
                ev_h2_last = None
                for dc in range(8):
                    sl = dc % 2
                    psh = self.ps_c if sl == 0 else self.ps_d
                    pnh = "ps_c" if sl == 0 else "ps_d"
                    ev = None
                    for fi in range(16):
                        w = []
                        if fi == 0:
                            w = _later(self.free.get((pnh,)),
                                       ev_w2 if (dc == 0 and qc == qc_list[0])
                                       else None)
                        if dc == 0:
                            w = _later(*(w + [ev_scale[fi]]))
                        ev = p.emit("tensor", lambda t,
                                    l=self.w2s[:, fi * 1024 + dc * 128:
                                               fi * 1024 + dc * 128 + 128],
                                    r=self.h1g[:, fi * 512:(fi + 1) * 512],
                                    st=(fi == 0),
                                    sp=(fi == 15 and (ei != 0 or
                                        bool(os.environ.get("KDBG_NOB")))), o=psh:
                                    t.matmul(o, l, r, start=st, stop=sp),
                                    waits=w,
                                    inc=(s_pe if (fi == 15 and (ei != 0 or
                                         bool(os.environ.get("KDBG_NOB"))))
                                         else None))
                    if ei == 0 and not os.environ.get("KDBG_NOB"):
                        ev = p.emit("tensor", lambda t, o=psh, d=dc, q=qc:
                                    t.matmul(o, self.b2s[0:8, d * 128:(d + 1) * 128],
                                             self.combT[0:8, q * 512:(q + 1) * 512],
                                             start=False, stop=True),
                                    inc=s_pe)
                    elif ei == 0:
                        pass
                    ev_h2_last = ev
                    half = (dc // 2) % 2
                    sd_ac = self.sems["sd_ac0"] if half == 0 else self.sems["sd_ac1"]
                    ev_e = p.emit("vector", lambda v,
                                  o=self.stg[:, half * 1024 + (dc % 2) * 512:
                                             half * 1024 + (dc % 2) * 512 + 512],
                                  i=psh[0:128, 0:512]:
                                  v.tensor_copy(o, i),
                                  waits=_later(ev,
                                               self.free.get(("stg2", half))
                                               if dc % 2 == 0 else None,
                                               self.free.get(("stg", half))
                                               if (e == 0 and qc == 0 and dc < 2)
                                               else None),
                                  inc=s_dve)
                    self.free[(pnh,)] = ev_e
                    if dc % 2 == 1:
                        dst = self.outscr.rearrange("(dc p) t -> p dc t", p=128)[
                            :, dc - 1:dc + 1, qc * 512:qc * 512 + 512]
                        src = self.stg[:, half * 1024: half * 1024 + 1024].rearrange(
                            "p (a t) -> p a t", t=512)
                        if ei == 0:
                            w = _later(ev_e, *ev_sp, ev_fin_prev[0]
                                       if ev_fin_prev else None,
                                       ev_fin_prev[1] if ev_fin_prev and
                                       len(ev_fin_prev) > 1 else None)
                        else:
                            w = [ev_e]
                        ev_ac = p.emit("gpsimd", lambda g, d=dst, i=src:
                                       g.dma_start(d, i, accum_op=ALU.add),
                                       waits=w, inc=sd_ac, inc_n=16)
                        self.free[("stg2", half)] = ev_ac
                self.free[("h1g",)] = ev_h2_last
            if ei + 1 < len(e_list):
                nxt = e_list[ei + 1]
                self.ev_w1[nxt] = p.emit(
                    "sync", lambda sy, ee=nxt:
                    sy.dma_start(self.w1s.rearrange("p (ci f) -> p ci f", f=2048),
                                 self.w1_b[ee].rearrange(
                        "(ci p) f -> p ci f", p=128)),
                    waits=_later(self.pe_now(), self.ev_cast[f"w1_{nxt}"]),
                    inc=sd_w1, inc_n=16)
                self.ev_w2[nxt] = p.emit(
                    "sync", lambda sy, ee=nxt:
                    sy.dma_start(self.w2s.rearrange("p (fi d) -> p fi d", d=1024),
                                 self.w2_b[ee].rearrange(
                        "(fi p) d -> p fi d", p=128)),
                    waits=_later(self.pe_now(), self.ev_cast[f"w2_{nxt}"]),
                    inc=sd_w2, inc_n=16)
        self.ev_acc = [(self.sems["sd_ac0"], self.sems["sd_ac0"].n),
                       (self.sems["sd_ac1"], self.sems["sd_ac1"].n)]
        self.free[("stg", 0)] = self.free.get(("stg2", 0))
        self.free[("stg", 1)] = self.free.get(("stg2", 1))
        pe_end = self.pe_now()
        self.free[("wtb",)] = pe_end
        self.free[("C",)] = pe_end
        self.free[("qbuf",)] = pe_end
        self.free[("xnT_w",)] = pe_end
        self.free[("vtok",)] = pe_end
        self.free[("vtok2",)] = (s_pool, s_pool.n)

    def final_phase(self, s):
        p = self.p
        s_pe, s_dve = self.sems["s_pe"], self.sems["s_dve"]
        for ti in range(16):
            b = ti % 2
            sd_f = self.sems["sd_f0"] if b == 0 else self.sems["sd_f1"]
            sd_x = self.sems["sd_x0"] if b == 0 else self.sems["sd_x1"]
            fin = self.fin_in[:, b * 1024:(b + 1) * 1024]
            src = self.outscr.rearrange("(dc p) (ti t) -> p ti dc t",
                                        p=128, t=128)[:, ti]
            ev_f = p.emit("sync", lambda sy, o=fin, i=src:
                          sy.dma_start(o, i),
                          waits=_later(*self.ev_acc, self.free.get(("fin", b))),
                          inc=sd_f, inc_n=16)
            xb = self.x_any[:, b * 1024:(b + 1) * 1024]
            ev_x = p.emit("sync", lambda sy, o=xb, ss=s, t=ti: sy.dma_start(
                o, self.x_in[ss, t * 128:(t + 1) * 128, :]),
                waits=_later(self.free.get(("x_any", b))), inc=sd_x, inc_n=16)
            evt = None
            for dc in range(8):
                if b == 0:
                    po = self.ps_big[:, dc * 128:(dc + 1) * 128]
                    pkeys = [("ps_big",)]
                else:
                    pt = self.ps_c if dc < 4 else self.ps_d
                    po = pt[:, (dc % 4) * 128:(dc % 4) * 128 + 128]
                    pkeys = [("ps_c",), ("ps_d",)]
                w = _later(ev_f,
                           *([self.free.get(k) for k in pkeys] if dc == 0 else []))
                evt = p.emit("tensor", lambda t, o=po,
                             i=fin[:, dc * 128:(dc + 1) * 128]:
                             t.transpose(o, i, self.identF),
                             waits=w, inc=(s_pe if dc == 7 else None))
            self.free[("fin", b)] = evt
            ev_o = None
            for hc in range(2):
                xp = self.out_st[:, b * 1024 + hc * 512: b * 1024 + hc * 512 + 512]
                ev1 = p.emit("vector", lambda v, o=xp,
                             i=xb[:, hc * 512:hc * 512 + 512], t=ti:
                             v.tensor_scalar(o, i, self.rstd1[:, t:t + 1],
                                             self.nmr[:, t:t + 1],
                                             op0=ALU.mult, op1=ALU.add),
                             waits=_later(ev_x, self.free.get(("out_st", b))
                                          if hc == 0 else None),
                             inc=s_dve)
                if b == 0:
                    pa = self.ps_big[:, hc * 512: hc * 512 + 512]
                else:
                    pa = (self.ps_c if hc == 0 else self.ps_d)[:, 0:512]
                ev_o = p.emit("vector", lambda v, o=xp, i=pa:
                              v.tensor_add(o, o, i),
                              waits=_later(evt, ev1), inc=s_dve)
            if b == 0:
                self.free[("ps_big",)] = ev_o
            else:
                self.free[("ps_c",)] = ev_o
                self.free[("ps_d",)] = ev_o
            self.free[("x_any", b)] = ev_o
            sd_st = self.sems["sd_st0"] if b == 0 else self.sems["sd_st1"]
            ev_st = p.emit("sync", lambda sy, ss=s, t=ti,
                           o=self.out_st[:, (ti % 2) * 1024:(ti % 2) * 1024 + 1024]:
                           sy.dma_start(self.y_out[ss, t * 128:(t + 1) * 128, :], o),
                           waits=[ev_o], inc=sd_st, inc_n=16)
            self.free[("out_st", b)] = ev_st
        self.free[("outscr_rd0",)] = (self.sems["sd_f0"], self.sems["sd_f0"].n)
        self.free[("outscr_rd1",)] = (self.sems["sd_f1"], self.sems["sd_f1"].n)
        self.free[("stg", 0)] = self.pe_now()
        self.free[("stg", 1)] = self.pe_now()

    def _dbg_dump(self, dumps):
        p = self.p
        sd_st = self.sems["sd_st0"]
        waits = []
        for nm in ("s_pe", "s_act", "s_dve", "s_pool"):
            waits.append((self.sems[nm], self.sems[nm].n))
        for nm in ("sd_wa", "sd_wb", "sd_w1", "sd_w2", "sd_x0", "sd_x1",
                   "sd_f0", "sd_f1", "sd_sp0", "sd_sp1", "sd_st0", "sd_st1",
                   "sd_ac0", "sd_ac1", "sd_g", "sd_misc"):
            waits.append((self.sems[nm], self.sems[nm].n))
        first = True
        for nm, ap, shape, dt in dumps:
            out = self.nc.dram_tensor("dbg_" + nm, shape, dt,
                                      kind="ExternalOutput").ap()
            p.emit("sync", lambda sy, o=out, i=ap: sy.dma_start(o, i),
                   waits=waits if first else (), inc=sd_st, inc_n=16)
            first = False
        p.emit("sync", None, waits=[(sd_st, sd_st.n)])

    def build_debug(self, stage):
        self.setup()
        p = self.p
        self.phase0()
        self.ev_w1 = {}
        self.ev_w2 = {}
        s = 0
        dumps = []
        while True:
            if stage == "bias":
                dumps = [("biasc", self.biasc, [128, 224], F32),
                         ("braw", self.braw, [8, 2048], F32)]
                break
            self.ev_wtA = self.load_wt(0, "aq",
                                       extra_waits=[self.free.get(("wtb",))])
            self.ev_wtB = self.load_wt(1, "ak",
                                       extra_waits=[self.free.get(("wtb",))])
            self.ln_phase(s)
            if stage == "ln":
                dumps = [("xnT", self.xnT, [128, 8 * S], BF16),
                         ("negm", self.negm, [128, 16], F32),
                         ("rstd", self.rstd, [128, 16], F32)]
                break
            self.conv_phase(s)
            self.kv1_phase(s)
            if stage == "kv1":
                dumps = [("xdT", self.xdT, [128, 8192], BF16),
                         ("kT", self.kT, [128, 8192], BF16),
                         ("v1tok", self.v1tok, [128, 8320], BF16)]
                break
            self.ev_wtB = self.load_wt(1, "ao")
            self.q_proj(s, 0, 1)
            if stage == "q1":
                dumps = [("qT", self.qT, [128, 8192], BF16)]
                break
            self.attn_phase(s, 0, 1)
            if stage == "att1":
                dumps = [("oT", self.oT, [128, 8192], BF16),
                         ("osb", self.osb, [128, 1024], F32)]
                break
            self.free[("qbuf",)] = self.pe_now()
            self.out_proj(s, 0)
            self.free[("oa",)] = self.pe_now()
            if stage == "outp":
                dumps = [("oscr", self.outscr, [D, S], F32)]
                break
            self.q_proj(s, 1, 1)
            self.attn_phase(s, 1, 1)
            self.free[("qbuf",)] = self.pe_now()
            self.out_proj(s, 1)
            self.free[("oa",)] = self.pe_now()
            self.ev_wtA = self.load_wt(0, "fq", extra_waits=[self.pe_now()])
            self.ev_wtB = self.load_wt(1, "fk", extra_waits=[self.pe_now()])
            self.kv2_phase(s)
            if stage == "kv2":
                dumps = [("k2T", self.k2T, [64, 16384], BF16),
                         ("v2tok", self.v2tok, [128, 8448], BF16)]
                break
            self.ev_foB = self.load_wt(1, "fo")
            self.q_proj(s, 0, 2)
            if stage == "q2":
                dumps = [("q2c", self.q2c, [64, 8192], BF16)]
                break
            self.attn_phase(s, 0, 2)
            if stage == "att2":
                dumps = [("aoT", self.aoT, [64, 8192], BF16)]
                break
            self.free[("qbuf",)] = self.pe_now()
            self.mla_proj(s, 0)
            self.free[("oa",)] = self.pe_now()
            if stage == "mla":
                dumps = [("mlaT", self.mlaT, [128, 8 * S], BF16)]
                break
            self.router_phase(s, 0)
            if stage == "rout":
                dumps = [("combT", self.combT, [8, S], BF16),
                         ("probsL", self.probsL, [8, 1024], F32),
                         ("combF", self.combF, [128, 8], F32),
                         ("eq8", self.eq8, [128, 8], F32),
                         ("stats", self.stats, [128, 80], F32),
                         ("combB", self.combB, [128, 8], BF16)]
                break
            self.q_proj(s, 1, 2)
            self.attn_phase(s, 1, 2)
            self.free[("qbuf",)] = self.pe_now()
            self.mla_proj(s, 1)
            self.free[("oa",)] = self.pe_now()
            self.router_phase(s, 1)
            e0 = int(os.environ.get("KDBG_ELIST", "0").split(",")[0])
            self.ev_w1[e0] = p.emit("sync", lambda sy, ee=e0: sy.dma_start(
                self.w1s.rearrange("p (ci f) -> p ci f", f=2048),
                self.w1_b[ee].rearrange("(ci p) f -> p ci f", p=128)),
                waits=_later(self.ev_cast[f"w1_{e0}"], self.pe_now()),
                inc=self.sems["sd_w1"], inc_n=16)
            self.ev_w2[e0] = p.emit("sync", lambda sy, ee=e0: sy.dma_start(
                self.w2s.rearrange("p (fi d) -> p fi d", d=1024),
                self.w2_b[ee].rearrange("(fi p) d -> p fi d", p=128)),
                waits=_later(self.ev_cast[f"w2_{e0}"], self.pe_now()),
                inc=self.sems["sd_w2"], inc_n=16)
            self.moe_phase(s)
            if stage == "moe" and os.environ.get("KDBG_H1PS"):
                dumps = [("h1ps", self.out_st, [128, 2048], F32),
                         ("mlaT", self.mlaT, [128, 8 * S], BF16),
                         ("w1s", self.w1s, [128, 16384], BF16)]
                break
            if stage == "moe":
                dumps = [("oscr", self.outscr, [D, S], F32),
                         ("combT", self.combT, [8, S], BF16),
                         ("mlaT", self.mlaT, [128, 8 * S], BF16),
                         ("h1g", self.h1g, [128, 8192], BF16),
                         ("bcast", self.bcast, [128, 2048], BF16),
                         ("w1s", self.w1s, [128, 16384], BF16),
                         ("w2s", self.w2s, [128, 16384], BF16),
                         ("b2s", self.b2s, [8, D], BF16),
                         ("attb", self.attb, [128, 2048], BF16)]
                break
            self.final_phase(s)
            dumps = [("y0", self.y_out[0], [S, D], F32)]
            break
        self._dbg_dump(dumps)
        with self.nc.Block() as block:
            block.tensor(lambda t: self.p.run("tensor", t))
            block.vector(lambda v: self.p.run("vector", v))
            block.scalar(lambda sc: self.p.run("scalar", sc))
            block.gpsimd(lambda g: self.p.run("gpsimd", g))
            block.sync(lambda sy: self.p.run("sync", sy))
        return self.nc

    def build(self):
        self.setup()
        p = self.p
        self.phase0()
        self.ev_w1 = {}
        self.ev_w2 = {}
        for s in range(NSEQ):
            self.ev_wtA = self.load_wt(0, "aq",
                                       extra_waits=[self.free.get(("wtb",))])
            self.ev_wtB = self.load_wt(1, "ak",
                                       extra_waits=[self.free.get(("wtb",))])
            self.ln_phase(s)
            self.conv_phase(s)
            self.kv1_phase(s)
            self.ev_wtB = self.load_wt(1, "ao")
            for qh in range(2):
                self.q_proj(s, qh, 1)
                self.attn_phase(s, qh, 1)
                self.free[("qbuf",)] = self.pe_now()
                self.out_proj(s, qh)
                self.free[("oa",)] = self.pe_now()
            self.ev_wtA = self.load_wt(0, "fq", extra_waits=[self.pe_now()])
            self.ev_wtB = self.load_wt(1, "fk", extra_waits=[self.pe_now()])
            self.kv2_phase(s)
            self.ev_foB = self.load_wt(1, "fo")
            for qh in range(2):
                self.q_proj(s, qh, 2)
                self.attn_phase(s, qh, 2)
                self.free[("qbuf",)] = self.pe_now()
                self.mla_proj(s, qh)
                self.free[("oa",)] = self.pe_now()
                self.router_phase(s, qh)
            e0 = int(os.environ.get("KDBG_ELIST", "0").split(",")[0])
            self.ev_w1[e0] = p.emit("sync", lambda sy, ee=e0: sy.dma_start(
                self.w1s.rearrange("p (ci f) -> p ci f", f=2048),
                self.w1_b[ee].rearrange("(ci p) f -> p ci f", p=128)),
                waits=_later(self.ev_cast[f"w1_{e0}"], self.pe_now()),
                inc=self.sems["sd_w1"], inc_n=16)
            self.ev_w2[e0] = p.emit("sync", lambda sy, ee=e0: sy.dma_start(
                self.w2s.rearrange("p (fi d) -> p fi d", d=1024),
                self.w2_b[ee].rearrange("(fi p) d -> p fi d", p=128)),
                waits=_later(self.ev_cast[f"w2_{e0}"], self.pe_now()),
                inc=self.sems["sd_w2"], inc_n=16)
            self.moe_phase(s)
            self.final_phase(s)
        p.emit("sync", None, waits=[(self.sems["sd_st0"], self.sems["sd_st0"].n),
                                    (self.sems["sd_st1"], self.sems["sd_st1"].n)])

        with self.nc.Block() as block:
            block.tensor(lambda t: self.p.run("tensor", t))
            block.vector(lambda v: self.p.run("vector", v))
            block.scalar(lambda sc: self.p.run("scalar", sc))
            block.gpsimd(lambda g: self.p.run("gpsimd", g))
            block.sync(lambda sy: self.p.run("sync", sy))
        return self.nc


_built = None


def _get_built():
    global _built
    if _built is None:
        _built = Build().build()
    return _built


def kernel(**inputs):
    nc = _get_built()
    x = np.ascontiguousarray(inputs["L0_attn_x"])
    shards = np.split(x, 8, axis=0)
    weights = {k: np.ascontiguousarray(v) for k, v in inputs.items()
               if k != "L0_attn_x"}
    in_maps = []
    for c in range(8):
        m = {"x": shards[c]}
        m.update(weights)
        in_maps.append(m)
    res = run_bass_kernel_spmd(nc, in_maps, core_ids=list(range(8)))
    return np.concatenate([r["y"] for r in res.results], axis=0)
